# version 06
# baseline (speedup 1.0000x reference)
import math
import numpy as np
import ml_dtypes
import concourse.bass as bass
import concourse.mybir as mybir
from concourse.bass_utils import run_bass_kernel_spmd
from contextlib import ExitStack

F32 = mybir.dt.float32
BF16 = mybir.dt.bfloat16
AF = mybir.ActivationFunctionType
ALU = mybir.AluOpType
AX = mybir.AxisListType
bfloat16 = ml_dtypes.bfloat16

D = 2048
S = 2048
NH = 16
HD = 128
NCH = 16
NT = 16
G = 2
NG = NH // G
WQ = 256
NQP = S // WQ
SCALE = 1.0 / math.sqrt(HD)
EPS = 1e-5

STAGE = 4
DBG4 = False
SERIAL = False
LAG = True
PIN_ACT = False
ACT_SET_ID = 0


def _perm_deinterleave():
    p = np.concatenate([np.arange(0, HD, 2), np.arange(1, HD, 2)])
    return np.concatenate([h * HD + p for h in range(NH)])


def _rope_tables():
    k = np.arange(HD)
    j = (k % 64).astype(np.float64)
    inv = np.power(10000.0, -2.0 * j / HD)
    ang = inv[:, None] * np.arange(S, dtype=np.float64)[None, :]
    c = np.cos(ang).astype(np.float32).astype(bfloat16)
    s = np.sin(ang).astype(np.float32).astype(bfloat16)
    return c, s


def build_nc(stage=STAGE):
    nc = bass.Bass("TRN2", target_bir_lowering=False, debug=False)
    dbg_names = []

    x_tm = nc.dram_tensor("x_tm", [S, D], F32, kind="ExternalInput").ap()
    x_fm = nc.dram_tensor("x_fm", [D, S], F32, kind="ExternalInput").ap()
    wq_d = nc.dram_tensor("wq", [D, D], F32, kind="ExternalInput").ap()
    wk_d = nc.dram_tensor("wk", [D, D], F32, kind="ExternalInput").ap()
    wv_d = nc.dram_tensor("wv", [D, D], F32, kind="ExternalInput").ap()
    wo_d = nc.dram_tensor("wo", [D, D], F32, kind="ExternalInput").ap()
    bq_d = nc.dram_tensor("bq", [1, D], F32, kind="ExternalInput").ap()
    bk_d = nc.dram_tensor("bk", [1, D], F32, kind="ExternalInput").ap()
    bv_d = nc.dram_tensor("bv", [1, D], F32, kind="ExternalInput").ap()
    bo_d = nc.dram_tensor("bo", [1, D], F32, kind="ExternalInput").ap()
    ffnw_d = nc.dram_tensor("ffnw", [D, 136], F32, kind="ExternalInput").ap()
    dgnt_d = nc.dram_tensor("dgnt", [64, D], F32, kind="ExternalInput").ap()
    bw_d = nc.dram_tensor("bw", [8, D], F32, kind="ExternalInput").ap()
    debt_d = nc.dram_tensor("debt", [D, 8], F32, kind="ExternalInput").ap()
    cen_d = nc.dram_tensor("cen", [1, 8], F32, kind="ExternalInput").ap()
    slp_d = nc.dram_tensor("slp", [1, 8], F32, kind="ExternalInput").ap()
    ropec_d = nc.dram_tensor("ropec", [HD, S], BF16, kind="ExternalInput").ap()
    ropes_d = nc.dram_tensor("ropes", [HD, S], BF16, kind="ExternalInput").ap()
    sel_d = nc.dram_tensor("sel16", [16, D], F32, kind="ExternalInput").ap()

    y_d = nc.dram_tensor("y", [S, D], F32, kind="ExternalOutput").ap()
    ot_hbm = nc.dram_tensor("ot_hbm", [NH, HD, S], BF16, kind="Internal").ap()
    x1b_hbm = nc.dram_tensor("x1b_hbm", [S, D], BF16, kind="Internal").ap()

    dbg = {}
    if stage == 4 and DBG4:
        dbg["d_x1"] = nc.dram_tensor("d_x1", [S, D], F32, kind="ExternalOutput").ap()
        dbg_names.append("d_x1")
        dbg["d_ota"] = nc.dram_tensor("d_ota", [NH, HD, S], F32, kind="ExternalOutput").ap()
        dbg_names.append("d_ota")
        dbg["d_gg"] = nc.dram_tensor("d_gg", [128, NT * 16], F32, kind="ExternalOutput").ap()
        dbg_names.append("d_gg")
        dbg["d_slpbc"] = nc.dram_tensor("d_slpbc", [128, 8], F32, kind="ExternalOutput").ap()
        dbg_names.append("d_slpbc")
        dbg["d_ets"] = nc.dram_tensor("d_ets", [8, 128], F32, kind="ExternalOutput").ap()
        dbg["d_mts"] = nc.dram_tensor("d_mts", [64, 128], F32, kind="ExternalOutput").ap()
        dbg["d_ebf"] = nc.dram_tensor("d_ebf", [128, 8], F32, kind="ExternalOutput").ap()
        dbg["d_mmb"] = nc.dram_tensor("d_mmb", [128, 64], F32, kind="ExternalOutput").ap()
        dbg_names.extend(["d_ets", "d_mts", "d_ebf", "d_mmb"])
    if stage < 4:
        for nm, shp in (("d_crow", [1, 136]), ("d_m2bc", [128, 8]), ("d_cen", [128, 8])):
            dbg[nm] = nc.dram_tensor(nm, shp, F32, kind="ExternalOutput").ap()
            dbg_names.append(nm)
    if stage in (2, 3):
        for nm, shp in (("d_q0", [HD, S]), ("d_k0", [HD, S]), ("d_v0", [128, 256]),
                        ("d_ot0", [HD, S]), ("d_ot15", [HD, S])):
            dbg[nm] = nc.dram_tensor(nm, shp, F32, kind="ExternalOutput").ap()
            dbg_names.append(nm)

    ident_bf = nc.inline_tensor(np.eye(128, dtype=bfloat16), name="ident_bf").ap()
    ident_f = nc.inline_tensor(np.eye(128, dtype=np.float32), name="ident_f").ap()
    rt = np.zeros((128, 128), dtype=np.float32)
    for i in range(64):
        rt[i, 64 + i] = 1.0
        rt[64 + i, i] = -1.0
    rt_bf = nc.inline_tensor(rt.astype(bfloat16), name="rt_bf").ap()
    ones_r_bf = nc.inline_tensor(np.ones((1, 512), dtype=bfloat16), name="ones_r_bf").ap()
    ones_c_bf = nc.inline_tensor(np.ones((128, 1), dtype=bfloat16), name="ones_c_bf").ap()
    ones_r_f = nc.inline_tensor(np.ones((1, 128), dtype=np.float32), name="ones_r_f").ap()
    ones8_f = nc.inline_tensor(np.ones((8, 1), dtype=np.float32), name="ones8_f").ap()
    eye8_c = nc.inline_tensor(np.eye(8, dtype=np.float32), name="eye8_c").ap()
    m64 = np.zeros((8, 64), dtype=np.float32)
    for n in range(8):
        m64[n, n * 8:(n + 1) * 8] = 1.0
    mask64_c = nc.inline_tensor(m64, name="mask64_c").ap()
    wt = np.tile((8.0 - np.arange(8, dtype=np.float32))[None, :], (128, 1))
    wtie_c = nc.inline_tensor(wt, name="wtie_c").ap()

    with ExitStack() as ctx:
        E = ctx.enter_context
        xT = E(nc.sbuf_tensor("xT", [128, NCH * S], BF16))
        ffnw = E(nc.sbuf_tensor("ffnw_sb", [128, NCH * 136], BF16))
        dgn = E(nc.sbuf_tensor("dgn", [64, D], BF16))
        bwsb = E(nc.sbuf_tensor("bwsb", [8, D], BF16))
        debt = E(nc.sbuf_tensor("debt_sb", [128, NCH * 8], BF16))
        bqr = E(nc.sbuf_tensor("bqr", [1, D], BF16))
        bkr = E(nc.sbuf_tensor("bkr", [1, D], BF16))
        bvr = E(nc.sbuf_tensor("bvr", [1, D], BF16))
        bor = E(nc.sbuf_tensor("bor", [1, D], BF16))
        cenr = E(nc.sbuf_tensor("cenr", [1, 8], F32))
        slpr = E(nc.sbuf_tensor("slpr", [1, 8], F32))
        selsb = E(nc.sbuf_tensor("selsb", [16, D], F32))
        ropecs = E(nc.sbuf_tensor("ropecs", [128, S], BF16))
        ropess = E(nc.sbuf_tensor("ropess", [128, S], BF16))
        idb = E(nc.sbuf_tensor("idb", [128, 128], BF16))
        idf = E(nc.sbuf_tensor("idf", [128, 128], F32))
        rtb = E(nc.sbuf_tensor("rtb", [128, 128], BF16))
        onerb = E(nc.sbuf_tensor("onerb", [1, 512], BF16))
        onecb = E(nc.sbuf_tensor("onecb", [128, 1], BF16))
        onerf = E(nc.sbuf_tensor("onerf", [1, 128], F32))
        one8f = E(nc.sbuf_tensor("one8f", [8, 1], F32))
        eye8s = E(nc.sbuf_tensor("eye8s", [8, 8], F32))
        m64s = E(nc.sbuf_tensor("m64s", [8, 64], F32))
        wties = E(nc.sbuf_tensor("wties", [128, 8], F32))
        crow = E(nc.sbuf_tensor("crow_sb", [1, 136], BF16))
        m2bc = E(nc.sbuf_tensor("m2bc", [128, 8], F32))
        cenbc = E(nc.sbuf_tensor("cenbc", [128, 8], F32))
        slpbc = E(nc.sbuf_tensor("slpbc", [128, 8], F32))
        scr8 = E(nc.sbuf_tensor("scr8", [8, 136], F32))

        PA = E(nc.psum_tensor("PA", [128, 1024], F32))
        PB = E(nc.psum_tensor("PB", [128, 1024], F32))
        PC = E(nc.psum_tensor("PC", [128, 512], F32))
        PD = E(nc.psum_tensor("PD", [128, 512], F32))
        PE_ = E(nc.psum_tensor("PEb", [128, 512], F32))
        PF = E(nc.psum_tensor("PFb", [128, 512], F32))
        PDb = PD[:, :].bitcast(BF16)

        sem = {}
        for name in ("pe", "act", "dve", "dG", "dS", "dA", "st"):
            sem[name] = E(nc.semaphore(name))
        C = {k: 0 for k in sem}
        block = E(nc.Block())
        M = {}

        def inc(ins, s_, n=1):
            ins.then_inc(sem[s_], n)
            C[s_] += n
            return C[s_]

        SNAP = {}

        def snap_ser():
            SNAP.update({k: C[k] for k in ("pe", "act", "dve")})

        def ser(eng):
            if SERIAL:
                for s2 in ("pe", "act", "dve"):
                    eng.wait_ge(sem[s2], SNAP.get(s2, 0))

        free = {"PA": None, "PB": None, "PC": None, "PD": None, "PE": None, "PF": None}

        def wait_free(eng, bank):
            if free[bank] is not None:
                s_, v_ = free[bank]
                eng.wait_ge(sem[s_], v_)

        @block.gpsimd
        def _(g):
            inc(g.dma_start(out=xT[:, :].rearrange("p (c t) -> p c t", c=NCH),
                            in_=x_fm.rearrange("(c p) t -> p c t", p=128)), "dG", 16)
            inc(g.dma_start(out=ffnw[:, :].rearrange("p (c f) -> p c f", c=NCH),
                            in_=ffnw_d.rearrange("(c p) f -> p c f", p=128)), "dG", 16)
            inc(g.dma_start(out=debt[:, :].rearrange("p (c f) -> p c f", c=NCH),
                            in_=debt_d.rearrange("(c p) f -> p c f", p=128)), "dG", 16)
            inc(g.dma_start(out=dgn[:, :], in_=dgnt_d), "dG", 16)
            inc(g.dma_start(out=bwsb[:, :], in_=bw_d), "dG", 16)
            inc(g.dma_start(out=bqr[:, :], in_=bq_d), "dG", 16)
            inc(g.dma_start(out=bkr[:, :], in_=bk_d), "dG", 16)
            inc(g.dma_start(out=bvr[:, :], in_=bv_d), "dG", 16)
            inc(g.dma_start(out=bor[:, :], in_=bo_d), "dG", 16)
            M["p0_gp"] = C["dG"]

        @block.sync
        def _(s_):
            for dst, src in ((idb, ident_bf), (idf, ident_f), (rtb, rt_bf),
                             (onerb, ones_r_bf), (onecb, ones_c_bf), (onerf, ones_r_f),
                             (one8f, ones8_f), (eye8s, eye8_c), (m64s, mask64_c),
                             (wties, wtie_c), (cenr, cen_d), (slpr, slp_d), (selsb, sel_d),
                             (ropecs, ropec_d), (ropess, ropes_d)):
                inc(s_.dma_start(out=dst[:, :], in_=src), "dS", 16)
            M["p0_sy"] = C["dS"]

        if PIN_ACT:
            @block.scalar
            def _(sc):
                sc.add_instruction(mybir.InstLoadActFuncSet(
                    name=nc.get_next_instruction_name(), act_func_set_id=ACT_SET_ID, ins=[], outs=[]))

        @block.tensor
        def _(t):
            t.wait_ge(sem["dG"], M["p0_gp"])
            t.wait_ge(sem["dS"], M["p0_sy"])
            for c in range(NCH):
                f0 = c * 136
                t.matmul(PC[0:8, 0:64], ffnw[:, f0:f0 + 8], ffnw[:, f0 + 8:f0 + 72],
                         start=(c == 0), stop=(c == NCH - 1))
            for c in range(NCH):
                f0 = c * 136
                t.matmul(PC[0:8, 64:128], debt[:, c * 8:(c + 1) * 8], ffnw[:, f0 + 72:f0 + 136],
                         start=(c == 0), stop=(c == NCH - 1))
            for c in range(NCH):
                f0 = c * 136
                m = t.matmul(PC[0:8, 128:136], ffnw[:, f0:f0 + 8], ffnw[:, f0:f0 + 8],
                             start=(c == 0), stop=(c == NCH - 1))
            M["p1_mm"] = inc(m, "pe")

        @block.vector
        def _(v):
            v.wait_ge(sem["pe"], M["p1_mm"])
            v.tensor_tensor(scr8[0:8, 0:64], PC[0:8, 0:64], m64s[:, :], ALU.mult)
            v.tensor_tensor(scr8[0:8, 64:128], PC[0:8, 64:128], m64s[:, :], ALU.mult)
            m = v.tensor_tensor(scr8[0:8, 128:136], PC[0:8, 128:136], eye8s[:, :], ALU.mult)
            M["p1_msk"] = inc(m, "dve")
        free["PC"] = ("dve", M["p1_msk"])

        @block.tensor
        def _(t):
            t.wait_ge(sem["dve"], M["p1_msk"])
            t.matmul(PD[0:1, 0:64], one8f[:, :], scr8[0:8, 0:64], start=True, stop=True)
            t.matmul(PD[0:1, 64:128], one8f[:, :], scr8[0:8, 64:128], start=True, stop=True)
            m = t.matmul(PD[0:1, 128:136], one8f[:, :], scr8[0:8, 128:136], start=True, stop=True)
            M["p1_row"] = inc(m, "pe")

        @block.vector
        def _(v):
            v.wait_ge(sem["pe"], M["p1_row"])
            v.memset(crow[:, :], 0.0)
            v.tensor_scalar(crow[0:1, 8:72], PD[0:1, 0:64], -1.0, None, ALU.mult)
            v.tensor_scalar(crow[0:1, 72:136], PD[0:1, 64:128], -1.0, None, ALU.mult)
            m = v.tensor_copy(scr8[0:1, 0:8], PD[0:1, 128:136])
            M["p1_m2"] = inc(m, "dve")
        free["PD"] = ("dve", M["p1_m2"])

        @block.tensor
        def _(t):
            t.wait_ge(sem["dve"], M["p1_m2"])
            t.matmul(PE_[:, 0:8], onerf[:, :], scr8[0:1, 0:8], start=True, stop=True)
            t.matmul(PE_[:, 8:16], onerf[:, :], cenr[:, :], start=True, stop=True)
            m = t.matmul(PE_[:, 16:24], onerf[:, :], slpr[:, :], start=True, stop=True)
            M["p1_bc"] = inc(m, "pe")

        @block.vector
        def _(v):
            v.wait_ge(sem["pe"], M["p1_bc"])
            v.tensor_copy(m2bc[:, :], PE_[:, 0:8])
            v.tensor_copy(cenbc[:, :], PE_[:, 8:16])
            m = v.tensor_copy(slpbc[:, :], PE_[:, 16:24])
            M["p1_done"] = inc(m, "dve")
        free["PE"] = ("dve", M["p1_done"])

        if stage < 4:
            @block.gpsimd
            def _(g):
                g.wait_ge(sem["dve"], M["p1_done"])
                inc(g.dma_start(out=dbg["d_crow"], in_=crow[:, :]), "st", 16)

            @block.sync
            def _(s_):
                s_.wait_ge(sem["dve"], M["p1_done"])
                inc(s_.dma_start(out=dbg["d_m2bc"], in_=m2bc[:, :]), "st", 16)
                inc(s_.dma_start(out=dbg["d_cen"], in_=cenbc[:, :]), "st", 16)

        if stage >= 2:
            bctx = ctx.enter_context(ExitStack())
            EB = bctx.enter_context
            wsl = {w: EB(nc.sbuf_tensor(f"wsl_{w}", [128, NCH * 256], BF16)) for w in "qkv"}
            qk = EB(nc.sbuf_tensor("qk", [128, 4 * S], BF16))
            vsb = EB(nc.sbuf_tensor("vsb", [128, NT * 256], BF16))
            rotb = EB(nc.sbuf_tensor("rotb", [128, S], BF16))
            foldb = EB(nc.sbuf_tensor("foldb", [128, S], BF16))
            pt = [EB(nc.sbuf_tensor(f"pt{i}", [128, 4096], BF16)) for i in range(2)]
            oth = EB(nc.sbuf_tensor("oth", [128, S], F32))
            otn = [EB(nc.sbuf_tensor(f"otn{i}", [128, S], BF16)) for i in range(2)]
            rcol = EB(nc.sbuf_tensor("rcol", [128, 16], F32))
            rz16 = EB(nc.sbuf_tensor("rz16", [16, 128], F32))
            nsh = EB(nc.sbuf_tensor("nsh", [128, 2], F32))
            sml = EB(nc.sbuf_tensor("sml", [128, 8], F32))
            smr = EB(nc.sbuf_tensor("smr", [1, 16], F32))
            smb = EB(nc.sbuf_tensor("smb", [128, 2], BF16))

            ngroups = NG if stage >= 3 else 1

            @block.gpsimd
            def _(g):
                for w, wd in (("q", wq_d), ("k", wk_d), ("v", wv_d)):
                    m = g.dma_start(out=wsl[w][:, :].rearrange("p (c f) -> p c f", c=NCH),
                                    in_=wd[:, 0:G * 128].rearrange("(c p) f -> p c f", p=128))
                    inc(m, "dG", 16)
                M["wsl_0"] = C["dG"]

            for gi in range(ngroups):
                h0 = G * gi

                act_base = C["act"]
                ev_mark = lambda pp: act_base + pp + 1

                @block.tensor
                def _(t, gi=gi, h0=h0, ev_mark=ev_mark):
                    t.wait_ge(sem["dG"], M[f"wsl_{gi}"])
                    if gi == 0:
                        t.wait_ge(sem["dG"], M["p0_gp"])
                        t.wait_ge(sem["dS"], M["p0_sy"])
                        t.wait_ge(sem["dve"], M["p1_done"])
                    pp = 0
                    for hl in range(G):
                        for wname, brow in (("q", bqr), ("k", bkr)):
                            for p in range(4):
                                ps = PA if pp % 2 == 0 else PB
                                if pp >= 2:
                                    t.wait_ge(sem["act"], ev_mark(pp - 2))
                                else:
                                    wait_free(t, "PA" if pp % 2 == 0 else "PB")
                                for c in range(NCH):
                                    t.matmul(ps[:, 0:512],
                                             wsl[wname][:, c * 256 + hl * 128: c * 256 + hl * 128 + 128],
                                             xT[:, c * S + p * 512: c * S + p * 512 + 512],
                                             start=(c == 0), stop=False)
                                m = t.matmul(ps[:, 0:512],
                                             brow[0:1, (h0 + hl) * 128:(h0 + hl) * 128 + 128],
                                             onerb[0:1, 0:512], start=False, stop=True)
                                M[f"qk_{gi}_{pp}"] = inc(m, "pe")
                                pp += 1

                @block.scalar
                def _(sc, gi=gi):
                    pp = 0
                    for hl in range(G):
                        for wname in ("q", "k"):
                            slot = (0 if wname == "q" else 2) + hl
                            for p in range(4):
                                ps = PA if pp % 2 == 0 else PB
                                sc.wait_ge(sem["pe"], M[f"qk_{gi}_{pp}"])
                                m = sc.activation(qk[:, slot * S + p * 512: slot * S + p * 512 + 512],
                                                  ps[:, 0:512], AF.Copy)
                                assert inc(m, "act") == ev_mark(pp)
                                pp += 1
                free["PA"] = ("act", ev_mark(14))
                free["PB"] = ("act", ev_mark(15))

                dve_vbase = C["dve"]
                vev_mark = lambda tt: dve_vbase + tt + 1

                @block.tensor
                def _(t, gi=gi, h0=h0, vev_mark=vev_mark):
                    wait_free(t, "PC")
                    wait_free(t, "PD")
                    for tt in range(NT):
                        ps = PC if tt % 2 == 0 else PD
                        if tt >= 2:
                            t.wait_ge(sem["dve"], vev_mark(tt - 2))
                        for c in range(NCH):
                            t.matmul(ps[:, 0:256],
                                     xT[:, c * S + tt * 128: c * S + tt * 128 + 128],
                                     wsl["v"][:, c * 256: c * 256 + 256],
                                     start=(c == 0), stop=False)
                        m = t.matmul(ps[:, 0:256], onerb[0:1, 0:128],
                                     bvr[0:1, h0 * 128: h0 * 128 + 256], start=False, stop=True)
                        M[f"v_{gi}_{tt}"] = inc(m, "pe")

                @block.vector
                def _(v, gi=gi, vev_mark=vev_mark):
                    for tt in range(NT):
                        ps = PC if tt % 2 == 0 else PD
                        v.wait_ge(sem["pe"], M[f"v_{gi}_{tt}"])
                        m = v.tensor_copy(vsb[:, tt * 256:(tt + 1) * 256], ps[:, 0:256])
                        M[f"vev_{gi}_{tt}"] = inc(m, "dve")
                        assert M[f"vev_{gi}_{tt}"] == vev_mark(tt)
                free["PC"] = ("dve", vev_mark(14))
                free["PD"] = ("dve", vev_mark(15))

                if gi + 1 < ngroups:
                    @block.gpsimd
                    def _(g, gi=gi):
                        g.wait_ge(sem["pe"], M[f"v_{gi}_{NT-1}"])
                        for w, wd in (("q", wq_d), ("k", wk_d), ("v", wv_d)):
                            c0 = (gi + 1) * G * 128
                            m = g.dma_start(out=wsl[w][:, :].rearrange("p (c f) -> p c f", c=NCH),
                                            in_=wd[:, c0:c0 + G * 128].rearrange("(c p) f -> p c f", p=128))
                            inc(m, "dG", 16)
                        M[f"wsl_{gi+1}"] = C["dG"]

                for hl in range(G):
                    for wi, wname in enumerate(("q", "k")):
                        slot = (0 if wname == "q" else 2) + hl
                        for half in range(2):
                            bank = "PA" if half == 0 else "PB"
                            ps = PA if half == 0 else PB

                            @block.tensor
                            def _(t, gi=gi, hl=hl, wi=wi, slot=slot, half=half, ps=ps, bank=bank):
                                wait_free(t, bank)
                                if wi == 0 and half == 0:
                                    pass
                                for j2 in range(2):
                                    o = half * 1024 + j2 * 512
                                    m = t.matmul(ps[:, j2 * 512:(j2 + 1) * 512], rtb[:, :],
                                                 qk[:, slot * S + o: slot * S + o + 512],
                                                 start=True, stop=True)
                                M[f"rot_{gi}_{hl}_{wi}_{half}"] = inc(m, "pe")

                            @block.scalar
                            def _(sc, gi=gi, hl=hl, wi=wi, half=half, ps=ps):
                                sc.wait_ge(sem["pe"], M[f"rot_{gi}_{hl}_{wi}_{half}"])
                                m = sc.activation(rotb[:, half * 1024:(half + 1) * 1024],
                                                  ps[:, 0:1024], AF.Copy)
                                M[f"roev_{gi}_{hl}_{wi}_{half}"] = inc(m, "act")
                            free[bank] = ("act", M[f"roev_{gi}_{hl}_{wi}_{half}"])

                            @block.vector
                            def _(v, gi=gi, hl=hl, wi=wi, slot=slot, half=half):
                                o = half * 1024
                                v.wait_ge(sem["act"], M[f"roev_{gi}_{hl}_{wi}_{half}"])
                                v.tensor_tensor(rotb[:, o:o + 1024], rotb[:, o:o + 1024],
                                                ropess[:, o:o + 1024], ALU.mult)
                                v.tensor_tensor(qk[:, slot * S + o: slot * S + o + 1024],
                                                qk[:, slot * S + o: slot * S + o + 1024],
                                                ropecs[:, o:o + 1024], ALU.mult)
                                m = v.tensor_tensor(qk[:, slot * S + o: slot * S + o + 1024],
                                                    qk[:, slot * S + o: slot * S + o + 1024],
                                                    rotb[:, o:o + 1024], ALU.add)
                                M[f"rope_{gi}_{hl}_{wi}_{half}"] = inc(m, "dve")

                for hl in range(G):
                    @block.vector
                    def _(v, gi=gi, hl=hl):
                        for wi in range(2):
                            for half in range(2):
                                v.wait_ge(sem["dve"], M[f"rope_{gi}_{hl}_{wi}_{half}"])
                        m = v.tensor_tensor(foldb[:, :], qk[:, hl * S:(hl + 1) * S],
                                            qk[:, hl * S:(hl + 1) * S], ALU.mult)
                        M[f"sqq_{gi}_{hl}"] = inc(m, "dve")

                    @block.tensor
                    def _(t, gi=gi, hl=hl):
                        t.wait_ge(sem["dve"], M[f"sqq_{gi}_{hl}"])
                        wait_free(t, "PF")
                        for qt in range(16):
                            m = t.matmul(PF[:, qt:qt + 1], foldb[:, qt * 128:(qt + 1) * 128],
                                         onecb[:, :], start=True, stop=True)
                        M[f"nq_{gi}_{hl}"] = inc(m, "pe")

                    @block.vector
                    def _(v, gi=gi, hl=hl):
                        v.wait_ge(sem["pe"], M[f"nq_{gi}_{hl}"])
                        v.reduce_max(sml[:, 0:1], PF[:, 0:16], axis=AX.X)
                        m = v.tensor_tensor(foldb[:, :], qk[:, (2 + hl) * S:(3 + hl) * S],
                                            qk[:, (2 + hl) * S:(3 + hl) * S], ALU.mult)
                        M[f"sqk_{gi}_{hl}"] = inc(m, "dve")
                        m = v.tensor_copy(smb[:, 0:1], sml[:, 0:1])
                        M[f"mq1_{gi}_{hl}"] = inc(m, "dve")

                    @block.tensor
                    def _(t, gi=gi, hl=hl):
                        t.wait_ge(sem["dve"], M[f"mq1_{gi}_{hl}"])
                        wait_free(t, "PD")
                        t.transpose(PDb[0:1, 0:128], smb[:, 0:1], idb[:, :])
                        t.wait_ge(sem["dve"], M[f"sqk_{gi}_{hl}"])
                        for qt in range(16):
                            m = t.matmul(PF[:, qt:qt + 1], foldb[:, qt * 128:(qt + 1) * 128],
                                         onecb[:, :], start=True, stop=True)
                        M[f"nk_{gi}_{hl}"] = inc(m, "pe")

                    @block.vector
                    def _(v, gi=gi, hl=hl):
                        v.wait_ge(sem["pe"], M[f"nk_{gi}_{hl}"])
                        v.reduce_max(sml[:, 1:2], PF[:, 0:16], axis=AX.X)
                        v.reduce_max(smr[0:1, 0:1], PDb[0:1, 0:128], axis=AX.X)
                        v.tensor_copy(sml[:, 3:4], sml[:, 0:1])
                        m = v.tensor_copy(smb[:, 1:2], sml[:, 1:2])
                        M[f"mk1_{gi}_{hl}"] = inc(m, "dve")

                    @block.tensor
                    def _(t, gi=gi, hl=hl):
                        t.wait_ge(sem["dve"], M[f"mk1_{gi}_{hl}"])
                        m = t.transpose(PDb[0:1, 128:256], smb[:, 1:2], idb[:, :])
                        M[f"tk_{gi}_{hl}"] = inc(m, "pe")

                    @block.vector
                    def _(v, gi=gi, hl=hl):
                        v.wait_ge(sem["pe"], M[f"tk_{gi}_{hl}"])
                        v.reduce_max(smr[0:1, 1:2], PDb[0:1, 128:256], axis=AX.X)
                        v.tensor_copy(sml[:, 4:5], sml[:, 0:1])
                        v.tensor_copy(sml[:, 4:5], sml[:, 1:2])
                        v.tensor_tensor(smr[0:1, 2:3], smr[0:1, 0:1], smr[0:1, 1:2], ALU.add)
                        v.tensor_copy(sml[:, 4:5], sml[:, 0:1])
                        v.tensor_copy(sml[:, 4:5], sml[:, 1:2])
                        m = v.tensor_scalar(smr[0:1, 3:4], smr[0:1, 2:3], -SCALE / 2.0, None, ALU.mult)
                        M[f"nsr_{gi}_{hl}"] = inc(m, "dve")
                    free["PD"] = ("dve", M[f"nsr_{gi}_{hl}"])
                    free["PF"] = ("dve", M[f"mk1_{gi}_{hl}"])

                    @block.tensor
                    def _(t, gi=gi, hl=hl):
                        t.wait_ge(sem["dve"], M[f"nsr_{gi}_{hl}"])
                        wait_free(t, "PE")
                        m = t.matmul(PE_[:, 256 + hl:257 + hl], onerf[:, :], smr[0:1, 3:4],
                                     start=True, stop=True)
                        M[f"nsb_{gi}_{hl}"] = inc(m, "pe")

                    @block.vector
                    def _(v, gi=gi, hl=hl):
                        v.wait_ge(sem["pe"], M[f"nsb_{gi}_{hl}"])
                        m = v.tensor_copy(nsh[:, hl:hl + 1], PE_[:, 256 + hl:257 + hl])
                        M[f"nsh_{gi}_{hl}"] = inc(m, "dve")
                    free["PE"] = ("dve", M[f"nsh_{gi}_{hl}"])

                for hl in range(G):
                    h = h0 + hl
                    act_hbase = C["act"]
                    exp_mark = lambda qp, kg: act_hbase + qp * 4 + kg + 1

                    def emit_sg(qp):
                        it = f"{gi}_{hl}_{qp}"

                        @block.tensor
                        def _(t, qp=qp, it=it):
                            if qp == 0:
                                t.wait_ge(sem["dve"], M[f"nsh_{gi}_{hl}"])
                                if hl == 0:
                                    t.wait_ge(sem["dve"], M[f"vev_{gi}_{NT-1}"])
                                else:
                                    t.wait_ge(sem["dve"], M[f"rcp_{gi}_{hl-1}"])
                                wait_free(t, "PA")
                                wait_free(t, "PB")
                            for kg in range(4):
                                ps = PA if kg % 2 == 0 else PB
                                ei = qp * 4 + kg
                                if ei >= 2:
                                    t.wait_ge(sem["act"], exp_mark((ei - 2) // 4, (ei - 2) % 4))
                                for k4 in range(4):
                                    kt = kg * 4 + k4
                                    m = t.matmul(ps[:, k4 * 256:(k4 + 1) * 256],
                                                 qk[:, (2 + hl) * S + kt * 128: (2 + hl) * S + kt * 128 + 128],
                                                 qk[:, hl * S + qp * 256: hl * S + qp * 256 + 256],
                                                 start=True, stop=True)
                                M[f"sg_{it}_{kg}"] = inc(m, "pe")

                        @block.scalar
                        def _(sc, qp=qp, it=it):
                            buf = pt[qp % 2]
                            for kg in range(4):
                                ps = PA if kg % 2 == 0 else PB
                                sc.wait_ge(sem["pe"], M[f"sg_{it}_{kg}"])
                                if qp >= 2 and kg == 0:
                                    sc.wait_ge(sem["pe"], M[f"pv_{gi}_{hl}_{qp-2}"])
                                    sc.wait_ge(sem["dve"], M[f"f1_{gi}_{hl}_{qp-2}"])
                                m = sc.activation(buf[:, kg * 1024:(kg + 1) * 1024], ps[:, 0:1024],
                                                  AF.Exp, bias=nsh[:, hl:hl + 1], scale=SCALE)
                                assert inc(m, "act") == exp_mark(qp, kg)

                    def emit_pv(qp):
                        it = f"{gi}_{hl}_{qp}"

                        @block.tensor
                        def _(t, qp=qp, it=it):
                            buf = pt[qp % 2]
                            t.wait_ge(sem["act"], exp_mark(qp, 3))
                            if qp >= 2:
                                t.wait_ge(sem["dve"], M[f"otev_{gi}_{hl}_{qp-2}"])
                            else:
                                wait_free(t, "PC" if qp % 2 == 0 else "PD")
                            ps = PC if qp % 2 == 0 else PD
                            for kt in range(16):
                                m = t.matmul(ps[:, 0:256],
                                             vsb[:, kt * 256 + hl * 128: kt * 256 + hl * 128 + 128],
                                             buf[:, kt * 256:(kt + 1) * 256],
                                             start=(kt == 0), stop=(kt == 15))
                            M[f"pv_{it}"] = inc(m, "pe")
                            t.wait_ge(sem["dve"], M[f"f1_{it}"])
                            if hl == 0 and qp == 0:
                                wait_free(t, "PF")
                            for j in range(2):
                                m = t.matmul(PF[:, qp * 2 + j: qp * 2 + j + 1],
                                             foldb[:, j * 128:(j + 1) * 128], onecb[:, :],
                                             start=True, stop=True)
                            M[f"z_{it}"] = inc(m, "pe")

                        @block.vector
                        def _(v, qp=qp, it=it):
                            v.wait_ge(sem["pe"], M[f"pv_{it}"])
                            ps = PC if qp % 2 == 0 else PD
                            m = v.tensor_copy(oth[:, qp * 256:(qp + 1) * 256], ps[:, 0:256])
                            M[f"otev_{it}"] = inc(m, "dve")

                    def emit_fold(qp):
                        it = f"{gi}_{hl}_{qp}"

                        @block.vector
                        def _(v, qp=qp, it=it):
                            buf = pt[qp % 2]
                            v.wait_ge(sem["act"], exp_mark(qp, 3))
                            if qp > 0:
                                v.wait_ge(sem["pe"], M[f"z_{gi}_{hl}_{qp-1}"])
                            v.tensor_tensor(foldb[:, 0:2048], buf[:, 0:2048], buf[:, 2048:4096], ALU.add)
                            v.tensor_tensor(foldb[:, 0:1024], foldb[:, 0:1024], foldb[:, 1024:2048], ALU.add)
                            v.tensor_tensor(foldb[:, 0:512], foldb[:, 0:512], foldb[:, 512:1024], ALU.add)
                            m = v.tensor_tensor(foldb[:, 0:256], foldb[:, 0:256], foldb[:, 256:512], ALU.add)
                            M[f"f1_{it}"] = inc(m, "dve")

                    if LAG:
                        for qp in range(NQP + 1):
                            if qp < NQP:
                                emit_sg(qp)
                            if qp >= 1:
                                emit_pv(qp - 1)
                            if qp < NQP:
                                emit_fold(qp)
                    else:
                        for qp in range(NQP):
                            emit_sg(qp)
                            emit_fold(qp)
                            emit_pv(qp)
                    free["PC"] = ("dve", M[f"otev_{gi}_{hl}_{NQP-2}"])
                    free["PD"] = ("dve", M[f"otev_{gi}_{hl}_{NQP-1}"])
                    free["PA"] = ("act", exp_mark(NQP - 1, 3))
                    free["PB"] = ("act", exp_mark(NQP - 1, 3))

                    @block.vector
                    def _(v, gi=gi, hl=hl):
                        v.wait_ge(sem["pe"], M[f"z_{gi}_{hl}_{NQP-1}"])
                        m = v.reciprocal(rcol[:, :], PF[:, 0:16])
                        M[f"rcp_{gi}_{hl}"] = inc(m, "dve")
                    free["PF"] = ("dve", M[f"rcp_{gi}_{hl}"])

                    @block.tensor
                    def _(t, gi=gi, hl=hl):
                        t.wait_ge(sem["dve"], M[f"rcp_{gi}_{hl}"])
                        wait_free(t, "PE")
                        m = t.transpose(PE_[0:16, 0:128], rcol[:, :], idf[:, :])
                        M[f"rct_{gi}_{hl}"] = inc(m, "pe")

                    @block.vector
                    def _(v, gi=gi, hl=hl):
                        v.wait_ge(sem["pe"], M[f"rct_{gi}_{hl}"])
                        m = v.tensor_copy(rz16[:, :], PE_[0:16, 0:128])
                        M[f"rz_{gi}_{hl}"] = inc(m, "dve")
                    free["PE"] = ("dve", M[f"rz_{gi}_{hl}"])

                    for qp in range(NQP):
                        @block.tensor
                        def _(t, gi=gi, hl=hl, qp=qp):
                            t.wait_ge(sem["dve"], M[f"rz_{gi}_{hl}"])
                            bank = "PE" if qp % 2 == 0 else "PF"
                            ps = PE_ if qp % 2 == 0 else PF
                            if qp >= 2:
                                t.wait_ge(sem["dve"], M[f"nrm_{gi}_{hl}_{qp-2}"])
                            else:
                                wait_free(t, bank)
                            for j in range(2):
                                jj = qp * 2 + j
                                m = t.matmul(ps[:, j * 128:(j + 1) * 128],
                                             selsb[:, jj * 128:(jj + 1) * 128],
                                             rz16[:, :], start=True, stop=True)
                            M[f"bmm_{gi}_{hl}_{qp}"] = inc(m, "pe")

                        @block.vector
                        def _(v, gi=gi, hl=hl, qp=qp, h=h):
                            v.wait_ge(sem["pe"], M[f"bmm_{gi}_{hl}_{qp}"])
                            if qp == 0 and h >= 2:
                                v.wait_ge(sem["dS"], M[f"ots_{h-2}"])
                            ps = PE_ if qp % 2 == 0 else PF
                            m = v.tensor_tensor(otn[hl][:, qp * 256:(qp + 1) * 256],
                                                oth[:, qp * 256:(qp + 1) * 256],
                                                ps[:, 0:256], ALU.mult)
                            M[f"nrm_{gi}_{hl}_{qp}"] = inc(m, "dve")
                    free["PE"] = ("dve", M[f"nrm_{gi}_{hl}_{NQP-2}"])
                    free["PF"] = ("dve", M[f"nrm_{gi}_{hl}_{NQP-1}"])

                    @block.sync
                    def _(s_, gi=gi, hl=hl, h=h):
                        for qp in range(NQP):
                            s_.wait_ge(sem["dve"], M[f"nrm_{gi}_{hl}_{qp}"])
                        m = s_.dma_start(out=ot_hbm[h, :, :], in_=otn[hl][:, :])
                        M[f"ots_{h}"] = inc(m, "dS", 16)

            if stage == 2:
                @block.gpsimd
                def _(g):
                    g.wait_ge(sem["dve"], M["rope_0_0_0_1"])
                    inc(g.dma_start(out=dbg["d_q0"], in_=qk[:, 0:S]), "st", 16)
                    g.wait_ge(sem["dve"], M["rope_0_0_1_1"])
                    inc(g.dma_start(out=dbg["d_k0"], in_=qk[:, 2 * S:3 * S]), "st", 16)
                    g.wait_ge(sem["dve"], M["vev_0_0"])
                    inc(g.dma_start(out=dbg["d_v0"], in_=vsb[:, 0:256]), "st", 16)
            if stage in (2, 3):
                @block.gpsimd
                def _(g):
                    g.wait_ge(sem["dS"], M["ots_0"])
                    inc(g.dma_start(out=dbg["d_ot0"], in_=ot_hbm[0, :, :]), "st", 16)
                    g.wait_ge(sem["dS"], M[f"ots_{2*ngroups-1}"])
                    inc(g.dma_start(out=dbg["d_ot15"], in_=ot_hbm[2*ngroups-1, :, :]), "st", 16)

        if stage >= 4:
            bctx.close()

            @block.gpsimd
            def _(g):
                g.wait_ge(sem["pe"], M[f"v_{NG-1}_{NT-1}"])
                m = g.dma_start(out=xT[:, :].rearrange("p (c f) -> p c f", c=NCH),
                                in_=wo_d.rearrange("(c p) f -> p c f", p=128))
                M["wo"] = inc(m, "dG", 16)

            cctx = ctx.enter_context(ExitStack())
            EC = cctx.enter_context
            otc = [EC(nc.sbuf_tensor(f"otc{i}", [128, S], BF16)) for i in range(2)]
            xtl = [EC(nc.sbuf_tensor(f"xtl{i}", [128, D], F32)) for i in range(2)]
            z1 = EC(nc.sbuf_tensor("z1", [128, D], F32))
            x1 = [EC(nc.sbuf_tensor(f"x1_{i}", [128, D], F32)) for i in range(2)]
            x1b = [EC(nc.sbuf_tensor(f"x1b{i}", [128, D], BF16)) for i in range(2)]
            x1t = [EC(nc.sbuf_tensor(f"x1t{i}", [128, D], BF16)) for i in range(2)]
            sqs = EC(nc.sbuf_tensor("sqs", [128, 512], F32))
            z2 = EC(nc.sbuf_tensor("z2", [128, D], F32))
            outb = [EC(nc.sbuf_tensor(f"outb{i}", [128, D], F32)) for i in range(2)]
            ts1 = EC(nc.sbuf_tensor("ts1", [128, 512], F32))
            stat = [EC(nc.sbuf_tensor(f"stat{i}", [128, 52], F32)) for i in range(2)]
            gsc = [EC(nc.sbuf_tensor(f"gsc{i}", [128, 8], F32)) for i in range(2)]
            prs = EC(nc.sbuf_tensor("prs", [128, 104], F32))
            pr2s = EC(nc.sbuf_tensor("pr2s", [128, 64], F32))
            prr = EC(nc.sbuf_tensor("prr", [128, 64], F32))
            mmbv = [EC(nc.sbuf_tensor(f"mmb{i}", [128, 64], BF16)) for i in range(2)]
            dbgg = EC(nc.sbuf_tensor("dbgg", [128, NT * 16], F32)) if DBG4 else None
            ebf = [EC(nc.sbuf_tensor(f"ebf{i}", [128, 8], BF16)) for i in range(2)]
            ets = [EC(nc.sbuf_tensor(f"ets{i}", [8, 128], BF16)) for i in range(2)]
            mts = [EC(nc.sbuf_tensor(f"mts{i}", [64, 128], BF16)) for i in range(2)]



            def newton_rsqrt(v, st_, vcol, ycol, tmp, j):
                A_, B_ = 0.430762, 0.476801
                sp = lambda: v.tensor_copy(st_[:, 33:34], st_[:, 8:9])
                v.tensor_scalar_min(st_[:, tmp[0]:tmp[0] + 1], st_[:, vcol:vcol + 1], 7.5)
                sp()
                sp()
                v.reciprocal(st_[:, tmp[1]:tmp[1] + 1], st_[:, tmp[0]:tmp[0] + 1])
                sp()
                sp()
                v.tensor_scalar(st_[:, ycol:ycol + 1], st_[:, tmp[1]:tmp[1] + 1],
                                -A_ / B_, B_, ALU.subtract, ALU.mult)
                for _ in range(5):
                    sp()
                    v.tensor_tensor(st_[:, tmp[2]:tmp[2] + 1], st_[:, ycol:ycol + 1],
                                    st_[:, ycol:ycol + 1], ALU.mult)
                    sp()
                    v.tensor_tensor(st_[:, tmp[3]:tmp[3] + 1], st_[:, tmp[2]:tmp[2] + 1],
                                    st_[:, vcol:vcol + 1], ALU.mult)
                    sp()
                    v.tensor_scalar(st_[:, tmp[2]:tmp[2] + 1], st_[:, tmp[3]:tmp[3] + 1],
                                    3.0, -0.5, ALU.subtract, ALU.mult)
                    sp()
                    m = v.tensor_tensor(st_[:, ycol:ycol + 1], st_[:, ycol:ycol + 1],
                                        st_[:, tmp[2]:tmp[2] + 1], ALU.mult)
                return m
            def front(j):
                bi = j % 2
                snap_ser()

                @block.sync
                def _(s_, j=j, bi=bi):
                    ser(s_)
                    if j == 0:
                        for h in range(NH):
                            s_.wait_ge(sem["dS"], M[f"ots_{h}"])
                    if j >= 2:
                        s_.wait_ge(sem["pe"], M[f"y_{j-2}_3"])
                        s_.wait_ge(sem["dve"], M[f"z1_{j-2}_3"])
                    m = s_.dma_start(out=xtl[bi][:, :], in_=x_tm[j * 128:(j + 1) * 128, :])
                    inc(m, "dS", 16)
                    m = s_.dma_start(out=otc[bi][:, :].rearrange("p (h t) -> p h t", h=NH),
                                     in_=ot_hbm[:, :, j * 128:(j + 1) * 128].rearrange("h p t -> p h t"))
                    M[f"cin_{j}"] = inc(m, "dS", 16)

                dve_fb = C["dve"]
                z1_mark = lambda p: dve_fb + p + 1

                @block.tensor
                def _(t, j=j, bi=bi, z1_mark=z1_mark):
                    ser(t)
                    t.wait_ge(sem["dS"], M[f"cin_{j}"])
                    if j == 0:
                        t.wait_ge(sem["dG"], M["wo"])
                        wait_free(t, "PA")
                        wait_free(t, "PB")
                    for p in range(4):
                        ps = PA if p % 2 == 0 else PB
                        if p >= 2:
                            t.wait_ge(sem["dve"], z1_mark(p - 2))
                        elif j > 0:
                            t.wait_ge(sem["dve"], M[f"z1_{j-1}_{p+2}"])
                        for c in range(NCH):
                            t.matmul(ps[:, 0:512], otc[bi][:, c * 128:(c + 1) * 128],
                                     xT[:, c * S + p * 512: c * S + p * 512 + 512],
                                     start=(c == 0), stop=False)
                        m = t.matmul(ps[:, 0:512], onerb[0:1, 0:128],
                                     bor[0:1, p * 512:(p + 1) * 512], start=False, stop=True)
                        M[f"y_{j}_{p}"] = inc(m, "pe")

                @block.vector
                def _(v, j=j, bi=bi, z1_mark=z1_mark):
                    ser(v)
                    st_ = stat[bi]
                    for p in range(4):
                        ps = PA if p % 2 == 0 else PB
                        v.wait_ge(sem["pe"], M[f"y_{j}_{p}"])
                        if p == 0 and j > 0:
                            v.wait_ge(sem["dve"], M[f"sq1_{j-1}_3"])
                        m = v.scalar_tensor_tensor(z1[:, p * 512:(p + 1) * 512],
                                                   xtl[bi][:, p * 512:(p + 1) * 512], 1.0,
                                                   ps[:, 0:512], ALU.mult, ALU.add,
                                                   accum_out=st_[:, p:p + 1])
                        M[f"z1_{j}_{p}"] = assert_eq(inc(m, "dve"), z1_mark(p))

                @block.vector
                def _(v, j=j, bi=bi):
                    ser(v)
                    st_ = stat[bi]
                    for p in range(4):
                        m = v.scalar_tensor_tensor(sqs[:, :], z1[:, p * 512:(p + 1) * 512], 1.0,
                                                   z1[:, p * 512:(p + 1) * 512], ALU.mult, ALU.mult,
                                                   accum_out=st_[:, 4 + p:5 + p])
                        M[f"sq1_{j}_{p}"] = inc(m, "dve")

                @block.vector
                def _(v, j=j, bi=bi):
                    ser(v)
                    st_ = stat[bi]
                    v.wait_ge(sem["dve"], M[f"sq1_{j}_3"])
                    v.reduce_sum(st_[:, 8:9], st_[:, 0:4], axis=AX.X)
                    v.reduce_sum(st_[:, 9:10], st_[:, 4:8], axis=AX.X)
                    v.tensor_scalar(st_[:, 10:11], st_[:, 8:9], 1.0 / D, None, ALU.mult)
                    v.tensor_scalar(st_[:, 11:12], st_[:, 9:10], 1.0 / D, None, ALU.mult)
                    v.tensor_tensor(st_[:, 12:13], st_[:, 10:11], st_[:, 10:11], ALU.mult)
                    v.tensor_copy(st_[:, 33:34], st_[:, 8:9])
                    v.tensor_copy(st_[:, 33:34], st_[:, 9:10])
                    v.tensor_tensor(st_[:, 13:14], st_[:, 11:12], st_[:, 12:13], ALU.subtract)
                    v.tensor_copy(st_[:, 33:34], st_[:, 8:9])
                    v.tensor_copy(st_[:, 33:34], st_[:, 9:10])
                    m = v.tensor_scalar(st_[:, 14:15], st_[:, 13:14], EPS, None, ALU.add)
                    M[f"st1_{j}"] = inc(m, "dve")

                @block.vector
                def _(v, j=j, bi=bi):
                    ser(v)
                    st_ = stat[bi]
                    m = newton_rsqrt(v, st_, 14, 16, (41, 42, 43, 15), j)
                    M[f"rs1_{j}"] = inc(m, "dve")
                    v.tensor_copy(st_[:, 33:34], st_[:, 8:9])
                    v.tensor_tensor(st_[:, 17:18], st_[:, 16:17], st_[:, 16:17], ALU.mult)
                    v.tensor_scalar(x1[bi][:, :], z1[:, :], st_[:, 10:11], st_[:, 16:17],
                                    ALU.subtract, ALU.mult)
                    v.tensor_scalar(st_[:, 19:20], st_[:, 17:18], -EPS * float(D), None, ALU.mult)
                    v.tensor_copy(st_[:, 33:34], st_[:, 14:15])
                    v.tensor_copy(st_[:, 33:34], st_[:, 17:18])
                    v.tensor_scalar(st_[:, 18:19], st_[:, 19:20], float(D), None, ALU.add)
                    v.tensor_copy(st_[:, 33:34], st_[:, 14:15])
                    v.tensor_copy(st_[:, 33:34], st_[:, 17:18])
                    m = v.tensor_scalar(st_[:, 44:52], m2bc[:, :], st_[:, 18:19], None, ALU.add)
                    if DBG4:
                        v.tensor_copy(dbgg[:, j * 16 + 12:j * 16 + 13], st_[:, 16:17])
                        v.tensor_copy(dbgg[:, j * 16 + 13:j * 16 + 14], st_[:, 17:18])
                        v.tensor_copy(dbgg[:, j * 16 + 14:j * 16 + 15], st_[:, 14:15])
                        v.tensor_copy(dbgg[:, j * 16 + 15:j * 16 + 16], st_[:, 18:19])
                    M[f"x1_{j}"] = inc(m, "dve")

                @block.scalar
                def _(sc, j=j, bi=bi):
                    ser(sc)
                    sc.wait_ge(sem["dve"], M[f"x1_{j}"])
                    if j >= 2:
                        sc.wait_ge(sem["pe"], M[f"x1tB_{j-2}"])
                    m = sc.activation(x1b[bi][:, :], x1[bi][:, :], AF.Copy)
                    M[f"x1b_{j}"] = inc(m, "act")

                @block.tensor
                def _(t, j=j, bi=bi):
                    ser(t)
                    t.wait_ge(sem["act"], M[f"x1b_{j}"])
                    if j >= 2:
                        t.wait_ge(sem["dve"], M[f"etv_{j-2}"])
                    elif j == 0:
                        wait_free(t, "PD")
                    for c in range(8):
                        m = t.transpose(PDb[:, c * 128:(c + 1) * 128],
                                        x1b[bi][:, c * 128:(c + 1) * 128], idb[:, :])
                    M[f"x1tA_{j}"] = inc(m, "pe")

                @block.vector
                def _(v, j=j, bi=bi):
                    ser(v)
                    v.wait_ge(sem["pe"], M[f"x1tA_{j}"])
                    m = v.tensor_copy(x1t[bi][:, 0:1024], PDb[:, 0:1024])
                    M[f"x1eA_{j}"] = inc(m, "dve")

                @block.tensor
                def _(t, j=j, bi=bi):
                    ser(t)
                    t.wait_ge(sem["dve"], M[f"x1eA_{j}"])
                    for c in range(8, NCH):
                        m = t.transpose(PDb[:, (c - 8) * 128:(c - 7) * 128],
                                        x1b[bi][:, c * 128:(c + 1) * 128], idb[:, :])
                    M[f"x1tB_{j}"] = inc(m, "pe")

                @block.vector
                def _(v, j=j, bi=bi):
                    ser(v)
                    v.wait_ge(sem["pe"], M[f"x1tB_{j}"])
                    m = v.tensor_copy(x1t[bi][:, 1024:2048], PDb[:, 0:1024])
                    M[f"x1t_{j}"] = inc(m, "dve")

            def back(j):
                bi = j % 2
                snap_ser()

                @block.tensor
                def _(t, j=j, bi=bi):
                    ser(t)
                    t.wait_ge(sem["dve"], M[f"x1t_{j}"])
                    if j >= 1:
                        t.wait_ge(sem["dve"], M[f"gate_{j-1}"])
                    else:
                        wait_free(t, "PC")
                    for c in range(NCH):
                        t.matmul(PC[:, 0:136], x1t[bi][:, c * 128:(c + 1) * 128],
                                 ffnw[:, c * 136:(c + 1) * 136], start=(c == 0), stop=False)
                    m = t.matmul(PC[:, 0:136], onerb[0:1, 0:128], crow[0:1, :],
                                 start=False, stop=True)
                    M[f"ffn_{j}"] = inc(m, "pe")

                @block.vector
                def _(v, j=j):
                    ser(v)
                    v.wait_ge(sem["pe"], M[f"ffn_{j}"])
                    v.tensor_copy(prr[:, :], PC[:, 8:72])
                    v.tensor_copy(prs[:, 96:104], m2bc[:, :])
                    v.tensor_copy(prs[:, 96:104], cenbc[:, :])
                    m = v.scalar_tensor_tensor(pr2s[:, :], prr[:, :], 1.0, prr[:, :],
                                               ALU.mult, ALU.mult)
                    M[f"pr2_{j}"] = inc(m, "dve")

                @block.vector
                def _(v, j=j, bi=bi):
                    ser(v)
                    st_ = stat[bi]
                    gs = gsc[bi]
                    v.wait_ge(sem["dve"], M[f"pr2_{j}"])
                    v.scalar_tensor_tensor(prs[:, 8:16], PC[:, 0:8], -2.0, st_[:, 44:52],
                                           ALU.mult, ALU.add)
                    for n in range(8):
                        v.reduce_sum(prs[:, 40 + n:41 + n], pr2s[:, n * 8:(n + 1) * 8], axis=AX.X)
                    v.reciprocal(prs[:, 16:24], prs[:, 8:16])
                    v.tensor_copy(prs[:, 96:104], m2bc[:, :])
                    v.tensor_copy(prs[:, 96:104], cenbc[:, :])
                    v.tensor_tensor(prs[:, 24:32], prs[:, 40:48], prs[:, 16:24], ALU.mult)
                    v.tensor_copy(prs[:, 96:104], m2bc[:, :])
                    v.tensor_copy(prs[:, 96:104], cenbc[:, :])
                    v.tensor_tensor(prs[:, 32:40], prs[:, 24:32], cenbc[:, :], ALU.subtract)
                    v.tensor_copy(prs[:, 96:104], m2bc[:, :])
                    v.tensor_copy(prs[:, 96:104], cenbc[:, :])
                    m = v.tensor_tensor(prs[:, 48:56], prs[:, 32:40], slpbc[:, :], ALU.mult)
                    M[f"arg_{j}"] = inc(m, "dve")

                @block.scalar
                def _(sc, j=j, bi=bi):
                    ser(sc)
                    gs = gsc[bi]
                    sc.wait_ge(sem["dve"], M[f"arg_{j}"])
                    m = sc.activation(prs[:, 56:64], prs[:, 48:56], AF.Exp, scale=-1.0)
                    M[f"eneg_{j}"] = inc(m, "act")

                @block.vector
                def _(v, j=j, bi=bi):
                    ser(v)
                    st_ = stat[bi]
                    gs = gsc[bi]
                    v.wait_ge(sem["act"], M[f"eneg_{j}"])
                    v.tensor_scalar(prs[:, 64:72], prs[:, 56:64], 1.0, None, ALU.add)
                    v.tensor_copy(prs[:, 96:104], m2bc[:, :])
                    v.tensor_copy(prs[:, 96:104], cenbc[:, :])
                    v.reciprocal(gs[:, :], prs[:, 64:72])
                    v.tensor_copy(prs[:, 96:104], m2bc[:, :])
                    v.tensor_copy(prs[:, 96:104], cenbc[:, :])
                    v.reduce_max(st_[:, 30:31], gs[:, :], axis=AX.X)
                    v.tensor_copy(prs[:, 96:104], m2bc[:, :])
                    v.tensor_copy(prs[:, 96:104], cenbc[:, :])
                    v.tensor_scalar(st_[:, 31:32], st_[:, 30:31], 2.0, -1.0, ALU.subtract, ALU.mult)
                    v.tensor_scalar(prs[:, 72:80], gs[:, :], st_[:, 30:31], None, ALU.is_ge)
                    v.tensor_copy(prs[:, 96:104], m2bc[:, :])
                    v.tensor_copy(prs[:, 96:104], cenbc[:, :])
                    v.tensor_tensor(prs[:, 80:88], prs[:, 72:80], wties[:, :], ALU.mult)
                    v.tensor_copy(prs[:, 96:104], m2bc[:, :])
                    v.tensor_copy(prs[:, 96:104], cenbc[:, :])
                    v.reduce_max(st_[:, 32:33], prs[:, 80:88], axis=AX.X)
                    v.tensor_copy(st_[:, 33:34], st_[:, 30:31])
                    v.tensor_copy(st_[:, 33:34], st_[:, 31:32])
                    v.tensor_scalar(prs[:, 88:96], prs[:, 80:88], st_[:, 32:33], None, ALU.is_ge)
                    v.tensor_copy(prs[:, 96:104], m2bc[:, :])
                    v.tensor_copy(prs[:, 96:104], cenbc[:, :])
                    v.tensor_copy(ebf[bi][:, :], prs[:, 88:96])
                    if DBG4:
                        v.tensor_copy(dbgg[:, j * 16:j * 16 + 8], gs[:, :])
                        v.tensor_copy(dbgg[:, j * 16 + 8:j * 16 + 9], st_[:, 30:31])
                        v.tensor_copy(dbgg[:, j * 16 + 9:j * 16 + 10], st_[:, 18:19])
                        v.tensor_copy(dbgg[:, j * 16 + 10:j * 16 + 11], prs[:, 8:9])
                        v.tensor_copy(dbgg[:, j * 16 + 11:j * 16 + 12], prs[:, 88:89])
                    last = None
                    for n in range(8):
                        last = v.tensor_scalar(mmbv[bi][:, n * 8:(n + 1) * 8],
                                               PC[:, 72 + n * 8: 80 + n * 8],
                                               prs[:, 88 + n:89 + n], None, ALU.mult)
                    M[f"gate_{j}"] = inc(last, "dve")

                @block.tensor
                def _(t, j=j, bi=bi):
                    ser(t)
                    t.wait_ge(sem["dve"], M[f"gate_{j}"])
                    t.wait_ge(sem["dve"], M[f"x1t_{j+1}"] if f"x1t_{j+1}" in M else M[f"x1t_{j}"])
                    t.transpose(PDb[0:8, 0:128], ebf[bi][:, :], idb[:, :])
                    m = t.transpose(PDb[0:64, 128:256], mmbv[bi][:, :], idb[:, :])
                    M[f"ett_{j}"] = inc(m, "pe")

                @block.vector
                def _(v, j=j, bi=bi):
                    ser(v)
                    v.wait_ge(sem["pe"], M[f"ett_{j}"])
                    v.tensor_copy(ets[bi][:, :], PDb[0:8, 0:128])
                    m = v.tensor_copy(mts[bi][:, :], PDb[0:64, 128:256])
                    M[f"etv_{j}"] = inc(m, "dve")

                dve_zb = C["dve"]
                z2_mark = lambda p: dve_zb + p + 1

                @block.tensor
                def _(t, j=j, bi=bi, z2_mark=z2_mark):
                    ser(t)
                    t.wait_ge(sem["dve"], M[f"etv_{j}"])
                    for p in range(4):
                        bank = "PE" if p % 2 == 0 else "PF"
                        ps = PE_ if p % 2 == 0 else PF
                        if p >= 2:
                            t.wait_ge(sem["dve"], z2_mark(p - 2))
                        elif j > 0:
                            t.wait_ge(sem["dve"], M[f"z2_{j-1}_{p+2}"])
                        else:
                            wait_free(t, bank)
                        t.matmul(ps[:, 0:512], mts[bi][:, :], dgn[:, p * 512:(p + 1) * 512],
                                 start=True, stop=False)
                        m = t.matmul(ps[:, 0:512], ets[bi][:, :], bwsb[:, p * 512:(p + 1) * 512],
                                     start=False, stop=True)
                        M[f"mbx_{j}_{p}"] = inc(m, "pe")

                @block.vector
                def _(v, j=j, bi=bi, z2_mark=z2_mark):
                    ser(v)
                    st_ = stat[bi]
                    for p in range(4):
                        ps = PE_ if p % 2 == 0 else PF
                        v.wait_ge(sem["pe"], M[f"mbx_{j}_{p}"])
                        if p == 0 and j > 0:
                            v.wait_ge(sem["dve"], M[f"sq2_{j-1}_3"])
                        v.tensor_scalar(ts1[:, :], x1[bi][:, p * 512:(p + 1) * 512],
                                        st_[:, 31:32], None, ALU.mult)
                        m = v.scalar_tensor_tensor(z2[:, p * 512:(p + 1) * 512], ps[:, 0:512],
                                                   st_[:, 30:31], ts1[:, :], ALU.mult, ALU.add,
                                                   accum_out=st_[:, 20 + p:21 + p])
                        M[f"z2_{j}_{p}"] = inc(m, "dve")
                        assert M[f"z2_{j}_{p}"] == z2_mark(p), (M[f"z2_{j}_{p}"], z2_mark(p))

                @block.vector
                def _(v, j=j, bi=bi):
                    ser(v)
                    st_ = stat[bi]
                    for p in range(4):
                        m = v.scalar_tensor_tensor(sqs[:, :], z2[:, p * 512:(p + 1) * 512], 1.0,
                                                   z2[:, p * 512:(p + 1) * 512], ALU.mult, ALU.mult,
                                                   accum_out=st_[:, 24 + p:25 + p])
                        M[f"sq2_{j}_{p}"] = inc(m, "dve")

                @block.vector
                def _(v, j=j, bi=bi):
                    ser(v)
                    st_ = stat[bi]
                    v.wait_ge(sem["dve"], M[f"sq2_{j}_3"])
                    v.reduce_sum(st_[:, 28:29], st_[:, 20:24], axis=AX.X)
                    v.reduce_sum(st_[:, 29:30], st_[:, 24:28], axis=AX.X)
                    v.tensor_scalar(st_[:, 34:35], st_[:, 28:29], 1.0 / D, None, ALU.mult)
                    v.tensor_scalar(st_[:, 35:36], st_[:, 29:30], 1.0 / D, None, ALU.mult)
                    v.tensor_tensor(st_[:, 36:37], st_[:, 34:35], st_[:, 34:35], ALU.mult)
                    v.tensor_copy(st_[:, 33:34], st_[:, 28:29])
                    v.tensor_copy(st_[:, 33:34], st_[:, 29:30])
                    v.tensor_tensor(st_[:, 37:38], st_[:, 35:36], st_[:, 36:37], ALU.subtract)
                    v.tensor_copy(st_[:, 33:34], st_[:, 28:29])
                    v.tensor_copy(st_[:, 33:34], st_[:, 29:30])
                    m = v.tensor_scalar(st_[:, 38:39], st_[:, 37:38], EPS, None, ALU.add)
                    M[f"st2_{j}"] = inc(m, "dve")

                @block.vector
                def _(v, j=j, bi=bi):
                    ser(v)
                    st_ = stat[bi]
                    m = newton_rsqrt(v, st_, 38, 40, (41, 42, 43, 39), j)
                    M[f"rs2_{j}"] = inc(m, "dve")
                    if j >= 2:
                        v.wait_ge(sem["st"], M[f"yst_{j-2}"])
                    v.tensor_copy(st_[:, 33:34], st_[:, 8:9])
                    v.tensor_copy(st_[:, 33:34], st_[:, 9:10])
                    m = v.tensor_scalar(outb[bi][:, :], z2[:, :], st_[:, 34:35], st_[:, 40:41],
                                        ALU.subtract, ALU.mult)
                    M[f"out_{j}"] = inc(m, "dve")

                @block.sync
                def _(s_, j=j, bi=bi):
                    ser(s_)
                    s_.wait_ge(sem["dve"], M[f"out_{j}"])
                    m = s_.dma_start(out=y_d[j * 128:(j + 1) * 128, :], in_=outb[bi][:, :])
                    M[f"yst_{j}"] = inc(m, "st", 16)

            for j in range(NT):
                front(j)
                if j >= 1:
                    back(j - 1)
            back(NT - 1)

            if DBG4:
                @block.gpsimd
                def _(g):
                    g.wait_ge(sem["dve"], M[f"x1t_{NT-1}"])
                    inc(g.dma_start(out=dbg["d_ota"], in_=ot_hbm[:, :, :]), "st", 16)
                    g.wait_ge(sem["dve"], M[f"gate_{NT-1}"])
                    inc(g.dma_start(out=dbg["d_gg"], in_=dbgg[:, :]), "st", 16)
                    g.wait_ge(sem["dve"], M["p1_done"])
                    inc(g.dma_start(out=dbg["d_slpbc"], in_=slpbc[:, :]), "st", 16)
                    g.wait_ge(sem["dve"], M[f"etv_{NT-1}"])
                    g.wait_ge(sem["dve"], M[f"out_{NT-1}"])
                    inc(g.dma_start(out=dbg["d_ets"], in_=ets[(NT-1) % 2][:, :]), "st", 16)
                    inc(g.dma_start(out=dbg["d_mts"], in_=mts[(NT-1) % 2][:, :]), "st", 16)
                    inc(g.dma_start(out=dbg["d_ebf"], in_=ebf[(NT-1) % 2][:, :]), "st", 16)
                    inc(g.dma_start(out=dbg["d_mmb"], in_=mmbv[(NT-1) % 2][:, :]), "st", 16)

            @block.sync
            def _(s_):
                s_.wait_ge(sem["st"], C["st"])
        else:
            @block.sync
            def _(s_):
                s_.wait_ge(sem["st"], C["st"])

    return nc, dbg_names


def assert_eq(a, b):
    assert a == b, (a, b)
    return a


def make_in_maps(inputs):
    x = np.asarray(inputs["L0_attn_hidden_state"])
    perm = _perm_deinterleave()
    Wq = np.ascontiguousarray(np.asarray(inputs["L0_attn_Wq"])[:, perm])
    Wk = np.ascontiguousarray(np.asarray(inputs["L0_attn_Wk"])[:, perm])
    Wv = np.ascontiguousarray(np.asarray(inputs["L0_attn_Wv"]))
    Wo = np.ascontiguousarray(np.asarray(inputs["L0_attn_Wo"]))
    bq = np.asarray(inputs["L0_attn_bq"])[perm][None, :].copy()
    bk = np.asarray(inputs["L0_attn_bk"])[perm][None, :].copy()
    bv = np.asarray(inputs["L0_attn_bv"])[None, :].copy()
    bo = np.asarray(inputs["L0_attn_bo"])[None, :].copy()
    selw = np.asarray(inputs["L0_ffn_select_weight"])
    md = np.asarray(inputs["L0_ffn_select_mean_diff"])
    cen = np.asarray(inputs["L0_ffn_imp_center"])[None, :].copy()
    slp = np.asarray(inputs["L0_ffn_imp_slope"])[None, :].copy()
    upd = np.asarray(inputs["L0_ffn_lora_update"])
    dgnw = np.asarray(inputs["L0_ffn_lora_degen"])
    bwv = np.ascontiguousarray(np.asarray(inputs["L0_ffn_bias_w"]))
    debw = np.asarray(inputs["L0_ffn_debias_w"])

    ffnw = np.ascontiguousarray(
        np.concatenate([md.T, selw.transpose(1, 0, 2).reshape(D, 64),
                        upd.transpose(1, 0, 2).reshape(D, 64)], axis=1)).astype(np.float32)
    dgnt = np.ascontiguousarray(dgnw.transpose(0, 2, 1).reshape(64, D))
    debt = np.ascontiguousarray(debw.T)
    ropec, ropes = _rope_tables()
    sel16 = np.zeros((16, D), dtype=np.float32)
    for jj in range(16):
        sel16[jj, jj * 128:(jj + 1) * 128] = 1.0

    in_maps = []
    for b in range(8):
        in_maps.append({
            "x_tm": np.ascontiguousarray(x[b]),
            "x_fm": np.ascontiguousarray(x[b].T),
            "wq": Wq, "wk": Wk, "wv": Wv, "wo": Wo,
            "bq": bq, "bk": bk, "bv": bv, "bo": bo,
            "ffnw": ffnw, "dgnt": dgnt, "bw": bwv, "debt": debt,
            "cen": cen, "slp": slp,
            "ropec": ropec, "ropes": ropes, "sel16": sel16,
        })
    return in_maps


def kernel(**inputs):
    in_maps = make_in_maps(inputs)
    nc, _ = build_nc(STAGE)
    res = run_bass_kernel_spmd(nc, in_maps, core_ids=list(range(8)))
    return np.stack([np.asarray(r["y"], dtype=np.float32) for r in res.results], axis=0)


# version 08
# speedup vs baseline: 1.2903x; 1.2903x over previous
import math
import numpy as np
import ml_dtypes
import concourse.bass as bass
import concourse.mybir as mybir
from concourse.bass_utils import run_bass_kernel_spmd
from contextlib import ExitStack

F32 = mybir.dt.float32
BF16 = mybir.dt.bfloat16
AF = mybir.ActivationFunctionType
ALU = mybir.AluOpType
AX = mybir.AxisListType
bfloat16 = ml_dtypes.bfloat16

D = 2048
S = 2048
NH = 16
HD = 128
NCH = 16
NT = 16
G = 2
NG = NH // G
WQ = 256
NQP = S // WQ
SCALE = 1.0 / math.sqrt(HD)
EPS = 1e-5

STAGE = 4
DBG4 = False
SERIAL = False
LAG = True
PIN_ACT = False
ACT_SET_ID = 0


def _perm_deinterleave():
    p = np.concatenate([np.arange(0, HD, 2), np.arange(1, HD, 2)])
    return np.concatenate([h * HD + p for h in range(NH)])


def _rope_tables():
    k = np.arange(HD)
    j = (k % 64).astype(np.float64)
    inv = np.power(10000.0, -2.0 * j / HD)
    ang = inv[:, None] * np.arange(S, dtype=np.float64)[None, :]
    c = np.cos(ang).astype(np.float32).astype(bfloat16)
    s = np.sin(ang).astype(np.float32).astype(bfloat16)
    return c, s


def build_nc(stage=STAGE):
    nc = bass.Bass("TRN2", target_bir_lowering=False, debug=False)
    dbg_names = []

    x_tm = nc.dram_tensor("x_tm", [S, D], F32, kind="ExternalInput").ap()
    x_fm = nc.dram_tensor("x_fm", [D, S], F32, kind="ExternalInput").ap()
    wq_d = nc.dram_tensor("wq", [D, D], F32, kind="ExternalInput").ap()
    wk_d = nc.dram_tensor("wk", [D, D], F32, kind="ExternalInput").ap()
    wv_d = nc.dram_tensor("wv", [D, D], F32, kind="ExternalInput").ap()
    wo_d = nc.dram_tensor("wo", [D, D], F32, kind="ExternalInput").ap()
    bq_d = nc.dram_tensor("bq", [1, D], F32, kind="ExternalInput").ap()
    bk_d = nc.dram_tensor("bk", [1, D], F32, kind="ExternalInput").ap()
    bv_d = nc.dram_tensor("bv", [1, D], F32, kind="ExternalInput").ap()
    bo_d = nc.dram_tensor("bo", [1, D], F32, kind="ExternalInput").ap()
    ffnw_d = nc.dram_tensor("ffnw", [D, 136], F32, kind="ExternalInput").ap()
    dgnt_d = nc.dram_tensor("dgnt", [64, D], F32, kind="ExternalInput").ap()
    bw_d = nc.dram_tensor("bw", [8, D], F32, kind="ExternalInput").ap()
    debt_d = nc.dram_tensor("debt", [D, 8], F32, kind="ExternalInput").ap()
    cen_d = nc.dram_tensor("cen", [1, 8], F32, kind="ExternalInput").ap()
    slp_d = nc.dram_tensor("slp", [1, 8], F32, kind="ExternalInput").ap()
    ropec_d = nc.dram_tensor("ropec", [HD, S], BF16, kind="ExternalInput").ap()
    ropes_d = nc.dram_tensor("ropes", [HD, S], BF16, kind="ExternalInput").ap()
    sel_d = nc.dram_tensor("sel16", [16, D], F32, kind="ExternalInput").ap()

    y_d = nc.dram_tensor("y", [S, D], F32, kind="ExternalOutput").ap()
    ot_hbm = nc.dram_tensor("ot_hbm", [NH, HD, S], BF16, kind="Internal").ap()
    x1b_hbm = nc.dram_tensor("x1b_hbm", [S, D], BF16, kind="Internal").ap()

    dbg = {}
    if stage == 4 and DBG4:
        dbg["d_x1"] = nc.dram_tensor("d_x1", [S, D], F32, kind="ExternalOutput").ap()
        dbg_names.append("d_x1")
        dbg["d_ota"] = nc.dram_tensor("d_ota", [NH, HD, S], F32, kind="ExternalOutput").ap()
        dbg_names.append("d_ota")
        dbg["d_gg"] = nc.dram_tensor("d_gg", [128, NT * 16], F32, kind="ExternalOutput").ap()
        dbg_names.append("d_gg")
        dbg["d_slpbc"] = nc.dram_tensor("d_slpbc", [128, 8], F32, kind="ExternalOutput").ap()
        dbg_names.append("d_slpbc")
        dbg["d_ets"] = nc.dram_tensor("d_ets", [8, 128], F32, kind="ExternalOutput").ap()
        dbg["d_mts"] = nc.dram_tensor("d_mts", [64, 128], F32, kind="ExternalOutput").ap()
        dbg["d_ebf"] = nc.dram_tensor("d_ebf", [128, 8], F32, kind="ExternalOutput").ap()
        dbg["d_mmb"] = nc.dram_tensor("d_mmb", [128, 64], F32, kind="ExternalOutput").ap()
        dbg_names.extend(["d_ets", "d_mts", "d_ebf", "d_mmb"])
    if stage < 4:
        for nm, shp in (("d_crow", [1, 136]), ("d_m2bc", [128, 8]), ("d_cen", [128, 8])):
            dbg[nm] = nc.dram_tensor(nm, shp, F32, kind="ExternalOutput").ap()
            dbg_names.append(nm)
    if stage in (2, 3):
        for nm, shp in (("d_q0", [HD, S]), ("d_k0", [HD, S]), ("d_v0", [128, 256]),
                        ("d_ot0", [HD, S]), ("d_ot15", [HD, S])):
            dbg[nm] = nc.dram_tensor(nm, shp, F32, kind="ExternalOutput").ap()
            dbg_names.append(nm)

    ident_bf = nc.inline_tensor(np.eye(128, dtype=bfloat16), name="ident_bf").ap()
    ident_f = nc.inline_tensor(np.eye(128, dtype=np.float32), name="ident_f").ap()
    rt = np.zeros((128, 128), dtype=np.float32)
    for i in range(64):
        rt[i, 64 + i] = 1.0
        rt[64 + i, i] = -1.0
    rt_bf = nc.inline_tensor(rt.astype(bfloat16), name="rt_bf").ap()
    ones_r_bf = nc.inline_tensor(np.ones((1, 512), dtype=bfloat16), name="ones_r_bf").ap()
    ones_c_bf = nc.inline_tensor(np.ones((128, 1), dtype=bfloat16), name="ones_c_bf").ap()
    ones_r_f = nc.inline_tensor(np.ones((1, 128), dtype=np.float32), name="ones_r_f").ap()
    ones8_f = nc.inline_tensor(np.ones((8, 1), dtype=np.float32), name="ones8_f").ap()
    eye8_c = nc.inline_tensor(np.eye(8, dtype=np.float32), name="eye8_c").ap()
    m64 = np.zeros((8, 64), dtype=np.float32)
    for n in range(8):
        m64[n, n * 8:(n + 1) * 8] = 1.0
    mask64_c = nc.inline_tensor(m64, name="mask64_c").ap()
    wt = np.tile((8.0 - np.arange(8, dtype=np.float32))[None, :], (128, 1))
    wtie_c = nc.inline_tensor(wt, name="wtie_c").ap()

    with ExitStack() as ctx:
        E = ctx.enter_context
        xT = E(nc.sbuf_tensor("xT", [128, NCH * S], BF16))
        ffnw = E(nc.sbuf_tensor("ffnw_sb", [128, NCH * 136], BF16))
        dgn = E(nc.sbuf_tensor("dgn", [64, D], BF16))
        bwsb = E(nc.sbuf_tensor("bwsb", [8, D], BF16))
        debt = E(nc.sbuf_tensor("debt_sb", [128, NCH * 8], BF16))
        bqr = E(nc.sbuf_tensor("bqr", [1, D], BF16))
        bkr = E(nc.sbuf_tensor("bkr", [1, D], BF16))
        bvr = E(nc.sbuf_tensor("bvr", [1, D], BF16))
        bor = E(nc.sbuf_tensor("bor", [1, D], BF16))
        cenr = E(nc.sbuf_tensor("cenr", [1, 8], F32))
        slpr = E(nc.sbuf_tensor("slpr", [1, 8], F32))
        selsb = E(nc.sbuf_tensor("selsb", [16, D], F32))
        ropecs = E(nc.sbuf_tensor("ropecs", [128, S], BF16))
        ropess = E(nc.sbuf_tensor("ropess", [128, S], BF16))
        idb = E(nc.sbuf_tensor("idb", [128, 128], BF16))
        idf = E(nc.sbuf_tensor("idf", [128, 128], F32))
        rtb = E(nc.sbuf_tensor("rtb", [128, 128], BF16))
        onerb = E(nc.sbuf_tensor("onerb", [1, 512], BF16))
        onecb = E(nc.sbuf_tensor("onecb", [128, 1], BF16))
        onerf = E(nc.sbuf_tensor("onerf", [1, 128], F32))
        one8f = E(nc.sbuf_tensor("one8f", [8, 1], F32))
        eye8s = E(nc.sbuf_tensor("eye8s", [8, 8], F32))
        m64s = E(nc.sbuf_tensor("m64s", [8, 64], F32))
        wties = E(nc.sbuf_tensor("wties", [128, 8], F32))
        crow = E(nc.sbuf_tensor("crow_sb", [1, 136], BF16))
        m2bc = E(nc.sbuf_tensor("m2bc", [128, 8], F32))
        cenbc = E(nc.sbuf_tensor("cenbc", [128, 8], F32))
        slpbc = E(nc.sbuf_tensor("slpbc", [128, 8], F32))
        scr8 = E(nc.sbuf_tensor("scr8", [8, 136], F32))

        PA = E(nc.psum_tensor("PA", [128, 1024], F32))
        PB = E(nc.psum_tensor("PB", [128, 1024], F32))
        PC = E(nc.psum_tensor("PC", [128, 512], F32))
        PD = E(nc.psum_tensor("PD", [128, 512], F32))
        PE_ = E(nc.psum_tensor("PEb", [128, 512], F32))
        PF = E(nc.psum_tensor("PFb", [128, 512], F32))
        PDb = PD[:, :].bitcast(BF16)

        sem = {}
        for name in ("pe", "act", "dve", "dG", "dS", "dA", "st"):
            sem[name] = E(nc.semaphore(name))
        C = {k: 0 for k in sem}
        block = E(nc.Block())
        M = {}

        def inc(ins, s_, n=1):
            ins.then_inc(sem[s_], n)
            C[s_] += n
            return C[s_]

        SNAP = {}

        def snap_ser():
            SNAP.update({k: C[k] for k in ("pe", "act", "dve")})

        def ser(eng):
            if SERIAL:
                for s2 in ("pe", "act", "dve"):
                    eng.wait_ge(sem[s2], SNAP.get(s2, 0))

        free = {"PA": None, "PB": None, "PC": None, "PD": None, "PE": None, "PF": None}

        def wait_free(eng, bank):
            if free[bank] is not None:
                s_, v_ = free[bank]
                eng.wait_ge(sem[s_], v_)

        @block.gpsimd
        def _(g):
            inc(g.dma_start(out=xT[:, :].rearrange("p (c t) -> p c t", c=NCH),
                            in_=x_fm.rearrange("(c p) t -> p c t", p=128)), "dG", 16)
            inc(g.dma_start(out=ffnw[:, :].rearrange("p (c f) -> p c f", c=NCH),
                            in_=ffnw_d.rearrange("(c p) f -> p c f", p=128)), "dG", 16)
            inc(g.dma_start(out=debt[:, :].rearrange("p (c f) -> p c f", c=NCH),
                            in_=debt_d.rearrange("(c p) f -> p c f", p=128)), "dG", 16)
            inc(g.dma_start(out=dgn[:, :], in_=dgnt_d), "dG", 16)
            inc(g.dma_start(out=bwsb[:, :], in_=bw_d), "dG", 16)
            inc(g.dma_start(out=bqr[:, :], in_=bq_d), "dG", 16)
            inc(g.dma_start(out=bkr[:, :], in_=bk_d), "dG", 16)
            inc(g.dma_start(out=bvr[:, :], in_=bv_d), "dG", 16)
            inc(g.dma_start(out=bor[:, :], in_=bo_d), "dG", 16)
            M["p0_gp"] = C["dG"]

        @block.sync
        def _(s_):
            for dst, src in ((idb, ident_bf), (idf, ident_f), (rtb, rt_bf),
                             (onerb, ones_r_bf), (onecb, ones_c_bf), (onerf, ones_r_f),
                             (one8f, ones8_f), (eye8s, eye8_c), (m64s, mask64_c),
                             (wties, wtie_c), (cenr, cen_d), (slpr, slp_d), (selsb, sel_d),
                             (ropecs, ropec_d), (ropess, ropes_d)):
                inc(s_.dma_start(out=dst[:, :], in_=src), "dS", 16)
            M["p0_sy"] = C["dS"]

        if PIN_ACT:
            @block.scalar
            def _(sc):
                sc.add_instruction(mybir.InstLoadActFuncSet(
                    name=nc.get_next_instruction_name(), act_func_set_id=ACT_SET_ID, ins=[], outs=[]))

        @block.tensor
        def _(t):
            t.wait_ge(sem["dG"], M["p0_gp"])
            t.wait_ge(sem["dS"], M["p0_sy"])
            for c in range(NCH):
                f0 = c * 136
                t.matmul(PC[0:8, 0:64], ffnw[:, f0:f0 + 8], ffnw[:, f0 + 8:f0 + 72],
                         start=(c == 0), stop=(c == NCH - 1))
            for c in range(NCH):
                f0 = c * 136
                t.matmul(PC[0:8, 64:128], debt[:, c * 8:(c + 1) * 8], ffnw[:, f0 + 72:f0 + 136],
                         start=(c == 0), stop=(c == NCH - 1))
            for c in range(NCH):
                f0 = c * 136
                m = t.matmul(PC[0:8, 128:136], ffnw[:, f0:f0 + 8], ffnw[:, f0:f0 + 8],
                             start=(c == 0), stop=(c == NCH - 1))
            M["p1_mm"] = inc(m, "pe")

        @block.vector
        def _(v):
            v.wait_ge(sem["pe"], M["p1_mm"])
            v.tensor_tensor(scr8[0:8, 0:64], PC[0:8, 0:64], m64s[:, :], ALU.mult)
            v.tensor_tensor(scr8[0:8, 64:128], PC[0:8, 64:128], m64s[:, :], ALU.mult)
            m = v.tensor_tensor(scr8[0:8, 128:136], PC[0:8, 128:136], eye8s[:, :], ALU.mult)
            M["p1_msk"] = inc(m, "dve")
        free["PC"] = ("dve", M["p1_msk"])

        @block.tensor
        def _(t):
            t.wait_ge(sem["dve"], M["p1_msk"])
            t.matmul(PD[0:1, 0:64], one8f[:, :], scr8[0:8, 0:64], start=True, stop=True)
            t.matmul(PD[0:1, 64:128], one8f[:, :], scr8[0:8, 64:128], start=True, stop=True)
            m = t.matmul(PD[0:1, 128:136], one8f[:, :], scr8[0:8, 128:136], start=True, stop=True)
            M["p1_row"] = inc(m, "pe")

        @block.vector
        def _(v):
            v.wait_ge(sem["pe"], M["p1_row"])
            v.memset(crow[:, :], 0.0)
            v.tensor_scalar(crow[0:1, 8:72], PD[0:1, 0:64], -1.0, None, ALU.mult)
            v.tensor_scalar(crow[0:1, 72:136], PD[0:1, 64:128], -1.0, None, ALU.mult)
            m = v.tensor_copy(scr8[0:1, 0:8], PD[0:1, 128:136])
            M["p1_m2"] = inc(m, "dve")
        free["PD"] = ("dve", M["p1_m2"])

        @block.tensor
        def _(t):
            t.wait_ge(sem["dve"], M["p1_m2"])
            t.matmul(PE_[:, 0:8], onerf[:, :], scr8[0:1, 0:8], start=True, stop=True)
            t.matmul(PE_[:, 8:16], onerf[:, :], cenr[:, :], start=True, stop=True)
            m = t.matmul(PE_[:, 16:24], onerf[:, :], slpr[:, :], start=True, stop=True)
            M["p1_bc"] = inc(m, "pe")

        @block.vector
        def _(v):
            v.wait_ge(sem["pe"], M["p1_bc"])
            v.tensor_copy(m2bc[:, :], PE_[:, 0:8])
            v.tensor_copy(cenbc[:, :], PE_[:, 8:16])
            m = v.tensor_copy(slpbc[:, :], PE_[:, 16:24])
            M["p1_done"] = inc(m, "dve")
        free["PE"] = ("dve", M["p1_done"])

        if stage < 4:
            @block.gpsimd
            def _(g):
                g.wait_ge(sem["dve"], M["p1_done"])
                inc(g.dma_start(out=dbg["d_crow"], in_=crow[:, :]), "st", 16)

            @block.sync
            def _(s_):
                s_.wait_ge(sem["dve"], M["p1_done"])
                inc(s_.dma_start(out=dbg["d_m2bc"], in_=m2bc[:, :]), "st", 16)
                inc(s_.dma_start(out=dbg["d_cen"], in_=cenbc[:, :]), "st", 16)

        if stage >= 2:
            bctx = ctx.enter_context(ExitStack())
            EB = bctx.enter_context
            wsl = {w: EB(nc.sbuf_tensor(f"wsl_{w}", [128, NCH * 256], BF16)) for w in "qkv"}
            qk = EB(nc.sbuf_tensor("qk", [128, 4 * S], BF16))
            vsb = EB(nc.sbuf_tensor("vsb", [128, NT * 256], BF16))
            rotb = EB(nc.sbuf_tensor("rotb", [128, S], BF16))
            foldb = EB(nc.sbuf_tensor("foldb", [128, S], BF16))
            pt = [EB(nc.sbuf_tensor(f"pt{i}", [128, 4096], BF16)) for i in range(2)]
            oth = EB(nc.sbuf_tensor("oth", [128, S], F32))
            otn = [EB(nc.sbuf_tensor(f"otn{i}", [128, S], BF16)) for i in range(2)]
            rcol = EB(nc.sbuf_tensor("rcol", [128, 16], F32))
            rz16 = EB(nc.sbuf_tensor("rz16", [16, 128], F32))
            nsh = EB(nc.sbuf_tensor("nsh", [128, 2], F32))
            sml = EB(nc.sbuf_tensor("sml", [128, 8], F32))
            smr = EB(nc.sbuf_tensor("smr", [1, 16], F32))
            smb = EB(nc.sbuf_tensor("smb", [128, 2], BF16))

            ngroups = NG if stage >= 3 else 1

            @block.gpsimd
            def _(g):
                for w, wd in (("q", wq_d), ("k", wk_d), ("v", wv_d)):
                    m = g.dma_start(out=wsl[w][:, :].rearrange("p (c f) -> p c f", c=NCH),
                                    in_=wd[:, 0:G * 128].rearrange("(c p) f -> p c f", p=128))
                    inc(m, "dG", 16)
                M["wsl_0"] = C["dG"]

            for gi in range(ngroups):
                h0 = G * gi

                act_base = C["act"]
                ev_mark = lambda pp: act_base + pp + 1

                @block.tensor
                def _(t, gi=gi, h0=h0, ev_mark=ev_mark):
                    t.wait_ge(sem["dG"], M[f"wsl_{gi}"])
                    if gi == 0:
                        t.wait_ge(sem["dG"], M["p0_gp"])
                        t.wait_ge(sem["dS"], M["p0_sy"])
                        t.wait_ge(sem["dve"], M["p1_done"])
                    pp = 0
                    for hl in range(G):
                        for wname, brow in (("q", bqr), ("k", bkr)):
                            for p in range(4):
                                ps = PA if pp % 2 == 0 else PB
                                if pp >= 2:
                                    t.wait_ge(sem["act"], ev_mark(pp - 2))
                                else:
                                    wait_free(t, "PA" if pp % 2 == 0 else "PB")
                                for c in range(NCH):
                                    t.matmul(ps[:, 0:512],
                                             wsl[wname][:, c * 256 + hl * 128: c * 256 + hl * 128 + 128],
                                             xT[:, c * S + p * 512: c * S + p * 512 + 512],
                                             start=(c == 0), stop=False)
                                m = t.matmul(ps[:, 0:512],
                                             brow[0:1, (h0 + hl) * 128:(h0 + hl) * 128 + 128],
                                             onerb[0:1, 0:512], start=False, stop=True)
                                M[f"qk_{gi}_{pp}"] = inc(m, "pe")
                                pp += 1

                @block.scalar
                def _(sc, gi=gi):
                    pp = 0
                    for hl in range(G):
                        for wname in ("q", "k"):
                            slot = (0 if wname == "q" else 2) + hl
                            for p in range(4):
                                ps = PA if pp % 2 == 0 else PB
                                sc.wait_ge(sem["pe"], M[f"qk_{gi}_{pp}"])
                                m = sc.activation(qk[:, slot * S + p * 512: slot * S + p * 512 + 512],
                                                  ps[:, 0:512], AF.Copy)
                                assert inc(m, "act") == ev_mark(pp)
                                pp += 1
                free["PA"] = ("act", ev_mark(14))
                free["PB"] = ("act", ev_mark(15))

                dve_vbase = C["dve"]
                vev_mark = lambda tt: dve_vbase + tt + 1

                @block.tensor
                def _(t, gi=gi, h0=h0, vev_mark=vev_mark):
                    wait_free(t, "PC")
                    wait_free(t, "PD")
                    for tt in range(NT):
                        ps = PC if tt % 2 == 0 else PD
                        if tt >= 2:
                            t.wait_ge(sem["dve"], vev_mark(tt - 2))
                        for c in range(NCH):
                            t.matmul(ps[:, 0:256],
                                     xT[:, c * S + tt * 128: c * S + tt * 128 + 128],
                                     wsl["v"][:, c * 256: c * 256 + 256],
                                     start=(c == 0), stop=False)
                        m = t.matmul(ps[:, 0:256], onerb[0:1, 0:128],
                                     bvr[0:1, h0 * 128: h0 * 128 + 256], start=False, stop=True)
                        M[f"v_{gi}_{tt}"] = inc(m, "pe")

                @block.vector
                def _(v, gi=gi, vev_mark=vev_mark):
                    for tt in range(NT):
                        ps = PC if tt % 2 == 0 else PD
                        v.wait_ge(sem["pe"], M[f"v_{gi}_{tt}"])
                        m = v.tensor_copy(vsb[:, tt * 256:(tt + 1) * 256], ps[:, 0:256])
                        M[f"vev_{gi}_{tt}"] = inc(m, "dve")
                        assert M[f"vev_{gi}_{tt}"] == vev_mark(tt)
                free["PC"] = ("dve", vev_mark(14))
                free["PD"] = ("dve", vev_mark(15))

                if gi + 1 < ngroups:
                    @block.gpsimd
                    def _(g, gi=gi):
                        g.wait_ge(sem["pe"], M[f"v_{gi}_{NT-1}"])
                        for w, wd in (("q", wq_d), ("k", wk_d), ("v", wv_d)):
                            c0 = (gi + 1) * G * 128
                            m = g.dma_start(out=wsl[w][:, :].rearrange("p (c f) -> p c f", c=NCH),
                                            in_=wd[:, c0:c0 + G * 128].rearrange("(c p) f -> p c f", p=128))
                            inc(m, "dG", 16)
                        M[f"wsl_{gi+1}"] = C["dG"]

                for hl in range(G):
                    for wi, wname in enumerate(("q", "k")):
                        slot = (0 if wname == "q" else 2) + hl
                        for half in range(2):
                            bank = "PA" if half == 0 else "PB"
                            ps = PA if half == 0 else PB

                            @block.tensor
                            def _(t, gi=gi, hl=hl, wi=wi, slot=slot, half=half, ps=ps, bank=bank):
                                wait_free(t, bank)
                                if wi == 0 and half == 0:
                                    pass
                                for j2 in range(2):
                                    o = half * 1024 + j2 * 512
                                    m = t.matmul(ps[:, j2 * 512:(j2 + 1) * 512], rtb[:, :],
                                                 qk[:, slot * S + o: slot * S + o + 512],
                                                 start=True, stop=True)
                                M[f"rot_{gi}_{hl}_{wi}_{half}"] = inc(m, "pe")

                            @block.scalar
                            def _(sc, gi=gi, hl=hl, wi=wi, half=half, ps=ps):
                                sc.wait_ge(sem["pe"], M[f"rot_{gi}_{hl}_{wi}_{half}"])
                                m = sc.activation(rotb[:, half * 1024:(half + 1) * 1024],
                                                  ps[:, 0:1024], AF.Copy)
                                M[f"roev_{gi}_{hl}_{wi}_{half}"] = inc(m, "act")
                            free[bank] = ("act", M[f"roev_{gi}_{hl}_{wi}_{half}"])

                            @block.vector
                            def _(v, gi=gi, hl=hl, wi=wi, slot=slot, half=half):
                                o = half * 1024
                                v.wait_ge(sem["act"], M[f"roev_{gi}_{hl}_{wi}_{half}"])
                                v.tensor_tensor(rotb[:, o:o + 1024], rotb[:, o:o + 1024],
                                                ropess[:, o:o + 1024], ALU.mult)
                                v.tensor_tensor(qk[:, slot * S + o: slot * S + o + 1024],
                                                qk[:, slot * S + o: slot * S + o + 1024],
                                                ropecs[:, o:o + 1024], ALU.mult)
                                m = v.tensor_tensor(qk[:, slot * S + o: slot * S + o + 1024],
                                                    qk[:, slot * S + o: slot * S + o + 1024],
                                                    rotb[:, o:o + 1024], ALU.add)
                                M[f"rope_{gi}_{hl}_{wi}_{half}"] = inc(m, "dve")

                for hl in range(G):
                    @block.vector
                    def _(v, gi=gi, hl=hl):
                        for wi in range(2):
                            for half in range(2):
                                v.wait_ge(sem["dve"], M[f"rope_{gi}_{hl}_{wi}_{half}"])
                        m = v.tensor_tensor(foldb[:, :], qk[:, hl * S:(hl + 1) * S],
                                            qk[:, hl * S:(hl + 1) * S], ALU.mult)
                        M[f"sqq_{gi}_{hl}"] = inc(m, "dve")

                    @block.tensor
                    def _(t, gi=gi, hl=hl):
                        t.wait_ge(sem["dve"], M[f"sqq_{gi}_{hl}"])
                        wait_free(t, "PF")
                        for qt in range(16):
                            m = t.matmul(PF[:, qt:qt + 1], foldb[:, qt * 128:(qt + 1) * 128],
                                         onecb[:, :], start=True, stop=True)
                        M[f"nq_{gi}_{hl}"] = inc(m, "pe")

                    @block.vector
                    def _(v, gi=gi, hl=hl):
                        v.wait_ge(sem["pe"], M[f"nq_{gi}_{hl}"])
                        v.reduce_max(sml[:, 0:1], PF[:, 0:16], axis=AX.X)
                        m = v.tensor_tensor(foldb[:, :], qk[:, (2 + hl) * S:(3 + hl) * S],
                                            qk[:, (2 + hl) * S:(3 + hl) * S], ALU.mult)
                        M[f"sqk_{gi}_{hl}"] = inc(m, "dve")
                        m = v.tensor_copy(smb[:, 0:1], sml[:, 0:1])
                        M[f"mq1_{gi}_{hl}"] = inc(m, "dve")

                    @block.tensor
                    def _(t, gi=gi, hl=hl):
                        t.wait_ge(sem["dve"], M[f"mq1_{gi}_{hl}"])
                        wait_free(t, "PD")
                        t.transpose(PDb[0:1, 0:128], smb[:, 0:1], idb[:, :])
                        t.wait_ge(sem["dve"], M[f"sqk_{gi}_{hl}"])
                        for qt in range(16):
                            m = t.matmul(PF[:, qt:qt + 1], foldb[:, qt * 128:(qt + 1) * 128],
                                         onecb[:, :], start=True, stop=True)
                        M[f"nk_{gi}_{hl}"] = inc(m, "pe")

                    @block.vector
                    def _(v, gi=gi, hl=hl):
                        v.wait_ge(sem["pe"], M[f"nk_{gi}_{hl}"])
                        v.reduce_max(sml[:, 1:2], PF[:, 0:16], axis=AX.X)
                        v.reduce_max(smr[0:1, 0:1], PDb[0:1, 0:128], axis=AX.X)
                        v.tensor_copy(sml[:, 3:4], sml[:, 0:1])
                        m = v.tensor_copy(smb[:, 1:2], sml[:, 1:2])
                        M[f"mk1_{gi}_{hl}"] = inc(m, "dve")

                    @block.tensor
                    def _(t, gi=gi, hl=hl):
                        t.wait_ge(sem["dve"], M[f"mk1_{gi}_{hl}"])
                        m = t.transpose(PDb[0:1, 128:256], smb[:, 1:2], idb[:, :])
                        M[f"tk_{gi}_{hl}"] = inc(m, "pe")

                    @block.vector
                    def _(v, gi=gi, hl=hl):
                        v.wait_ge(sem["pe"], M[f"tk_{gi}_{hl}"])
                        v.reduce_max(smr[0:1, 1:2], PDb[0:1, 128:256], axis=AX.X)
                        v.tensor_copy(sml[:, 4:5], sml[:, 0:1])
                        v.tensor_copy(sml[:, 4:5], sml[:, 1:2])
                        v.tensor_tensor(smr[0:1, 2:3], smr[0:1, 0:1], smr[0:1, 1:2], ALU.add)
                        v.tensor_copy(sml[:, 4:5], sml[:, 0:1])
                        v.tensor_copy(sml[:, 4:5], sml[:, 1:2])
                        m = v.tensor_scalar(smr[0:1, 3:4], smr[0:1, 2:3], -SCALE / 2.0, None, ALU.mult)
                        M[f"nsr_{gi}_{hl}"] = inc(m, "dve")
                    free["PD"] = ("dve", M[f"nsr_{gi}_{hl}"])
                    free["PF"] = ("dve", M[f"mk1_{gi}_{hl}"])

                    @block.tensor
                    def _(t, gi=gi, hl=hl):
                        t.wait_ge(sem["dve"], M[f"nsr_{gi}_{hl}"])
                        wait_free(t, "PE")
                        m = t.matmul(PE_[:, 256 + hl:257 + hl], onerf[:, :], smr[0:1, 3:4],
                                     start=True, stop=True)
                        M[f"nsb_{gi}_{hl}"] = inc(m, "pe")

                    @block.vector
                    def _(v, gi=gi, hl=hl):
                        v.wait_ge(sem["pe"], M[f"nsb_{gi}_{hl}"])
                        m = v.tensor_copy(nsh[:, hl:hl + 1], PE_[:, 256 + hl:257 + hl])
                        M[f"nsh_{gi}_{hl}"] = inc(m, "dve")
                    free["PE"] = ("dve", M[f"nsh_{gi}_{hl}"])

                for hl in range(G):
                    h = h0 + hl
                    act_hbase = C["act"]
                    exp_mark = lambda qp, kg: act_hbase + qp * 4 + kg + 1
                    exp_jobs = []

                    def emit_sg(qp):
                        it = f"{gi}_{hl}_{qp}"

                        @block.tensor
                        def _(t, qp=qp, it=it):
                            if qp == 0:
                                t.wait_ge(sem["dve"], M[f"nsh_{gi}_{hl}"])
                                if hl == 0:
                                    t.wait_ge(sem["dve"], M[f"vev_{gi}_{NT-1}"])
                                else:
                                    t.wait_ge(sem["dve"], M[f"rcp_{gi}_{hl-1}"])
                                wait_free(t, "PA")
                                wait_free(t, "PB")
                            for kg in range(4):
                                ps = PA if kg % 2 == 0 else PB
                                ei = qp * 4 + kg
                                if ei >= 2:
                                    t.wait_ge(sem["act"], exp_mark((ei - 2) // 4, (ei - 2) % 4))
                                for k4 in range(4):
                                    kt = kg * 4 + k4
                                    m = t.matmul(ps[:, k4 * 256:(k4 + 1) * 256],
                                                 qk[:, (2 + hl) * S + kt * 128: (2 + hl) * S + kt * 128 + 128],
                                                 qk[:, hl * S + qp * 256: hl * S + qp * 256 + 256],
                                                 start=True, stop=True)
                                M[f"sg_{it}_{kg}"] = inc(m, "pe")

                        exp_jobs.append((qp, it))

                    def emit_pv(qp):
                        it = f"{gi}_{hl}_{qp}"

                        @block.tensor
                        def _(t, qp=qp, it=it):
                            buf = pt[qp % 2]
                            t.wait_ge(sem["act"], exp_mark(qp, 3))
                            if qp >= 2:
                                t.wait_ge(sem["dve"], M[f"otev_{gi}_{hl}_{qp-2}"])
                            else:
                                wait_free(t, "PC" if qp % 2 == 0 else "PD")
                            ps = PC if qp % 2 == 0 else PD
                            for kt in range(16):
                                m = t.matmul(ps[:, 0:256],
                                             vsb[:, kt * 256 + hl * 128: kt * 256 + hl * 128 + 128],
                                             buf[:, kt * 256:(kt + 1) * 256],
                                             start=(kt == 0), stop=(kt == 15))
                            M[f"pv_{it}"] = inc(m, "pe")
                            t.wait_ge(sem["dve"], M[f"f1_{it}"])
                            if hl == 0 and qp == 0:
                                wait_free(t, "PF")
                            for j in range(2):
                                m = t.matmul(PF[:, qp * 2 + j: qp * 2 + j + 1],
                                             foldb[:, j * 128:(j + 1) * 128], onecb[:, :],
                                             start=True, stop=True)
                            M[f"z_{it}"] = inc(m, "pe")

                        @block.vector
                        def _(v, qp=qp, it=it):
                            v.wait_ge(sem["pe"], M[f"pv_{it}"])
                            ps = PC if qp % 2 == 0 else PD
                            m = v.tensor_copy(oth[:, qp * 256:(qp + 1) * 256], ps[:, 0:256])
                            M[f"otev_{it}"] = inc(m, "dve")

                    def emit_fold(qp):
                        it = f"{gi}_{hl}_{qp}"

                        @block.vector
                        def _(v, qp=qp, it=it):
                            buf = pt[qp % 2]
                            v.wait_ge(sem["act"], exp_mark(qp, 3))
                            if qp > 0:
                                v.wait_ge(sem["pe"], M[f"z_{gi}_{hl}_{qp-1}"])
                            v.tensor_tensor(foldb[:, 0:2048], buf[:, 0:2048], buf[:, 2048:4096], ALU.add)
                            v.tensor_tensor(foldb[:, 0:1024], foldb[:, 0:1024], foldb[:, 1024:2048], ALU.add)
                            v.tensor_tensor(foldb[:, 0:512], foldb[:, 0:512], foldb[:, 512:1024], ALU.add)
                            m = v.tensor_tensor(foldb[:, 0:256], foldb[:, 0:256], foldb[:, 256:512], ALU.add)
                            M[f"f1_{it}"] = inc(m, "dve")

                    def emit_exps():
                        @block.scalar
                        def _(sc):
                            for qp, it in exp_jobs:
                                buf = pt[qp % 2]
                                for kg in range(4):
                                    ps = PA if kg % 2 == 0 else PB
                                    sc.wait_ge(sem["pe"], M[f"sg_{it}_{kg}"])
                                    if qp >= 2 and kg == 0:
                                        sc.wait_ge(sem["pe"], M[f"pv_{gi}_{hl}_{qp-2}"])
                                        sc.wait_ge(sem["dve"], M[f"f1_{gi}_{hl}_{qp-2}"])
                                    m = sc.activation(buf[:, kg * 1024:(kg + 1) * 1024], ps[:, 0:1024],
                                                      AF.Exp, bias=nsh[:, hl:hl + 1], scale=SCALE)
                                    assert inc(m, "act") == exp_mark(qp, kg)
                            exp_jobs.clear()

                    if LAG:
                        for qp in range(NQP + 1):
                            if qp < NQP:
                                emit_sg(qp)
                            if qp == NQP - 1:
                                emit_exps()
                            if qp >= 1:
                                emit_pv(qp - 1)
                            if qp < NQP:
                                emit_fold(qp)
                        emit_exps()
                    else:
                        for qp in range(NQP):
                            emit_sg(qp)
                            emit_exps()
                            emit_fold(qp)
                            emit_pv(qp)
                    free["PC"] = ("dve", M[f"otev_{gi}_{hl}_{NQP-2}"])
                    free["PD"] = ("dve", M[f"otev_{gi}_{hl}_{NQP-1}"])
                    free["PA"] = ("act", exp_mark(NQP - 1, 3))
                    free["PB"] = ("act", exp_mark(NQP - 1, 3))

                    @block.vector
                    def _(v, gi=gi, hl=hl):
                        v.wait_ge(sem["pe"], M[f"z_{gi}_{hl}_{NQP-1}"])
                        m = v.reciprocal(rcol[:, :], PF[:, 0:16])
                        M[f"rcp_{gi}_{hl}"] = inc(m, "dve")
                    free["PF"] = ("dve", M[f"rcp_{gi}_{hl}"])

                    @block.tensor
                    def _(t, gi=gi, hl=hl):
                        t.wait_ge(sem["dve"], M[f"rcp_{gi}_{hl}"])
                        wait_free(t, "PE")
                        m = t.transpose(PE_[0:16, 0:128], rcol[:, :], idf[:, :])
                        M[f"rct_{gi}_{hl}"] = inc(m, "pe")

                    @block.vector
                    def _(v, gi=gi, hl=hl):
                        v.wait_ge(sem["pe"], M[f"rct_{gi}_{hl}"])
                        m = v.tensor_copy(rz16[:, :], PE_[0:16, 0:128])
                        M[f"rz_{gi}_{hl}"] = inc(m, "dve")
                    free["PE"] = ("dve", M[f"rz_{gi}_{hl}"])

                    for qp in range(NQP):
                        @block.tensor
                        def _(t, gi=gi, hl=hl, qp=qp):
                            t.wait_ge(sem["dve"], M[f"rz_{gi}_{hl}"])
                            bank = "PE" if qp % 2 == 0 else "PF"
                            ps = PE_ if qp % 2 == 0 else PF
                            if qp >= 2:
                                t.wait_ge(sem["dve"], M[f"nrm_{gi}_{hl}_{qp-2}"])
                            else:
                                wait_free(t, bank)
                            for j in range(2):
                                jj = qp * 2 + j
                                m = t.matmul(ps[:, j * 128:(j + 1) * 128],
                                             selsb[:, jj * 128:(jj + 1) * 128],
                                             rz16[:, :], start=True, stop=True)
                            M[f"bmm_{gi}_{hl}_{qp}"] = inc(m, "pe")

                        @block.vector
                        def _(v, gi=gi, hl=hl, qp=qp, h=h):
                            v.wait_ge(sem["pe"], M[f"bmm_{gi}_{hl}_{qp}"])
                            if qp == 0 and h >= 2:
                                v.wait_ge(sem["dS"], M[f"ots_{h-2}"])
                            ps = PE_ if qp % 2 == 0 else PF
                            m = v.tensor_tensor(otn[hl][:, qp * 256:(qp + 1) * 256],
                                                oth[:, qp * 256:(qp + 1) * 256],
                                                ps[:, 0:256], ALU.mult)
                            M[f"nrm_{gi}_{hl}_{qp}"] = inc(m, "dve")
                    free["PE"] = ("dve", M[f"nrm_{gi}_{hl}_{NQP-2}"])
                    free["PF"] = ("dve", M[f"nrm_{gi}_{hl}_{NQP-1}"])

                    @block.sync
                    def _(s_, gi=gi, hl=hl, h=h):
                        for qp in range(NQP):
                            s_.wait_ge(sem["dve"], M[f"nrm_{gi}_{hl}_{qp}"])
                        m = s_.dma_start(out=ot_hbm[h, :, :], in_=otn[hl][:, :])
                        M[f"ots_{h}"] = inc(m, "dS", 16)

            if stage == 2:
                @block.gpsimd
                def _(g):
                    g.wait_ge(sem["dve"], M["rope_0_0_0_1"])
                    inc(g.dma_start(out=dbg["d_q0"], in_=qk[:, 0:S]), "st", 16)
                    g.wait_ge(sem["dve"], M["rope_0_0_1_1"])
                    inc(g.dma_start(out=dbg["d_k0"], in_=qk[:, 2 * S:3 * S]), "st", 16)
                    g.wait_ge(sem["dve"], M["vev_0_0"])
                    inc(g.dma_start(out=dbg["d_v0"], in_=vsb[:, 0:256]), "st", 16)
            if stage in (2, 3):
                @block.gpsimd
                def _(g):
                    g.wait_ge(sem["dS"], M["ots_0"])
                    inc(g.dma_start(out=dbg["d_ot0"], in_=ot_hbm[0, :, :]), "st", 16)
                    g.wait_ge(sem["dS"], M[f"ots_{2*ngroups-1}"])
                    inc(g.dma_start(out=dbg["d_ot15"], in_=ot_hbm[2*ngroups-1, :, :]), "st", 16)

        if stage >= 4:
            bctx.close()

            @block.gpsimd
            def _(g):
                g.wait_ge(sem["pe"], M[f"v_{NG-1}_{NT-1}"])
                m = g.dma_start(out=xT[:, :].rearrange("p (c f) -> p c f", c=NCH),
                                in_=wo_d.rearrange("(c p) f -> p c f", p=128))
                M["wo"] = inc(m, "dG", 16)

            cctx = ctx.enter_context(ExitStack())
            EC = cctx.enter_context
            otc = [EC(nc.sbuf_tensor(f"otc{i}", [128, S], BF16)) for i in range(2)]
            xtl = [EC(nc.sbuf_tensor(f"xtl{i}", [128, D], F32)) for i in range(2)]
            z1 = EC(nc.sbuf_tensor("z1", [128, D], F32))
            x1 = [EC(nc.sbuf_tensor(f"x1_{i}", [128, D], F32)) for i in range(2)]
            x1b = [EC(nc.sbuf_tensor(f"x1b{i}", [128, D], BF16)) for i in range(2)]
            x1t = [EC(nc.sbuf_tensor(f"x1t{i}", [128, D], BF16)) for i in range(2)]
            sqs = EC(nc.sbuf_tensor("sqs", [128, 512], F32))
            z2 = EC(nc.sbuf_tensor("z2", [128, D], F32))
            outb = [EC(nc.sbuf_tensor(f"outb{i}", [128, D], F32)) for i in range(2)]
            ts1 = EC(nc.sbuf_tensor("ts1", [128, 512], F32))
            stat = [EC(nc.sbuf_tensor(f"stat{i}", [128, 52], F32)) for i in range(2)]
            gsc = [EC(nc.sbuf_tensor(f"gsc{i}", [128, 8], F32)) for i in range(2)]
            prs = EC(nc.sbuf_tensor("prs", [128, 104], F32))
            pr2s = EC(nc.sbuf_tensor("pr2s", [128, 64], F32))
            prr = EC(nc.sbuf_tensor("prr", [128, 64], F32))
            mmbv = [EC(nc.sbuf_tensor(f"mmb{i}", [128, 64], BF16)) for i in range(2)]
            dbgg = EC(nc.sbuf_tensor("dbgg", [128, NT * 16], F32)) if DBG4 else None
            ebf = [EC(nc.sbuf_tensor(f"ebf{i}", [128, 8], BF16)) for i in range(2)]
            ets = [EC(nc.sbuf_tensor(f"ets{i}", [8, 128], BF16)) for i in range(2)]
            mts = [EC(nc.sbuf_tensor(f"mts{i}", [64, 128], BF16)) for i in range(2)]



            def newton_rsqrt(v, st_, vcol, ycol, tmp, j):
                A_, B_ = 0.430762, 0.476801
                sp = lambda: v.tensor_copy(st_[:, 33:34], st_[:, 8:9])
                v.tensor_scalar_min(st_[:, tmp[0]:tmp[0] + 1], st_[:, vcol:vcol + 1], 7.5)
                sp()
                sp()
                v.reciprocal(st_[:, tmp[1]:tmp[1] + 1], st_[:, tmp[0]:tmp[0] + 1])
                sp()
                sp()
                v.tensor_scalar(st_[:, ycol:ycol + 1], st_[:, tmp[1]:tmp[1] + 1],
                                -A_ / B_, B_, ALU.subtract, ALU.mult)
                for _ in range(5):
                    sp()
                    v.tensor_tensor(st_[:, tmp[2]:tmp[2] + 1], st_[:, ycol:ycol + 1],
                                    st_[:, ycol:ycol + 1], ALU.mult)
                    sp()
                    v.tensor_tensor(st_[:, tmp[3]:tmp[3] + 1], st_[:, tmp[2]:tmp[2] + 1],
                                    st_[:, vcol:vcol + 1], ALU.mult)
                    sp()
                    v.tensor_scalar(st_[:, tmp[2]:tmp[2] + 1], st_[:, tmp[3]:tmp[3] + 1],
                                    3.0, -0.5, ALU.subtract, ALU.mult)
                    sp()
                    m = v.tensor_tensor(st_[:, ycol:ycol + 1], st_[:, ycol:ycol + 1],
                                        st_[:, tmp[2]:tmp[2] + 1], ALU.mult)
                return m
            def front(j):
                bi = j % 2
                snap_ser()

                @block.sync
                def _(s_, j=j, bi=bi):
                    ser(s_)
                    if j == 0:
                        for h in range(NH):
                            s_.wait_ge(sem["dS"], M[f"ots_{h}"])
                    if j >= 2:
                        s_.wait_ge(sem["pe"], M[f"y_{j-2}_3"])
                        s_.wait_ge(sem["dve"], M[f"z1_{j-2}_3"])
                    m = s_.dma_start(out=xtl[bi][:, :], in_=x_tm[j * 128:(j + 1) * 128, :])
                    inc(m, "dS", 16)
                    m = s_.dma_start(out=otc[bi][:, :].rearrange("p (h t) -> p h t", h=NH),
                                     in_=ot_hbm[:, :, j * 128:(j + 1) * 128].rearrange("h p t -> p h t"))
                    M[f"cin_{j}"] = inc(m, "dS", 16)

                dve_fb = C["dve"]
                z1_mark = lambda p: dve_fb + p + 1

                @block.tensor
                def _(t, j=j, bi=bi, z1_mark=z1_mark):
                    ser(t)
                    t.wait_ge(sem["dS"], M[f"cin_{j}"])
                    if j == 0:
                        t.wait_ge(sem["dG"], M["wo"])
                        wait_free(t, "PA")
                        wait_free(t, "PB")
                    for p in range(4):
                        ps = PA if p % 2 == 0 else PB
                        if p >= 2:
                            t.wait_ge(sem["dve"], z1_mark(p - 2))
                        elif j > 0:
                            t.wait_ge(sem["dve"], M[f"z1_{j-1}_{p+2}"])
                        for c in range(NCH):
                            t.matmul(ps[:, 0:512], otc[bi][:, c * 128:(c + 1) * 128],
                                     xT[:, c * S + p * 512: c * S + p * 512 + 512],
                                     start=(c == 0), stop=False)
                        m = t.matmul(ps[:, 0:512], onerb[0:1, 0:128],
                                     bor[0:1, p * 512:(p + 1) * 512], start=False, stop=True)
                        M[f"y_{j}_{p}"] = inc(m, "pe")

                @block.vector
                def _(v, j=j, bi=bi, z1_mark=z1_mark):
                    ser(v)
                    st_ = stat[bi]
                    for p in range(4):
                        ps = PA if p % 2 == 0 else PB
                        v.wait_ge(sem["pe"], M[f"y_{j}_{p}"])
                        if p == 0 and j > 0:
                            v.wait_ge(sem["act"], M[f"sq1_{j-1}_3"])
                        m = v.scalar_tensor_tensor(z1[:, p * 512:(p + 1) * 512],
                                                   xtl[bi][:, p * 512:(p + 1) * 512], 1.0,
                                                   ps[:, 0:512], ALU.mult, ALU.add,
                                                   accum_out=st_[:, p:p + 1])
                        M[f"z1_{j}_{p}"] = assert_eq(inc(m, "dve"), z1_mark(p))

                @block.scalar
                def _(sc, j=j, bi=bi):
                    ser(sc)
                    st_ = stat[bi]
                    for p in range(4):
                        sc.wait_ge(sem["dve"], M[f"z1_{j}_{p}"])
                        m = sc.activation(sqs[:, :], z1[:, p * 512:(p + 1) * 512], AF.Square,
                                          accum_out=st_[:, 4 + p:5 + p])
                        M[f"sq1_{j}_{p}"] = inc(m, "act")

                @block.vector
                def _(v, j=j, bi=bi):
                    ser(v)
                    st_ = stat[bi]
                    v.wait_ge(sem["act"], M[f"sq1_{j}_3"])
                    v.reduce_sum(st_[:, 8:9], st_[:, 0:4], axis=AX.X)
                    v.reduce_sum(st_[:, 9:10], st_[:, 4:8], axis=AX.X)
                    v.tensor_scalar(st_[:, 10:11], st_[:, 8:9], 1.0 / D, None, ALU.mult)
                    v.tensor_scalar(st_[:, 11:12], st_[:, 9:10], 1.0 / D, None, ALU.mult)
                    v.tensor_tensor(st_[:, 12:13], st_[:, 10:11], st_[:, 10:11], ALU.mult)
                    v.tensor_copy(st_[:, 33:34], st_[:, 8:9])
                    v.tensor_copy(st_[:, 33:34], st_[:, 9:10])
                    v.tensor_tensor(st_[:, 13:14], st_[:, 11:12], st_[:, 12:13], ALU.subtract)
                    v.tensor_copy(st_[:, 33:34], st_[:, 8:9])
                    v.tensor_copy(st_[:, 33:34], st_[:, 9:10])
                    m = v.tensor_scalar(st_[:, 14:15], st_[:, 13:14], EPS, None, ALU.add)
                    M[f"st1_{j}"] = inc(m, "dve")

                @block.vector
                def _(v, j=j, bi=bi):
                    ser(v)
                    st_ = stat[bi]
                    m = newton_rsqrt(v, st_, 14, 16, (41, 42, 43, 15), j)
                    M[f"rs1_{j}"] = inc(m, "dve")
                    v.tensor_copy(st_[:, 33:34], st_[:, 8:9])
                    v.tensor_tensor(st_[:, 17:18], st_[:, 16:17], st_[:, 16:17], ALU.mult)
                    v.tensor_scalar(x1[bi][:, :], z1[:, :], st_[:, 10:11], st_[:, 16:17],
                                    ALU.subtract, ALU.mult)
                    v.tensor_scalar(st_[:, 19:20], st_[:, 17:18], -EPS * float(D), None, ALU.mult)
                    v.tensor_copy(st_[:, 33:34], st_[:, 14:15])
                    v.tensor_copy(st_[:, 33:34], st_[:, 17:18])
                    v.tensor_scalar(st_[:, 18:19], st_[:, 19:20], float(D), None, ALU.add)
                    v.tensor_copy(st_[:, 33:34], st_[:, 14:15])
                    v.tensor_copy(st_[:, 33:34], st_[:, 17:18])
                    m = v.tensor_scalar(st_[:, 44:52], m2bc[:, :], st_[:, 18:19], None, ALU.add)
                    if DBG4:
                        v.tensor_copy(dbgg[:, j * 16 + 12:j * 16 + 13], st_[:, 16:17])
                        v.tensor_copy(dbgg[:, j * 16 + 13:j * 16 + 14], st_[:, 17:18])
                        v.tensor_copy(dbgg[:, j * 16 + 14:j * 16 + 15], st_[:, 14:15])
                        v.tensor_copy(dbgg[:, j * 16 + 15:j * 16 + 16], st_[:, 18:19])
                    M[f"x1_{j}"] = inc(m, "dve")

                @block.scalar
                def _(sc, j=j, bi=bi):
                    ser(sc)
                    sc.wait_ge(sem["dve"], M[f"x1_{j}"])
                    if j >= 2:
                        sc.wait_ge(sem["pe"], M[f"x1tB_{j-2}"])
                    m = sc.activation(x1b[bi][:, :], x1[bi][:, :], AF.Copy)
                    M[f"x1b_{j}"] = inc(m, "act")


            def back(j):
                bi = j % 2
                snap_ser()

                @block.tensor
                def _(t, j=j, bi=bi):
                    ser(t)
                    t.wait_ge(sem["act"], M[f"x1b_{j}"])
                    if j >= 1:
                        t.wait_ge(sem["dve"], M[f"etv_{j-1}"])
                    else:
                        wait_free(t, "PD")
                    for c in range(8):
                        m = t.transpose(PDb[:, c * 128:(c + 1) * 128],
                                        x1b[bi][:, c * 128:(c + 1) * 128], idb[:, :])
                    M[f"x1tA_{j}"] = inc(m, "pe")

                @block.vector
                def _(v, j=j, bi=bi):
                    ser(v)
                    v.wait_ge(sem["pe"], M[f"x1tA_{j}"])
                    m = v.tensor_copy(x1t[bi][:, 0:1024], PDb[:, 0:1024])
                    M[f"x1eA_{j}"] = inc(m, "dve")

                @block.tensor
                def _(t, j=j, bi=bi):
                    ser(t)
                    t.wait_ge(sem["dve"], M[f"x1eA_{j}"])
                    for c in range(8, NCH):
                        m = t.transpose(PDb[:, (c - 8) * 128:(c - 7) * 128],
                                        x1b[bi][:, c * 128:(c + 1) * 128], idb[:, :])
                    M[f"x1tB_{j}"] = inc(m, "pe")

                @block.vector
                def _(v, j=j, bi=bi):
                    ser(v)
                    v.wait_ge(sem["pe"], M[f"x1tB_{j}"])
                    m = v.tensor_copy(x1t[bi][:, 1024:2048], PDb[:, 0:1024])
                    M[f"x1t_{j}"] = inc(m, "dve")

                @block.tensor
                def _(t, j=j, bi=bi):
                    ser(t)
                    t.wait_ge(sem["dve"], M[f"x1t_{j}"])
                    if j >= 1:
                        t.wait_ge(sem["dve"], M[f"gate_{j-1}"])
                    else:
                        wait_free(t, "PC")
                    for c in range(NCH):
                        t.matmul(PC[:, 0:136], x1t[bi][:, c * 128:(c + 1) * 128],
                                 ffnw[:, c * 136:(c + 1) * 136], start=(c == 0), stop=False)
                    m = t.matmul(PC[:, 0:136], onerb[0:1, 0:128], crow[0:1, :],
                                 start=False, stop=True)
                    M[f"ffn_{j}"] = inc(m, "pe")

                @block.vector
                def _(v, j=j):
                    ser(v)
                    v.wait_ge(sem["pe"], M[f"ffn_{j}"])
                    v.tensor_copy(prr[:, :], PC[:, 8:72])
                    v.tensor_copy(prs[:, 96:104], m2bc[:, :])
                    v.tensor_copy(prs[:, 96:104], cenbc[:, :])
                    m = v.scalar_tensor_tensor(pr2s[:, :], prr[:, :], 1.0, prr[:, :],
                                               ALU.mult, ALU.mult)
                    M[f"pr2_{j}"] = inc(m, "dve")

                @block.vector
                def _(v, j=j, bi=bi):
                    ser(v)
                    st_ = stat[bi]
                    gs = gsc[bi]
                    v.wait_ge(sem["dve"], M[f"pr2_{j}"])
                    v.scalar_tensor_tensor(prs[:, 8:16], PC[:, 0:8], -2.0, st_[:, 44:52],
                                           ALU.mult, ALU.add)
                    for n in range(8):
                        v.reduce_sum(prs[:, 40 + n:41 + n], pr2s[:, n * 8:(n + 1) * 8], axis=AX.X)
                    v.reciprocal(prs[:, 16:24], prs[:, 8:16])
                    v.tensor_copy(prs[:, 96:104], m2bc[:, :])
                    v.tensor_copy(prs[:, 96:104], cenbc[:, :])
                    v.tensor_tensor(prs[:, 24:32], prs[:, 40:48], prs[:, 16:24], ALU.mult)
                    v.tensor_copy(prs[:, 96:104], m2bc[:, :])
                    v.tensor_copy(prs[:, 96:104], cenbc[:, :])
                    v.tensor_tensor(prs[:, 32:40], prs[:, 24:32], cenbc[:, :], ALU.subtract)
                    v.tensor_copy(prs[:, 96:104], m2bc[:, :])
                    v.tensor_copy(prs[:, 96:104], cenbc[:, :])
                    m = v.tensor_tensor(prs[:, 48:56], prs[:, 32:40], slpbc[:, :], ALU.mult)
                    M[f"arg_{j}"] = inc(m, "dve")

                @block.scalar
                def _(sc, j=j, bi=bi):
                    ser(sc)
                    gs = gsc[bi]
                    sc.wait_ge(sem["dve"], M[f"arg_{j}"])
                    m = sc.activation(prs[:, 56:64], prs[:, 48:56], AF.Exp, scale=-1.0)
                    M[f"eneg_{j}"] = inc(m, "act")

                @block.vector
                def _(v, j=j, bi=bi):
                    ser(v)
                    st_ = stat[bi]
                    gs = gsc[bi]
                    v.wait_ge(sem["act"], M[f"eneg_{j}"])
                    v.tensor_scalar(prs[:, 64:72], prs[:, 56:64], 1.0, None, ALU.add)
                    v.tensor_copy(prs[:, 96:104], m2bc[:, :])
                    v.tensor_copy(prs[:, 96:104], cenbc[:, :])
                    v.reciprocal(gs[:, :], prs[:, 64:72])
                    v.tensor_copy(prs[:, 96:104], m2bc[:, :])
                    v.tensor_copy(prs[:, 96:104], cenbc[:, :])
                    v.reduce_max(st_[:, 30:31], gs[:, :], axis=AX.X)
                    v.tensor_copy(prs[:, 96:104], m2bc[:, :])
                    v.tensor_copy(prs[:, 96:104], cenbc[:, :])
                    v.tensor_scalar(st_[:, 31:32], st_[:, 30:31], 2.0, -1.0, ALU.subtract, ALU.mult)
                    v.tensor_scalar(prs[:, 72:80], gs[:, :], st_[:, 30:31], None, ALU.is_ge)
                    v.tensor_copy(prs[:, 96:104], m2bc[:, :])
                    v.tensor_copy(prs[:, 96:104], cenbc[:, :])
                    v.tensor_tensor(prs[:, 80:88], prs[:, 72:80], wties[:, :], ALU.mult)
                    v.tensor_copy(prs[:, 96:104], m2bc[:, :])
                    v.tensor_copy(prs[:, 96:104], cenbc[:, :])
                    v.reduce_max(st_[:, 32:33], prs[:, 80:88], axis=AX.X)
                    v.tensor_copy(st_[:, 33:34], st_[:, 30:31])
                    v.tensor_copy(st_[:, 33:34], st_[:, 31:32])
                    v.tensor_scalar(prs[:, 88:96], prs[:, 80:88], st_[:, 32:33], None, ALU.is_ge)
                    v.tensor_copy(prs[:, 96:104], m2bc[:, :])
                    v.tensor_copy(prs[:, 96:104], cenbc[:, :])
                    v.tensor_copy(ebf[bi][:, :], prs[:, 88:96])
                    if DBG4:
                        v.tensor_copy(dbgg[:, j * 16:j * 16 + 8], gs[:, :])
                        v.tensor_copy(dbgg[:, j * 16 + 8:j * 16 + 9], st_[:, 30:31])
                        v.tensor_copy(dbgg[:, j * 16 + 9:j * 16 + 10], st_[:, 18:19])
                        v.tensor_copy(dbgg[:, j * 16 + 10:j * 16 + 11], prs[:, 8:9])
                        v.tensor_copy(dbgg[:, j * 16 + 11:j * 16 + 12], prs[:, 88:89])
                    last = None
                    for n in range(8):
                        last = v.tensor_scalar(mmbv[bi][:, n * 8:(n + 1) * 8],
                                               PC[:, 72 + n * 8: 80 + n * 8],
                                               prs[:, 88 + n:89 + n], None, ALU.mult)
                    M[f"gate_{j}"] = inc(last, "dve")

                @block.tensor
                def _(t, j=j, bi=bi):
                    ser(t)
                    t.wait_ge(sem["dve"], M[f"gate_{j}"])
                    t.wait_ge(sem["dve"], M[f"x1t_{j}"])
                    t.transpose(PDb[0:8, 0:128], ebf[bi][:, :], idb[:, :])
                    m = t.transpose(PDb[0:64, 128:256], mmbv[bi][:, :], idb[:, :])
                    M[f"ett_{j}"] = inc(m, "pe")

                @block.vector
                def _(v, j=j, bi=bi):
                    ser(v)
                    v.wait_ge(sem["pe"], M[f"ett_{j}"])
                    v.tensor_copy(ets[bi][:, :], PDb[0:8, 0:128])
                    m = v.tensor_copy(mts[bi][:, :], PDb[0:64, 128:256])
                    M[f"etv_{j}"] = inc(m, "dve")

                dve_zb = C["dve"]
                z2_mark = lambda p: dve_zb + p + 1

                @block.tensor
                def _(t, j=j, bi=bi, z2_mark=z2_mark):
                    ser(t)
                    t.wait_ge(sem["dve"], M[f"etv_{j}"])
                    for p in range(4):
                        bank = "PE" if p % 2 == 0 else "PF"
                        ps = PE_ if p % 2 == 0 else PF
                        if p >= 2:
                            t.wait_ge(sem["dve"], z2_mark(p - 2))
                        elif j > 0:
                            t.wait_ge(sem["dve"], M[f"z2_{j-1}_{p+2}"])
                        else:
                            wait_free(t, bank)
                        t.matmul(ps[:, 0:512], mts[bi][:, :], dgn[:, p * 512:(p + 1) * 512],
                                 start=True, stop=False)
                        m = t.matmul(ps[:, 0:512], ets[bi][:, :], bwsb[:, p * 512:(p + 1) * 512],
                                     start=False, stop=True)
                        M[f"mbx_{j}_{p}"] = inc(m, "pe")

                @block.vector
                def _(v, j=j, bi=bi, z2_mark=z2_mark):
                    ser(v)
                    st_ = stat[bi]
                    for p in range(4):
                        ps = PE_ if p % 2 == 0 else PF
                        v.wait_ge(sem["pe"], M[f"mbx_{j}_{p}"])
                        if p == 0 and j > 0:
                            v.wait_ge(sem["act"], M[f"sq2_{j-1}_3"])
                        v.tensor_scalar(ts1[:, :], x1[bi][:, p * 512:(p + 1) * 512],
                                        st_[:, 31:32], None, ALU.mult)
                        m = v.scalar_tensor_tensor(z2[:, p * 512:(p + 1) * 512], ps[:, 0:512],
                                                   st_[:, 30:31], ts1[:, :], ALU.mult, ALU.add,
                                                   accum_out=st_[:, 20 + p:21 + p])
                        M[f"z2_{j}_{p}"] = inc(m, "dve")
                        assert M[f"z2_{j}_{p}"] == z2_mark(p), (M[f"z2_{j}_{p}"], z2_mark(p))

                @block.scalar
                def _(sc, j=j, bi=bi):
                    ser(sc)
                    st_ = stat[bi]
                    for p in range(4):
                        sc.wait_ge(sem["dve"], M[f"z2_{j}_{p}"])
                        m = sc.activation(sqs[:, :], z2[:, p * 512:(p + 1) * 512], AF.Square,
                                          accum_out=st_[:, 24 + p:25 + p])
                        M[f"sq2_{j}_{p}"] = inc(m, "act")

                @block.vector
                def _(v, j=j, bi=bi):
                    ser(v)
                    st_ = stat[bi]
                    v.wait_ge(sem["act"], M[f"sq2_{j}_3"])
                    v.reduce_sum(st_[:, 28:29], st_[:, 20:24], axis=AX.X)
                    v.reduce_sum(st_[:, 29:30], st_[:, 24:28], axis=AX.X)
                    v.tensor_scalar(st_[:, 34:35], st_[:, 28:29], 1.0 / D, None, ALU.mult)
                    v.tensor_scalar(st_[:, 35:36], st_[:, 29:30], 1.0 / D, None, ALU.mult)
                    v.tensor_tensor(st_[:, 36:37], st_[:, 34:35], st_[:, 34:35], ALU.mult)
                    v.tensor_copy(st_[:, 33:34], st_[:, 28:29])
                    v.tensor_copy(st_[:, 33:34], st_[:, 29:30])
                    v.tensor_tensor(st_[:, 37:38], st_[:, 35:36], st_[:, 36:37], ALU.subtract)
                    v.tensor_copy(st_[:, 33:34], st_[:, 28:29])
                    v.tensor_copy(st_[:, 33:34], st_[:, 29:30])
                    m = v.tensor_scalar(st_[:, 38:39], st_[:, 37:38], EPS, None, ALU.add)
                    M[f"st2_{j}"] = inc(m, "dve")

                @block.vector
                def _(v, j=j, bi=bi):
                    ser(v)
                    st_ = stat[bi]
                    m = newton_rsqrt(v, st_, 38, 40, (41, 42, 43, 39), j)
                    M[f"rs2_{j}"] = inc(m, "dve")
                    if j >= 2:
                        v.wait_ge(sem["st"], M[f"yst_{j-2}"])
                    v.tensor_copy(st_[:, 33:34], st_[:, 8:9])
                    v.tensor_copy(st_[:, 33:34], st_[:, 9:10])
                    m = v.tensor_scalar(outb[bi][:, :], z2[:, :], st_[:, 34:35], st_[:, 40:41],
                                        ALU.subtract, ALU.mult)
                    M[f"out_{j}"] = inc(m, "dve")

                @block.sync
                def _(s_, j=j, bi=bi):
                    ser(s_)
                    s_.wait_ge(sem["dve"], M[f"out_{j}"])
                    m = s_.dma_start(out=y_d[j * 128:(j + 1) * 128, :], in_=outb[bi][:, :])
                    M[f"yst_{j}"] = inc(m, "st", 16)

            for j in range(NT):
                front(j)
                if j >= 1:
                    back(j - 1)
            back(NT - 1)

            if DBG4:
                @block.gpsimd
                def _(g):
                    g.wait_ge(sem["dve"], M[f"x1t_{NT-1}"])
                    inc(g.dma_start(out=dbg["d_ota"], in_=ot_hbm[:, :, :]), "st", 16)
                    g.wait_ge(sem["dve"], M[f"gate_{NT-1}"])
                    inc(g.dma_start(out=dbg["d_gg"], in_=dbgg[:, :]), "st", 16)
                    g.wait_ge(sem["dve"], M["p1_done"])
                    inc(g.dma_start(out=dbg["d_slpbc"], in_=slpbc[:, :]), "st", 16)
                    g.wait_ge(sem["dve"], M[f"etv_{NT-1}"])
                    g.wait_ge(sem["dve"], M[f"out_{NT-1}"])
                    inc(g.dma_start(out=dbg["d_ets"], in_=ets[(NT-1) % 2][:, :]), "st", 16)
                    inc(g.dma_start(out=dbg["d_mts"], in_=mts[(NT-1) % 2][:, :]), "st", 16)
                    inc(g.dma_start(out=dbg["d_ebf"], in_=ebf[(NT-1) % 2][:, :]), "st", 16)
                    inc(g.dma_start(out=dbg["d_mmb"], in_=mmbv[(NT-1) % 2][:, :]), "st", 16)

            @block.sync
            def _(s_):
                s_.wait_ge(sem["st"], C["st"])
        else:
            @block.sync
            def _(s_):
                s_.wait_ge(sem["st"], C["st"])

    return nc, dbg_names


def assert_eq(a, b):
    assert a == b, (a, b)
    return a


def make_in_maps(inputs):
    x = np.asarray(inputs["L0_attn_hidden_state"])
    perm = _perm_deinterleave()
    Wq = np.ascontiguousarray(np.asarray(inputs["L0_attn_Wq"])[:, perm])
    Wk = np.ascontiguousarray(np.asarray(inputs["L0_attn_Wk"])[:, perm])
    Wv = np.ascontiguousarray(np.asarray(inputs["L0_attn_Wv"]))
    Wo = np.ascontiguousarray(np.asarray(inputs["L0_attn_Wo"]))
    bq = np.asarray(inputs["L0_attn_bq"])[perm][None, :].copy()
    bk = np.asarray(inputs["L0_attn_bk"])[perm][None, :].copy()
    bv = np.asarray(inputs["L0_attn_bv"])[None, :].copy()
    bo = np.asarray(inputs["L0_attn_bo"])[None, :].copy()
    selw = np.asarray(inputs["L0_ffn_select_weight"])
    md = np.asarray(inputs["L0_ffn_select_mean_diff"])
    cen = np.asarray(inputs["L0_ffn_imp_center"])[None, :].copy()
    slp = np.asarray(inputs["L0_ffn_imp_slope"])[None, :].copy()
    upd = np.asarray(inputs["L0_ffn_lora_update"])
    dgnw = np.asarray(inputs["L0_ffn_lora_degen"])
    bwv = np.ascontiguousarray(np.asarray(inputs["L0_ffn_bias_w"]))
    debw = np.asarray(inputs["L0_ffn_debias_w"])

    ffnw = np.ascontiguousarray(
        np.concatenate([md.T, selw.transpose(1, 0, 2).reshape(D, 64),
                        upd.transpose(1, 0, 2).reshape(D, 64)], axis=1)).astype(np.float32)
    dgnt = np.ascontiguousarray(dgnw.transpose(0, 2, 1).reshape(64, D))
    debt = np.ascontiguousarray(debw.T)
    ropec, ropes = _rope_tables()
    sel16 = np.zeros((16, D), dtype=np.float32)
    for jj in range(16):
        sel16[jj, jj * 128:(jj + 1) * 128] = 1.0

    in_maps = []
    for b in range(8):
        in_maps.append({
            "x_tm": np.ascontiguousarray(x[b]),
            "x_fm": np.ascontiguousarray(x[b].T),
            "wq": Wq, "wk": Wk, "wv": Wv, "wo": Wo,
            "bq": bq, "bk": bk, "bv": bv, "bo": bo,
            "ffnw": ffnw, "dgnt": dgnt, "bw": bwv, "debt": debt,
            "cen": cen, "slp": slp,
            "ropec": ropec, "ropes": ropes, "sel16": sel16,
        })
    return in_maps


def kernel(**inputs):
    in_maps = make_in_maps(inputs)
    nc, _ = build_nc(STAGE)
    res = run_bass_kernel_spmd(nc, in_maps, core_ids=list(range(8)))
    return np.stack([np.asarray(r["y"], dtype=np.float32) for r in res.results], axis=0)


# version 11
# speedup vs baseline: 1.3931x; 1.0797x over previous
import math
import numpy as np
import ml_dtypes
import concourse.bass as bass
import concourse.mybir as mybir
from concourse.bass_utils import run_bass_kernel_spmd
from contextlib import ExitStack

F32 = mybir.dt.float32
BF16 = mybir.dt.bfloat16
AF = mybir.ActivationFunctionType
ALU = mybir.AluOpType
AX = mybir.AxisListType
bfloat16 = ml_dtypes.bfloat16

D = 2048
S = 2048
NH = 16
HD = 128
NCH = 16
NT = 16
G = 2
NG = NH // G
WQ = 256
NQP = S // WQ
SCALE = 1.0 / math.sqrt(HD)
EPS = 1e-5

STAGE = 4
DBG4 = False
SERIAL = False
LAG = True
PIN_ACT = False
ACT_SET_ID = 0


def _perm_deinterleave():
    p = np.concatenate([np.arange(0, HD, 2), np.arange(1, HD, 2)])
    return np.concatenate([h * HD + p for h in range(NH)])


def _rope_tables():
    k = np.arange(HD)
    j = (k % 64).astype(np.float64)
    inv = np.power(10000.0, -2.0 * j / HD)
    ang = inv[:, None] * np.arange(S, dtype=np.float64)[None, :]
    c = np.cos(ang).astype(np.float32).astype(bfloat16)
    s = np.sin(ang).astype(np.float32).astype(bfloat16)
    return c, s


def build_nc(stage=STAGE):
    nc = bass.Bass("TRN2", target_bir_lowering=False, debug=False)
    dbg_names = []

    x_tm = nc.dram_tensor("x_tm", [S, D], F32, kind="ExternalInput").ap()
    x_fm = nc.dram_tensor("x_fm", [D, S], F32, kind="ExternalInput").ap()
    wq_d = nc.dram_tensor("wq", [D, D], F32, kind="ExternalInput").ap()
    wk_d = nc.dram_tensor("wk", [D, D], F32, kind="ExternalInput").ap()
    wv_d = nc.dram_tensor("wv", [D, D], F32, kind="ExternalInput").ap()
    wo_d = nc.dram_tensor("wo", [D, D], F32, kind="ExternalInput").ap()
    bq_d = nc.dram_tensor("bq", [1, D], F32, kind="ExternalInput").ap()
    bk_d = nc.dram_tensor("bk", [1, D], F32, kind="ExternalInput").ap()
    bv_d = nc.dram_tensor("bv", [1, D], F32, kind="ExternalInput").ap()
    bo_d = nc.dram_tensor("bo", [1, D], F32, kind="ExternalInput").ap()
    ffnw_d = nc.dram_tensor("ffnw", [D, 136], F32, kind="ExternalInput").ap()
    dgnt_d = nc.dram_tensor("dgnt", [64, D], F32, kind="ExternalInput").ap()
    bw_d = nc.dram_tensor("bw", [8, D], F32, kind="ExternalInput").ap()
    debt_d = nc.dram_tensor("debt", [D, 8], F32, kind="ExternalInput").ap()
    cen_d = nc.dram_tensor("cen", [1, 8], F32, kind="ExternalInput").ap()
    slp_d = nc.dram_tensor("slp", [1, 8], F32, kind="ExternalInput").ap()
    ropec_d = nc.dram_tensor("ropec", [HD, S], BF16, kind="ExternalInput").ap()
    ropes_d = nc.dram_tensor("ropes", [HD, S], BF16, kind="ExternalInput").ap()
    sel_d = nc.dram_tensor("sel16", [16, D], F32, kind="ExternalInput").ap()

    y_d = nc.dram_tensor("y", [S, D], F32, kind="ExternalOutput").ap()
    ot_hbm = nc.dram_tensor("ot_hbm", [NH, HD, S], BF16, kind="Internal").ap()
    x1b_hbm = nc.dram_tensor("x1b_hbm", [S, D], BF16, kind="Internal").ap()

    dbg = {}
    if stage == 4 and DBG4:
        dbg["d_x1"] = nc.dram_tensor("d_x1", [S, D], F32, kind="ExternalOutput").ap()
        dbg_names.append("d_x1")
        dbg["d_ota"] = nc.dram_tensor("d_ota", [NH, HD, S], F32, kind="ExternalOutput").ap()
        dbg_names.append("d_ota")
        dbg["d_gg"] = nc.dram_tensor("d_gg", [128, NT * 16], F32, kind="ExternalOutput").ap()
        dbg_names.append("d_gg")
        dbg["d_slpbc"] = nc.dram_tensor("d_slpbc", [128, 8], F32, kind="ExternalOutput").ap()
        dbg_names.append("d_slpbc")
        dbg["d_ets"] = nc.dram_tensor("d_ets", [8, 128], F32, kind="ExternalOutput").ap()
        dbg["d_mts"] = nc.dram_tensor("d_mts", [64, 128], F32, kind="ExternalOutput").ap()
        dbg["d_ebf"] = nc.dram_tensor("d_ebf", [128, 8], F32, kind="ExternalOutput").ap()
        dbg["d_mmb"] = nc.dram_tensor("d_mmb", [128, 64], F32, kind="ExternalOutput").ap()
        dbg_names.extend(["d_ets", "d_mts", "d_ebf", "d_mmb"])
    if stage < 4:
        for nm, shp in (("d_crow", [1, 136]), ("d_m2bc", [128, 8]), ("d_cen", [128, 8])):
            dbg[nm] = nc.dram_tensor(nm, shp, F32, kind="ExternalOutput").ap()
            dbg_names.append(nm)
    if stage in (2, 3):
        for nm, shp in (("d_q0", [HD, S]), ("d_k0", [HD, S]), ("d_v0", [128, 256]),
                        ("d_ot0", [HD, S]), ("d_ot15", [HD, S])):
            dbg[nm] = nc.dram_tensor(nm, shp, F32, kind="ExternalOutput").ap()
            dbg_names.append(nm)

    ident_bf = nc.inline_tensor(np.eye(128, dtype=bfloat16), name="ident_bf").ap()
    ident_f = nc.inline_tensor(np.eye(128, dtype=np.float32), name="ident_f").ap()
    rt = np.zeros((128, 128), dtype=np.float32)
    for i in range(64):
        rt[i, 64 + i] = 1.0
        rt[64 + i, i] = -1.0
    rt_bf = nc.inline_tensor(rt.astype(bfloat16), name="rt_bf").ap()
    ones_r_bf = nc.inline_tensor(np.ones((1, 512), dtype=bfloat16), name="ones_r_bf").ap()
    ones_c_bf = nc.inline_tensor(np.ones((128, 1), dtype=bfloat16), name="ones_c_bf").ap()
    ones_r_f = nc.inline_tensor(np.ones((1, 128), dtype=np.float32), name="ones_r_f").ap()
    ones8_f = nc.inline_tensor(np.ones((8, 1), dtype=np.float32), name="ones8_f").ap()
    eye8_c = nc.inline_tensor(np.eye(8, dtype=np.float32), name="eye8_c").ap()
    m64 = np.zeros((8, 64), dtype=np.float32)
    for n in range(8):
        m64[n, n * 8:(n + 1) * 8] = 1.0
    mask64_c = nc.inline_tensor(m64, name="mask64_c").ap()
    wt = np.tile((8.0 - np.arange(8, dtype=np.float32))[None, :], (128, 1))
    wtie_c = nc.inline_tensor(wt, name="wtie_c").ap()

    with ExitStack() as ctx:
        E = ctx.enter_context
        xT = E(nc.sbuf_tensor("xT", [128, NCH * S], BF16))
        ffnw = E(nc.sbuf_tensor("ffnw_sb", [128, NCH * 136], BF16))
        dgn = E(nc.sbuf_tensor("dgn", [64, D], BF16))
        bwsb = E(nc.sbuf_tensor("bwsb", [8, D], BF16))
        debt = E(nc.sbuf_tensor("debt_sb", [128, NCH * 8], BF16))
        bqr = E(nc.sbuf_tensor("bqr", [1, D], BF16))
        bkr = E(nc.sbuf_tensor("bkr", [1, D], BF16))
        bvr = E(nc.sbuf_tensor("bvr", [1, D], BF16))
        bor = E(nc.sbuf_tensor("bor", [1, D], BF16))
        cenr = E(nc.sbuf_tensor("cenr", [1, 8], F32))
        slpr = E(nc.sbuf_tensor("slpr", [1, 8], F32))
        selsb = E(nc.sbuf_tensor("selsb", [16, D], F32))
        ropecs = E(nc.sbuf_tensor("ropecs", [128, S], BF16))
        ropess = E(nc.sbuf_tensor("ropess", [128, S], BF16))
        idb = E(nc.sbuf_tensor("idb", [128, 128], BF16))
        idf = E(nc.sbuf_tensor("idf", [128, 128], F32))
        rtb = E(nc.sbuf_tensor("rtb", [128, 128], BF16))
        onerb = E(nc.sbuf_tensor("onerb", [1, 512], BF16))
        onecb = E(nc.sbuf_tensor("onecb", [128, 1], BF16))
        onerf = E(nc.sbuf_tensor("onerf", [1, 128], F32))
        one8f = E(nc.sbuf_tensor("one8f", [8, 1], F32))
        eye8s = E(nc.sbuf_tensor("eye8s", [8, 8], F32))
        m64s = E(nc.sbuf_tensor("m64s", [8, 64], F32))
        wties = E(nc.sbuf_tensor("wties", [128, 8], F32))
        crow = E(nc.sbuf_tensor("crow_sb", [1, 136], BF16))
        m2bc = E(nc.sbuf_tensor("m2bc", [128, 8], F32))
        cenbc = E(nc.sbuf_tensor("cenbc", [128, 8], F32))
        slpbc = E(nc.sbuf_tensor("slpbc", [128, 8], F32))
        scr8 = E(nc.sbuf_tensor("scr8", [8, 136], F32))

        PA = E(nc.psum_tensor("PA", [128, 1024], F32))
        PB = E(nc.psum_tensor("PB", [128, 1024], F32))
        PC = E(nc.psum_tensor("PC", [128, 512], F32))
        PD = E(nc.psum_tensor("PD", [128, 512], F32))
        PE_ = E(nc.psum_tensor("PEb", [128, 512], F32))
        PF = E(nc.psum_tensor("PFb", [128, 512], F32))
        PDb = PD[:, :].bitcast(BF16)
        PBb = PB[:, :].bitcast(BF16)

        sem = {}
        for name in ("pe", "act", "dve", "dG", "dS", "dA", "st"):
            sem[name] = E(nc.semaphore(name))
        C = {k: 0 for k in sem}
        block = E(nc.Block())
        M = {}

        def inc(ins, s_, n=1):
            ins.then_inc(sem[s_], n)
            C[s_] += n
            return C[s_]

        SNAP = {}

        def snap_ser():
            SNAP.update({k: C[k] for k in ("pe", "act", "dve")})

        def ser(eng):
            if SERIAL:
                for s2 in ("pe", "act", "dve"):
                    eng.wait_ge(sem[s2], SNAP.get(s2, 0))

        free = {"PA": None, "PB": None, "PC": None, "PD": None, "PE": None, "PF": None}

        def wait_free(eng, bank):
            if free[bank] is not None:
                s_, v_ = free[bank]
                eng.wait_ge(sem[s_], v_)

        @block.gpsimd
        def _(g):
            inc(g.dma_start(out=xT[:, :].rearrange("p (c t) -> p c t", c=NCH),
                            in_=x_fm.rearrange("(c p) t -> p c t", p=128)), "dG", 16)
            inc(g.dma_start(out=ffnw[:, :].rearrange("p (c f) -> p c f", c=NCH),
                            in_=ffnw_d.rearrange("(c p) f -> p c f", p=128)), "dG", 16)
            inc(g.dma_start(out=debt[:, :].rearrange("p (c f) -> p c f", c=NCH),
                            in_=debt_d.rearrange("(c p) f -> p c f", p=128)), "dG", 16)
            inc(g.dma_start(out=dgn[:, :], in_=dgnt_d), "dG", 16)
            inc(g.dma_start(out=bwsb[:, :], in_=bw_d), "dG", 16)
            inc(g.dma_start(out=bqr[:, :], in_=bq_d), "dG", 16)
            inc(g.dma_start(out=bkr[:, :], in_=bk_d), "dG", 16)
            inc(g.dma_start(out=bvr[:, :], in_=bv_d), "dG", 16)
            inc(g.dma_start(out=bor[:, :], in_=bo_d), "dG", 16)
            M["p0_gp"] = C["dG"]

        @block.sync
        def _(s_):
            for dst, src in ((idb, ident_bf), (idf, ident_f), (rtb, rt_bf),
                             (onerb, ones_r_bf), (onecb, ones_c_bf), (onerf, ones_r_f),
                             (one8f, ones8_f), (eye8s, eye8_c), (m64s, mask64_c),
                             (wties, wtie_c), (cenr, cen_d), (slpr, slp_d), (selsb, sel_d),
                             (ropecs, ropec_d), (ropess, ropes_d)):
                inc(s_.dma_start(out=dst[:, :], in_=src), "dS", 16)
            M["p0_sy"] = C["dS"]

        if PIN_ACT:
            @block.scalar
            def _(sc):
                sc.add_instruction(mybir.InstLoadActFuncSet(
                    name=nc.get_next_instruction_name(), act_func_set_id=ACT_SET_ID, ins=[], outs=[]))

        @block.tensor
        def _(t):
            t.wait_ge(sem["dG"], M["p0_gp"])
            t.wait_ge(sem["dS"], M["p0_sy"])
            for c in range(NCH):
                f0 = c * 136
                t.matmul(PC[0:8, 0:64], ffnw[:, f0:f0 + 8], ffnw[:, f0 + 8:f0 + 72],
                         start=(c == 0), stop=(c == NCH - 1))
            for c in range(NCH):
                f0 = c * 136
                t.matmul(PC[0:8, 64:128], debt[:, c * 8:(c + 1) * 8], ffnw[:, f0 + 72:f0 + 136],
                         start=(c == 0), stop=(c == NCH - 1))
            for c in range(NCH):
                f0 = c * 136
                m = t.matmul(PC[0:8, 128:136], ffnw[:, f0:f0 + 8], ffnw[:, f0:f0 + 8],
                             start=(c == 0), stop=(c == NCH - 1))
            M["p1_mm"] = inc(m, "pe")

        @block.vector
        def _(v):
            v.wait_ge(sem["pe"], M["p1_mm"])
            v.tensor_tensor(scr8[0:8, 0:64], PC[0:8, 0:64], m64s[:, :], ALU.mult)
            v.tensor_tensor(scr8[0:8, 64:128], PC[0:8, 64:128], m64s[:, :], ALU.mult)
            m = v.tensor_tensor(scr8[0:8, 128:136], PC[0:8, 128:136], eye8s[:, :], ALU.mult)
            M["p1_msk"] = inc(m, "dve")
        free["PC"] = ("dve", M["p1_msk"])

        @block.tensor
        def _(t):
            t.wait_ge(sem["dve"], M["p1_msk"])
            t.matmul(PD[0:1, 0:64], one8f[:, :], scr8[0:8, 0:64], start=True, stop=True)
            t.matmul(PD[0:1, 64:128], one8f[:, :], scr8[0:8, 64:128], start=True, stop=True)
            m = t.matmul(PD[0:1, 128:136], one8f[:, :], scr8[0:8, 128:136], start=True, stop=True)
            M["p1_row"] = inc(m, "pe")

        @block.vector
        def _(v):
            v.wait_ge(sem["pe"], M["p1_row"])
            v.memset(crow[:, :], 0.0)
            v.tensor_scalar(crow[0:1, 8:72], PD[0:1, 0:64], -1.0, None, ALU.mult)
            v.tensor_scalar(crow[0:1, 72:136], PD[0:1, 64:128], -1.0, None, ALU.mult)
            m = v.tensor_copy(scr8[0:1, 0:8], PD[0:1, 128:136])
            M["p1_m2"] = inc(m, "dve")
        free["PD"] = ("dve", M["p1_m2"])

        @block.tensor
        def _(t):
            t.wait_ge(sem["dve"], M["p1_m2"])
            t.matmul(PE_[:, 0:8], onerf[:, :], scr8[0:1, 0:8], start=True, stop=True)
            t.matmul(PE_[:, 8:16], onerf[:, :], cenr[:, :], start=True, stop=True)
            m = t.matmul(PE_[:, 16:24], onerf[:, :], slpr[:, :], start=True, stop=True)
            M["p1_bc"] = inc(m, "pe")

        @block.vector
        def _(v):
            v.wait_ge(sem["pe"], M["p1_bc"])
            v.tensor_copy(m2bc[:, :], PE_[:, 0:8])
            v.tensor_copy(cenbc[:, :], PE_[:, 8:16])
            m = v.tensor_copy(slpbc[:, :], PE_[:, 16:24])
            M["p1_done"] = inc(m, "dve")
        free["PE"] = ("dve", M["p1_done"])

        if stage < 4:
            @block.gpsimd
            def _(g):
                g.wait_ge(sem["dve"], M["p1_done"])
                inc(g.dma_start(out=dbg["d_crow"], in_=crow[:, :]), "st", 16)

            @block.sync
            def _(s_):
                s_.wait_ge(sem["dve"], M["p1_done"])
                inc(s_.dma_start(out=dbg["d_m2bc"], in_=m2bc[:, :]), "st", 16)
                inc(s_.dma_start(out=dbg["d_cen"], in_=cenbc[:, :]), "st", 16)

        if stage >= 2:
            bctx = ctx.enter_context(ExitStack())
            EB = bctx.enter_context
            wsl = {w: EB(nc.sbuf_tensor(f"wsl_{w}", [128, NCH * 256], BF16)) for w in "qkv"}
            qk = EB(nc.sbuf_tensor("qk", [128, 4 * S], BF16))
            vsb = EB(nc.sbuf_tensor("vsb", [128, NT * 256], BF16))
            rotb = EB(nc.sbuf_tensor("rotb", [128, S], BF16))
            foldb = EB(nc.sbuf_tensor("foldb", [128, S], BF16))
            pt = [EB(nc.sbuf_tensor(f"pt{i}", [128, 4096], BF16)) for i in range(2)]
            oth = EB(nc.sbuf_tensor("oth", [128, S], F32))
            otn = [EB(nc.sbuf_tensor(f"otn{i}", [128, S], BF16)) for i in range(2)]
            rcol = EB(nc.sbuf_tensor("rcol", [128, 16], F32))
            rz16 = EB(nc.sbuf_tensor("rz16", [16, 128], F32))
            nsh = EB(nc.sbuf_tensor("nsh", [128, 2], F32))
            sml = EB(nc.sbuf_tensor("sml", [128, 8], F32))
            smr = EB(nc.sbuf_tensor("smr", [1, 16], F32))
            smb = EB(nc.sbuf_tensor("smb", [128, 2], BF16))

            ngroups = NG if stage >= 3 else 1

            @block.gpsimd
            def _(g):
                for w, wd in (("q", wq_d), ("k", wk_d), ("v", wv_d)):
                    m = g.dma_start(out=wsl[w][:, :].rearrange("p (c f) -> p c f", c=NCH),
                                    in_=wd[:, 0:G * 128].rearrange("(c p) f -> p c f", p=128))
                    inc(m, "dG", 16)
                M["wsl_0"] = C["dG"]

            for gi in range(ngroups):
                h0 = G * gi

                act_base = C["act"]
                ev_mark = lambda pp: act_base + pp + 1

                @block.tensor
                def _(t, gi=gi, h0=h0, ev_mark=ev_mark):
                    t.wait_ge(sem["dG"], M[f"wsl_{gi}"])
                    if gi == 0:
                        t.wait_ge(sem["dG"], M["p0_gp"])
                        t.wait_ge(sem["dS"], M["p0_sy"])
                        t.wait_ge(sem["dve"], M["p1_done"])
                    pp = 0
                    for hl in range(G):
                        for wname, brow in (("q", bqr), ("k", bkr)):
                            for p in range(4):
                                ps = PA if pp % 2 == 0 else PB
                                if pp >= 2:
                                    t.wait_ge(sem["act"], ev_mark(pp - 2))
                                else:
                                    wait_free(t, "PA" if pp % 2 == 0 else "PB")
                                for c in range(NCH):
                                    t.matmul(ps[:, 0:512],
                                             wsl[wname][:, c * 256 + hl * 128: c * 256 + hl * 128 + 128],
                                             xT[:, c * S + p * 512: c * S + p * 512 + 512],
                                             start=(c == 0), stop=False)
                                m = t.matmul(ps[:, 0:512],
                                             brow[0:1, (h0 + hl) * 128:(h0 + hl) * 128 + 128],
                                             onerb[0:1, 0:512], start=False, stop=True)
                                M[f"qk_{gi}_{pp}"] = inc(m, "pe")
                                pp += 1

                @block.scalar
                def _(sc, gi=gi):
                    pp = 0
                    for hl in range(G):
                        for wname in ("q", "k"):
                            slot = (0 if wname == "q" else 2) + hl
                            for p in range(4):
                                ps = PA if pp % 2 == 0 else PB
                                sc.wait_ge(sem["pe"], M[f"qk_{gi}_{pp}"])
                                m = sc.activation(qk[:, slot * S + p * 512: slot * S + p * 512 + 512],
                                                  ps[:, 0:512], AF.Copy)
                                assert inc(m, "act") == ev_mark(pp)
                                pp += 1
                free["PA"] = ("act", ev_mark(14))
                free["PB"] = ("act", ev_mark(15))

                dve_vbase = C["dve"]
                vev_mark = lambda tt: dve_vbase + tt + 1

                @block.tensor
                def _(t, gi=gi, h0=h0, vev_mark=vev_mark):
                    wait_free(t, "PC")
                    wait_free(t, "PD")
                    for tt in range(NT):
                        ps = PC if tt % 2 == 0 else PD
                        if tt >= 2:
                            t.wait_ge(sem["dve"], vev_mark(tt - 2))
                        for c in range(NCH):
                            t.matmul(ps[:, 0:256],
                                     xT[:, c * S + tt * 128: c * S + tt * 128 + 128],
                                     wsl["v"][:, c * 256: c * 256 + 256],
                                     start=(c == 0), stop=False)
                        m = t.matmul(ps[:, 0:256], onerb[0:1, 0:128],
                                     bvr[0:1, h0 * 128: h0 * 128 + 256], start=False, stop=True)
                        M[f"v_{gi}_{tt}"] = inc(m, "pe")

                @block.vector
                def _(v, gi=gi, vev_mark=vev_mark):
                    for tt in range(NT):
                        ps = PC if tt % 2 == 0 else PD
                        v.wait_ge(sem["pe"], M[f"v_{gi}_{tt}"])
                        m = v.tensor_copy(vsb[:, tt * 256:(tt + 1) * 256], ps[:, 0:256])
                        M[f"vev_{gi}_{tt}"] = inc(m, "dve")
                        assert M[f"vev_{gi}_{tt}"] == vev_mark(tt)
                free["PC"] = ("dve", vev_mark(14))
                free["PD"] = ("dve", vev_mark(15))

                if gi + 1 < ngroups:
                    @block.gpsimd
                    def _(g, gi=gi):
                        g.wait_ge(sem["pe"], M[f"v_{gi}_{NT-1}"])
                        for w, wd in (("q", wq_d), ("k", wk_d), ("v", wv_d)):
                            c0 = (gi + 1) * G * 128
                            m = g.dma_start(out=wsl[w][:, :].rearrange("p (c f) -> p c f", c=NCH),
                                            in_=wd[:, c0:c0 + G * 128].rearrange("(c p) f -> p c f", p=128))
                            inc(m, "dG", 16)
                        M[f"wsl_{gi+1}"] = C["dG"]

                for hl in range(G):
                    for wi, wname in enumerate(("q", "k")):
                        slot = (0 if wname == "q" else 2) + hl
                        for half in range(2):
                            bank = "PA" if half == 0 else "PB"
                            ps = PA if half == 0 else PB

                            @block.tensor
                            def _(t, gi=gi, hl=hl, wi=wi, slot=slot, half=half, ps=ps, bank=bank):
                                wait_free(t, bank)
                                if wi == 0 and half == 0:
                                    pass
                                for j2 in range(2):
                                    o = half * 1024 + j2 * 512
                                    m = t.matmul(ps[:, j2 * 512:(j2 + 1) * 512], rtb[:, :],
                                                 qk[:, slot * S + o: slot * S + o + 512],
                                                 start=True, stop=True)
                                M[f"rot_{gi}_{hl}_{wi}_{half}"] = inc(m, "pe")

                            @block.scalar
                            def _(sc, gi=gi, hl=hl, wi=wi, half=half, ps=ps):
                                sc.wait_ge(sem["pe"], M[f"rot_{gi}_{hl}_{wi}_{half}"])
                                m = sc.activation(rotb[:, half * 1024:(half + 1) * 1024],
                                                  ps[:, 0:1024], AF.Copy)
                                M[f"roev_{gi}_{hl}_{wi}_{half}"] = inc(m, "act")
                            free[bank] = ("act", M[f"roev_{gi}_{hl}_{wi}_{half}"])

                            @block.vector
                            def _(v, gi=gi, hl=hl, wi=wi, slot=slot, half=half):
                                o = half * 1024
                                v.wait_ge(sem["act"], M[f"roev_{gi}_{hl}_{wi}_{half}"])
                                v.tensor_tensor(rotb[:, o:o + 1024], rotb[:, o:o + 1024],
                                                ropess[:, o:o + 1024], ALU.mult)
                                v.tensor_tensor(qk[:, slot * S + o: slot * S + o + 1024],
                                                qk[:, slot * S + o: slot * S + o + 1024],
                                                ropecs[:, o:o + 1024], ALU.mult)
                                m = v.tensor_tensor(qk[:, slot * S + o: slot * S + o + 1024],
                                                    qk[:, slot * S + o: slot * S + o + 1024],
                                                    rotb[:, o:o + 1024], ALU.add)
                                M[f"rope_{gi}_{hl}_{wi}_{half}"] = inc(m, "dve")

                for hl in range(G):
                    @block.vector
                    def _(v, gi=gi, hl=hl):
                        for wi in range(2):
                            for half in range(2):
                                v.wait_ge(sem["dve"], M[f"rope_{gi}_{hl}_{wi}_{half}"])
                        m = v.tensor_tensor(foldb[:, :], qk[:, hl * S:(hl + 1) * S],
                                            qk[:, hl * S:(hl + 1) * S], ALU.mult)
                        M[f"sqq_{gi}_{hl}"] = inc(m, "dve")

                    @block.tensor
                    def _(t, gi=gi, hl=hl):
                        t.wait_ge(sem["dve"], M[f"sqq_{gi}_{hl}"])
                        wait_free(t, "PF")
                        for qt in range(16):
                            m = t.matmul(PF[:, qt:qt + 1], foldb[:, qt * 128:(qt + 1) * 128],
                                         onecb[:, :], start=True, stop=True)
                        M[f"nq_{gi}_{hl}"] = inc(m, "pe")

                    @block.vector
                    def _(v, gi=gi, hl=hl):
                        v.wait_ge(sem["pe"], M[f"nq_{gi}_{hl}"])
                        v.reduce_max(sml[:, 0:1], PF[:, 0:16], axis=AX.X)
                        m = v.tensor_tensor(foldb[:, :], qk[:, (2 + hl) * S:(3 + hl) * S],
                                            qk[:, (2 + hl) * S:(3 + hl) * S], ALU.mult)
                        M[f"sqk_{gi}_{hl}"] = inc(m, "dve")
                        m = v.tensor_copy(smb[:, 0:1], sml[:, 0:1])
                        M[f"mq1_{gi}_{hl}"] = inc(m, "dve")

                    @block.tensor
                    def _(t, gi=gi, hl=hl):
                        t.wait_ge(sem["dve"], M[f"mq1_{gi}_{hl}"])
                        wait_free(t, "PD")
                        t.transpose(PDb[0:1, 0:128], smb[:, 0:1], idb[:, :])
                        t.wait_ge(sem["dve"], M[f"sqk_{gi}_{hl}"])
                        for qt in range(16):
                            m = t.matmul(PF[:, qt:qt + 1], foldb[:, qt * 128:(qt + 1) * 128],
                                         onecb[:, :], start=True, stop=True)
                        M[f"nk_{gi}_{hl}"] = inc(m, "pe")

                    @block.vector
                    def _(v, gi=gi, hl=hl):
                        v.wait_ge(sem["pe"], M[f"nk_{gi}_{hl}"])
                        v.reduce_max(sml[:, 1:2], PF[:, 0:16], axis=AX.X)
                        v.reduce_max(smr[0:1, 0:1], PDb[0:1, 0:128], axis=AX.X)
                        v.tensor_copy(sml[:, 3:4], sml[:, 0:1])
                        m = v.tensor_copy(smb[:, 1:2], sml[:, 1:2])
                        M[f"mk1_{gi}_{hl}"] = inc(m, "dve")

                    @block.tensor
                    def _(t, gi=gi, hl=hl):
                        t.wait_ge(sem["dve"], M[f"mk1_{gi}_{hl}"])
                        m = t.transpose(PDb[0:1, 128:256], smb[:, 1:2], idb[:, :])
                        M[f"tk_{gi}_{hl}"] = inc(m, "pe")

                    @block.vector
                    def _(v, gi=gi, hl=hl):
                        v.wait_ge(sem["pe"], M[f"tk_{gi}_{hl}"])
                        v.reduce_max(smr[0:1, 1:2], PDb[0:1, 128:256], axis=AX.X)
                        v.tensor_copy(sml[:, 4:5], sml[:, 0:1])
                        v.tensor_copy(sml[:, 4:5], sml[:, 1:2])
                        v.tensor_tensor(smr[0:1, 2:3], smr[0:1, 0:1], smr[0:1, 1:2], ALU.add)
                        v.tensor_copy(sml[:, 4:5], sml[:, 0:1])
                        v.tensor_copy(sml[:, 4:5], sml[:, 1:2])
                        m = v.tensor_scalar(smr[0:1, 3:4], smr[0:1, 2:3], -SCALE / 2.0, None, ALU.mult)
                        M[f"nsr_{gi}_{hl}"] = inc(m, "dve")
                    free["PD"] = ("dve", M[f"nsr_{gi}_{hl}"])
                    free["PF"] = ("dve", M[f"mk1_{gi}_{hl}"])

                    @block.tensor
                    def _(t, gi=gi, hl=hl):
                        t.wait_ge(sem["dve"], M[f"nsr_{gi}_{hl}"])
                        wait_free(t, "PE")
                        m = t.matmul(PE_[:, 256 + hl:257 + hl], onerf[:, :], smr[0:1, 3:4],
                                     start=True, stop=True)
                        M[f"nsb_{gi}_{hl}"] = inc(m, "pe")

                    @block.vector
                    def _(v, gi=gi, hl=hl):
                        v.wait_ge(sem["pe"], M[f"nsb_{gi}_{hl}"])
                        m = v.tensor_copy(nsh[:, hl:hl + 1], PE_[:, 256 + hl:257 + hl])
                        M[f"nsh_{gi}_{hl}"] = inc(m, "dve")
                    free["PE"] = ("dve", M[f"nsh_{gi}_{hl}"])

                for hl in range(G):
                    h = h0 + hl
                    act_hbase = C["act"]
                    exp_mark = lambda qp, kg: act_hbase + qp * 4 + kg + 1
                    exp_jobs = []

                    def emit_sg(qp):
                        it = f"{gi}_{hl}_{qp}"

                        @block.tensor
                        def _(t, qp=qp, it=it):
                            if qp == 0:
                                t.wait_ge(sem["dve"], M[f"nsh_{gi}_{hl}"])
                                if hl == 0:
                                    t.wait_ge(sem["dve"], M[f"vev_{gi}_{NT-1}"])
                                else:
                                    t.wait_ge(sem["dve"], M[f"rcp_{gi}_{hl-1}"])
                                wait_free(t, "PA")
                                wait_free(t, "PB")
                            for kg in range(4):
                                ps = PA if kg % 2 == 0 else PB
                                ei = qp * 4 + kg
                                if ei >= 2:
                                    t.wait_ge(sem["act"], exp_mark((ei - 2) // 4, (ei - 2) % 4))
                                for k4 in range(4):
                                    kt = kg * 4 + k4
                                    m = t.matmul(ps[:, k4 * 256:(k4 + 1) * 256],
                                                 qk[:, (2 + hl) * S + kt * 128: (2 + hl) * S + kt * 128 + 128],
                                                 qk[:, hl * S + qp * 256: hl * S + qp * 256 + 256],
                                                 start=True, stop=True)
                                M[f"sg_{it}_{kg}"] = inc(m, "pe")

                        exp_jobs.append((qp, it))

                    def emit_pv(qp):
                        it = f"{gi}_{hl}_{qp}"

                        @block.tensor
                        def _(t, qp=qp, it=it):
                            buf = pt[qp % 2]
                            t.wait_ge(sem["act"], exp_mark(qp, 3))
                            if qp >= 2:
                                t.wait_ge(sem["dve"], M[f"otev_{gi}_{hl}_{qp-2}"])
                            else:
                                wait_free(t, "PC" if qp % 2 == 0 else "PD")
                            ps = PC if qp % 2 == 0 else PD
                            for kt in range(16):
                                m = t.matmul(ps[:, 0:256],
                                             vsb[:, kt * 256 + hl * 128: kt * 256 + hl * 128 + 128],
                                             buf[:, kt * 256:(kt + 1) * 256],
                                             start=(kt == 0), stop=(kt == 15))
                            M[f"pv_{it}"] = inc(m, "pe")
                            t.wait_ge(sem["dve"], M[f"f1_{it}"])
                            if hl == 0 and qp == 0:
                                wait_free(t, "PF")
                            for j in range(2):
                                m = t.matmul(PF[:, qp * 2 + j: qp * 2 + j + 1],
                                             foldb[:, j * 128:(j + 1) * 128], onecb[:, :],
                                             start=True, stop=True)
                            M[f"z_{it}"] = inc(m, "pe")

                        @block.vector
                        def _(v, qp=qp, it=it):
                            v.wait_ge(sem["pe"], M[f"pv_{it}"])
                            ps = PC if qp % 2 == 0 else PD
                            m = v.tensor_copy(oth[:, qp * 256:(qp + 1) * 256], ps[:, 0:256])
                            M[f"otev_{it}"] = inc(m, "dve")

                    def emit_fold(qp):
                        it = f"{gi}_{hl}_{qp}"

                        @block.vector
                        def _(v, qp=qp, it=it):
                            buf = pt[qp % 2]
                            v.wait_ge(sem["act"], exp_mark(qp, 3))
                            if qp > 0:
                                v.wait_ge(sem["pe"], M[f"z_{gi}_{hl}_{qp-1}"])
                            v.tensor_tensor(foldb[:, 0:2048], buf[:, 0:2048], buf[:, 2048:4096], ALU.add)
                            v.tensor_tensor(foldb[:, 0:1024], foldb[:, 0:1024], foldb[:, 1024:2048], ALU.add)
                            v.tensor_tensor(foldb[:, 0:512], foldb[:, 0:512], foldb[:, 512:1024], ALU.add)
                            m = v.tensor_tensor(foldb[:, 0:256], foldb[:, 0:256], foldb[:, 256:512], ALU.add)
                            M[f"f1_{it}"] = inc(m, "dve")

                    def emit_exps():
                        @block.scalar
                        def _(sc):
                            for qp, it in exp_jobs:
                                buf = pt[qp % 2]
                                for kg in range(4):
                                    ps = PA if kg % 2 == 0 else PB
                                    sc.wait_ge(sem["pe"], M[f"sg_{it}_{kg}"])
                                    if qp >= 2 and kg == 0:
                                        sc.wait_ge(sem["pe"], M[f"pv_{gi}_{hl}_{qp-2}"])
                                        sc.wait_ge(sem["dve"], M[f"f1_{gi}_{hl}_{qp-2}"])
                                    m = sc.activation(buf[:, kg * 1024:(kg + 1) * 1024], ps[:, 0:1024],
                                                      AF.Exp, bias=nsh[:, hl:hl + 1], scale=SCALE)
                                    assert inc(m, "act") == exp_mark(qp, kg)
                            exp_jobs.clear()

                    if LAG:
                        for qp in range(NQP + 1):
                            if qp < NQP:
                                emit_sg(qp)
                            if qp == NQP - 1:
                                emit_exps()
                            if qp >= 1:
                                emit_pv(qp - 1)
                            if qp < NQP:
                                emit_fold(qp)
                        emit_exps()
                    else:
                        for qp in range(NQP):
                            emit_sg(qp)
                            emit_exps()
                            emit_fold(qp)
                            emit_pv(qp)
                    free["PC"] = ("dve", M[f"otev_{gi}_{hl}_{NQP-2}"])
                    free["PD"] = ("dve", M[f"otev_{gi}_{hl}_{NQP-1}"])
                    free["PA"] = ("act", exp_mark(NQP - 1, 3))
                    free["PB"] = ("act", exp_mark(NQP - 1, 3))

                    @block.vector
                    def _(v, gi=gi, hl=hl):
                        v.wait_ge(sem["pe"], M[f"z_{gi}_{hl}_{NQP-1}"])
                        m = v.reciprocal(rcol[:, :], PF[:, 0:16])
                        M[f"rcp_{gi}_{hl}"] = inc(m, "dve")
                    free["PF"] = ("dve", M[f"rcp_{gi}_{hl}"])

                    @block.tensor
                    def _(t, gi=gi, hl=hl):
                        t.wait_ge(sem["dve"], M[f"rcp_{gi}_{hl}"])
                        wait_free(t, "PE")
                        m = t.transpose(PE_[0:16, 0:128], rcol[:, :], idf[:, :])
                        M[f"rct_{gi}_{hl}"] = inc(m, "pe")

                    @block.vector
                    def _(v, gi=gi, hl=hl):
                        v.wait_ge(sem["pe"], M[f"rct_{gi}_{hl}"])
                        m = v.tensor_copy(rz16[:, :], PE_[0:16, 0:128])
                        M[f"rz_{gi}_{hl}"] = inc(m, "dve")
                    free["PE"] = ("dve", M[f"rz_{gi}_{hl}"])

                    for qp in range(NQP):
                        @block.tensor
                        def _(t, gi=gi, hl=hl, qp=qp):
                            t.wait_ge(sem["dve"], M[f"rz_{gi}_{hl}"])
                            bank = "PE" if qp % 2 == 0 else "PF"
                            ps = PE_ if qp % 2 == 0 else PF
                            if qp >= 2:
                                t.wait_ge(sem["dve"], M[f"nrm_{gi}_{hl}_{qp-2}"])
                            else:
                                wait_free(t, bank)
                            for j in range(2):
                                jj = qp * 2 + j
                                m = t.matmul(ps[:, j * 128:(j + 1) * 128],
                                             selsb[:, jj * 128:(jj + 1) * 128],
                                             rz16[:, :], start=True, stop=True)
                            M[f"bmm_{gi}_{hl}_{qp}"] = inc(m, "pe")

                        @block.vector
                        def _(v, gi=gi, hl=hl, qp=qp, h=h):
                            v.wait_ge(sem["pe"], M[f"bmm_{gi}_{hl}_{qp}"])
                            if qp == 0 and h >= 2:
                                v.wait_ge(sem["dS"], M[f"ots_{h-2}"])
                            ps = PE_ if qp % 2 == 0 else PF
                            m = v.tensor_tensor(otn[hl][:, qp * 256:(qp + 1) * 256],
                                                oth[:, qp * 256:(qp + 1) * 256],
                                                ps[:, 0:256], ALU.mult)
                            M[f"nrm_{gi}_{hl}_{qp}"] = inc(m, "dve")
                    free["PE"] = ("dve", M[f"nrm_{gi}_{hl}_{NQP-2}"])
                    free["PF"] = ("dve", M[f"nrm_{gi}_{hl}_{NQP-1}"])

                    @block.sync
                    def _(s_, gi=gi, hl=hl, h=h):
                        for qp in range(NQP):
                            s_.wait_ge(sem["dve"], M[f"nrm_{gi}_{hl}_{qp}"])
                        m = s_.dma_start(out=ot_hbm[h, :, :], in_=otn[hl][:, :])
                        M[f"ots_{h}"] = inc(m, "dS", 16)

            if stage == 2:
                @block.gpsimd
                def _(g):
                    g.wait_ge(sem["dve"], M["rope_0_0_0_1"])
                    inc(g.dma_start(out=dbg["d_q0"], in_=qk[:, 0:S]), "st", 16)
                    g.wait_ge(sem["dve"], M["rope_0_0_1_1"])
                    inc(g.dma_start(out=dbg["d_k0"], in_=qk[:, 2 * S:3 * S]), "st", 16)
                    g.wait_ge(sem["dve"], M["vev_0_0"])
                    inc(g.dma_start(out=dbg["d_v0"], in_=vsb[:, 0:256]), "st", 16)
            if stage in (2, 3):
                @block.gpsimd
                def _(g):
                    g.wait_ge(sem["dS"], M["ots_0"])
                    inc(g.dma_start(out=dbg["d_ot0"], in_=ot_hbm[0, :, :]), "st", 16)
                    g.wait_ge(sem["dS"], M[f"ots_{2*ngroups-1}"])
                    inc(g.dma_start(out=dbg["d_ot15"], in_=ot_hbm[2*ngroups-1, :, :]), "st", 16)

        if stage >= 4:
            bctx.close()

            @block.gpsimd
            def _(g):
                g.wait_ge(sem["pe"], M[f"v_{NG-1}_{NT-1}"])
                m = g.dma_start(out=xT[:, :].rearrange("p (c f) -> p c f", c=NCH),
                                in_=wo_d.rearrange("(c p) f -> p c f", p=128))
                M["wo"] = inc(m, "dG", 16)

            cctx = ctx.enter_context(ExitStack())
            EC = cctx.enter_context
            otc = [EC(nc.sbuf_tensor(f"otc{i}", [128, S], BF16)) for i in range(2)]
            xtl = [EC(nc.sbuf_tensor(f"xtl{i}", [128, D], F32)) for i in range(2)]
            z1 = EC(nc.sbuf_tensor("z1", [128, D], F32))
            x1 = [EC(nc.sbuf_tensor(f"x1_{i}", [128, D], F32)) for i in range(3)]
            x1b = [EC(nc.sbuf_tensor(f"x1b{i}", [128, D], BF16)) for i in range(3)]
            x1t = [EC(nc.sbuf_tensor("x1t0", [128, D], BF16))]
            sqs = EC(nc.sbuf_tensor("sqs", [128, 512], F32))
            z2 = EC(nc.sbuf_tensor("z2", [128, D], F32))
            outb = [EC(nc.sbuf_tensor("outb0", [128, D], F32))]
            ts1 = EC(nc.sbuf_tensor("ts1", [128, 512], F32))
            stat = [EC(nc.sbuf_tensor(f"stat{i}", [128, 52], F32)) for i in range(3)]
            gsc = [EC(nc.sbuf_tensor(f"gsc{i}", [128, 8], F32)) for i in range(3)]
            prs = EC(nc.sbuf_tensor("prs", [128, 104], F32))
            pr2s = EC(nc.sbuf_tensor("pr2s", [128, 64], F32))
            prr = EC(nc.sbuf_tensor("prr", [128, 64], F32))
            mmbv = [EC(nc.sbuf_tensor("mmb0", [128, 64], BF16))]
            dbgg = EC(nc.sbuf_tensor("dbgg", [128, NT * 16], F32)) if DBG4 else None
            ebf = [EC(nc.sbuf_tensor("ebf0", [128, 8], BF16))]
            ets = [EC(nc.sbuf_tensor("ets0", [8, 128], BF16))]
            mts = [EC(nc.sbuf_tensor("mts0", [64, 128], BF16))]



            def newton_rsqrt(v, st_, vcol, ycol, tmp, j):
                A_, B_ = 0.430762, 0.476801
                sp = lambda: v.tensor_copy(st_[:, 33:34], st_[:, 8:9])
                v.tensor_scalar_min(st_[:, tmp[0]:tmp[0] + 1], st_[:, vcol:vcol + 1], 7.5)
                sp()
                sp()
                v.reciprocal(st_[:, tmp[1]:tmp[1] + 1], st_[:, tmp[0]:tmp[0] + 1])
                sp()
                sp()
                v.tensor_scalar(st_[:, ycol:ycol + 1], st_[:, tmp[1]:tmp[1] + 1],
                                -A_ / B_, B_, ALU.subtract, ALU.mult)
                for _ in range(5):
                    sp()
                    v.tensor_tensor(st_[:, tmp[2]:tmp[2] + 1], st_[:, ycol:ycol + 1],
                                    st_[:, ycol:ycol + 1], ALU.mult)
                    sp()
                    v.tensor_tensor(st_[:, tmp[3]:tmp[3] + 1], st_[:, tmp[2]:tmp[2] + 1],
                                    st_[:, vcol:vcol + 1], ALU.mult)
                    sp()
                    v.tensor_scalar(st_[:, tmp[2]:tmp[2] + 1], st_[:, tmp[3]:tmp[3] + 1],
                                    3.0, -0.5, ALU.subtract, ALU.mult)
                    sp()
                    m = v.tensor_tensor(st_[:, ycol:ycol + 1], st_[:, ycol:ycol + 1],
                                        st_[:, tmp[2]:tmp[2] + 1], ALU.mult)
                return m
            def front(j):
                bi = j % 2
                ti = j % 3
                snap_ser()

                @block.sync
                def _(s_, j=j, bi=bi):
                    ser(s_)
                    if j == 0:
                        jjs = [0, 1]
                        for h in range(NH):
                            s_.wait_ge(sem["dS"], M[f"ots_{h}"])
                    elif j + 1 < NT:
                        jjs = [j + 1]
                        s_.wait_ge(sem["pe"], M[f"y_{j-1}_3"])
                        s_.wait_ge(sem["dve"], M[f"z1_{j-1}_3"])
                    else:
                        jjs = []
                    for jj in jjs:
                        bj = jj % 2
                        m = s_.dma_start(out=xtl[bj][:, :], in_=x_tm[jj * 128:(jj + 1) * 128, :])
                        inc(m, "dS", 16)
                        m = s_.dma_start(out=otc[bj][:, :].rearrange("p (h t) -> p h t", h=NH),
                                         in_=ot_hbm[:, :, jj * 128:(jj + 1) * 128].rearrange("h p t -> p h t"))
                        M[f"cin_{jj}"] = inc(m, "dS", 16)

                dve_fb = C["dve"]
                z1_mark = lambda p: dve_fb + p + 1

                @block.tensor
                def _(t, j=j, bi=bi, ti=j % 3, z1_mark=z1_mark):
                    ser(t)
                    t.wait_ge(sem["dS"], M[f"cin_{j}"])
                    if j == 0:
                        t.wait_ge(sem["dG"], M["wo"])
                        wait_free(t, "PA")
                        wait_free(t, "PB")
                    for p in range(4):
                        ps = PA if p % 2 == 0 else PB
                        if p >= 2:
                            t.wait_ge(sem["dve"], z1_mark(p - 2))
                        elif j > 0:
                            t.wait_ge(sem["dve"], M[f"z1_{j-1}_{p+2}"])
                        for c in range(NCH):
                            t.matmul(ps[:, 0:512], otc[bi][:, c * 128:(c + 1) * 128],
                                     xT[:, c * S + p * 512: c * S + p * 512 + 512],
                                     start=(c == 0), stop=False)
                        m = t.matmul(ps[:, 0:512], onerb[0:1, 0:128],
                                     bor[0:1, p * 512:(p + 1) * 512], start=False, stop=True)
                        M[f"y_{j}_{p}"] = inc(m, "pe")

                @block.vector
                def _(v, j=j, bi=bi, ti=j % 3, z1_mark=z1_mark):
                    ser(v)
                    st_ = stat[ti]
                    for p in range(4):
                        ps = PA if p % 2 == 0 else PB
                        v.wait_ge(sem["pe"], M[f"y_{j}_{p}"])
                        if p == 0 and j > 0:
                            v.wait_ge(sem["act"], M[f"sq1_{j-1}_3"])
                        m = v.scalar_tensor_tensor(z1[:, p * 512:(p + 1) * 512],
                                                   xtl[bi][:, p * 512:(p + 1) * 512], 1.0,
                                                   ps[:, 0:512], ALU.mult, ALU.add,
                                                   accum_out=st_[:, p:p + 1])
                        M[f"z1_{j}_{p}"] = assert_eq(inc(m, "dve"), z1_mark(p))

                @block.scalar
                def _(sc, j=j, bi=bi, ti=j % 3):
                    ser(sc)
                    st_ = stat[ti]
                    for p in range(4):
                        sc.wait_ge(sem["dve"], M[f"z1_{j}_{p}"])
                        m = sc.activation(sqs[:, :], z1[:, p * 512:(p + 1) * 512], AF.Square,
                                          accum_out=st_[:, 4 + p:5 + p])
                        M[f"sq1_{j}_{p}"] = inc(m, "act")

                @block.vector
                def _(v, j=j, bi=bi, ti=j % 3):
                    ser(v)
                    st_ = stat[ti]
                    v.wait_ge(sem["act"], M[f"sq1_{j}_3"])
                    v.reduce_sum(st_[:, 8:9], st_[:, 0:4], axis=AX.X)
                    v.reduce_sum(st_[:, 9:10], st_[:, 4:8], axis=AX.X)
                    v.tensor_scalar(st_[:, 10:11], st_[:, 8:9], 1.0 / D, None, ALU.mult)
                    v.tensor_scalar(st_[:, 11:12], st_[:, 9:10], 1.0 / D, None, ALU.mult)
                    v.tensor_tensor(st_[:, 12:13], st_[:, 10:11], st_[:, 10:11], ALU.mult)
                    v.tensor_copy(st_[:, 33:34], st_[:, 8:9])
                    v.tensor_copy(st_[:, 33:34], st_[:, 9:10])
                    v.tensor_tensor(st_[:, 13:14], st_[:, 11:12], st_[:, 12:13], ALU.subtract)
                    v.tensor_copy(st_[:, 33:34], st_[:, 8:9])
                    v.tensor_copy(st_[:, 33:34], st_[:, 9:10])
                    m = v.tensor_scalar(st_[:, 14:15], st_[:, 13:14], EPS, None, ALU.add)
                    M[f"st1_{j}"] = inc(m, "dve")

                @block.vector
                def _(v, j=j, bi=bi, ti=j % 3):
                    ser(v)
                    st_ = stat[ti]
                    m = newton_rsqrt(v, st_, 14, 16, (41, 42, 43, 15), j)
                    M[f"rs1_{j}"] = inc(m, "dve")
                    v.tensor_copy(st_[:, 33:34], st_[:, 8:9])
                    v.tensor_tensor(st_[:, 17:18], st_[:, 16:17], st_[:, 16:17], ALU.mult)
                    v.tensor_scalar(x1[ti][:, :], z1[:, :], st_[:, 10:11], st_[:, 16:17],
                                    ALU.subtract, ALU.mult)
                    v.tensor_scalar(st_[:, 19:20], st_[:, 17:18], -EPS * float(D), None, ALU.mult)
                    v.tensor_copy(st_[:, 33:34], st_[:, 14:15])
                    v.tensor_copy(st_[:, 33:34], st_[:, 17:18])
                    v.tensor_scalar(st_[:, 18:19], st_[:, 19:20], float(D), None, ALU.add)
                    v.tensor_copy(st_[:, 33:34], st_[:, 14:15])
                    v.tensor_copy(st_[:, 33:34], st_[:, 17:18])
                    m = v.tensor_scalar(st_[:, 44:52], m2bc[:, :], st_[:, 18:19], None, ALU.add)
                    if DBG4:
                        v.tensor_copy(dbgg[:, j * 16 + 12:j * 16 + 13], st_[:, 16:17])
                        v.tensor_copy(dbgg[:, j * 16 + 13:j * 16 + 14], st_[:, 17:18])
                        v.tensor_copy(dbgg[:, j * 16 + 14:j * 16 + 15], st_[:, 14:15])
                        v.tensor_copy(dbgg[:, j * 16 + 15:j * 16 + 16], st_[:, 18:19])
                    M[f"x1_{j}"] = inc(m, "dve")

                @block.scalar
                def _(sc, j=j, bi=bi, ti=j % 3):
                    ser(sc)
                    sc.wait_ge(sem["dve"], M[f"x1_{j}"])
                    if j >= 3:
                        sc.wait_ge(sem["pe"], M[f"x1tB_{j-3}"])
                    m = sc.activation(x1b[ti][:, :], x1[ti][:, :], AF.Copy)
                    M[f"x1b_{j}"] = inc(m, "act")


            def back1(j):
                bi = j % 2
                ti = j % 3
                snap_ser()

                @block.tensor
                def _(t, j=j, bi=bi, ti=j % 3):
                    ser(t)
                    t.wait_ge(sem["act"], M[f"x1b_{j}"])
                    if j >= 1:
                        t.wait_ge(sem["dve"], M[f"x1t_{j-1}"])
                    else:
                        wait_free(t, "PD")
                    for c in range(8):
                        m = t.transpose(PDb[:, c * 128:(c + 1) * 128],
                                        x1b[ti][:, c * 128:(c + 1) * 128], idb[:, :])
                    M[f"x1tA_{j}"] = inc(m, "pe")

                @block.vector
                def _(v, j=j, bi=bi, ti=j % 3):
                    ser(v)
                    v.wait_ge(sem["pe"], M[f"x1tA_{j}"])
                    m = v.tensor_copy(x1t[0][:, 0:1024], PDb[:, 0:1024])
                    M[f"x1eA_{j}"] = inc(m, "dve")

                @block.tensor
                def _(t, j=j, bi=bi, ti=j % 3):
                    ser(t)
                    t.wait_ge(sem["dve"], M[f"x1eA_{j}"])
                    for c in range(8, NCH):
                        m = t.transpose(PDb[:, (c - 8) * 128:(c - 7) * 128],
                                        x1b[ti][:, c * 128:(c + 1) * 128], idb[:, :])
                    M[f"x1tB_{j}"] = inc(m, "pe")

                @block.vector
                def _(v, j=j, bi=bi, ti=j % 3):
                    ser(v)
                    v.wait_ge(sem["pe"], M[f"x1tB_{j}"])
                    m = v.tensor_copy(x1t[0][:, 1024:2048], PDb[:, 0:1024])
                    M[f"x1t_{j}"] = inc(m, "dve")

                @block.tensor
                def _(t, j=j, bi=bi, ti=j % 3):
                    ser(t)
                    t.wait_ge(sem["dve"], M[f"x1t_{j}"])
                    if j >= 1:
                        t.wait_ge(sem["dve"], M[f"gate_{j-1}"])
                    else:
                        wait_free(t, "PC")
                    for c in range(NCH):
                        t.matmul(PC[:, 0:136], x1t[0][:, c * 128:(c + 1) * 128],
                                 ffnw[:, c * 136:(c + 1) * 136], start=(c == 0), stop=False)
                    m = t.matmul(PC[:, 0:136], onerb[0:1, 0:128], crow[0:1, :],
                                 start=False, stop=True)
                    M[f"ffn_{j}"] = inc(m, "pe")

                @block.vector
                def _(v, j=j):
                    ser(v)
                    v.wait_ge(sem["pe"], M[f"ffn_{j}"])
                    v.tensor_copy(prr[:, :], PC[:, 8:72])
                    v.tensor_copy(prs[:, 96:104], m2bc[:, :])
                    v.tensor_copy(prs[:, 96:104], cenbc[:, :])
                    m = v.scalar_tensor_tensor(pr2s[:, :], prr[:, :], 1.0, prr[:, :],
                                               ALU.mult, ALU.mult)
                    M[f"pr2_{j}"] = inc(m, "dve")

                @block.vector
                def _(v, j=j, bi=bi, ti=j % 3):
                    ser(v)
                    st_ = stat[ti]
                    gs = gsc[ti]
                    v.wait_ge(sem["dve"], M[f"pr2_{j}"])
                    v.scalar_tensor_tensor(prs[:, 8:16], PC[:, 0:8], -2.0, st_[:, 44:52],
                                           ALU.mult, ALU.add)
                    for n in range(8):
                        v.reduce_sum(prs[:, 40 + n:41 + n], pr2s[:, n * 8:(n + 1) * 8], axis=AX.X)
                    v.reciprocal(prs[:, 16:24], prs[:, 8:16])
                    v.tensor_copy(prs[:, 96:104], m2bc[:, :])
                    v.tensor_copy(prs[:, 96:104], cenbc[:, :])
                    v.tensor_tensor(prs[:, 24:32], prs[:, 40:48], prs[:, 16:24], ALU.mult)
                    v.tensor_copy(prs[:, 96:104], m2bc[:, :])
                    v.tensor_copy(prs[:, 96:104], cenbc[:, :])
                    v.tensor_tensor(prs[:, 32:40], prs[:, 24:32], cenbc[:, :], ALU.subtract)
                    v.tensor_copy(prs[:, 96:104], m2bc[:, :])
                    v.tensor_copy(prs[:, 96:104], cenbc[:, :])
                    m = v.tensor_tensor(prs[:, 48:56], prs[:, 32:40], slpbc[:, :], ALU.mult)
                    M[f"arg_{j}"] = inc(m, "dve")

                @block.scalar
                def _(sc, j=j, bi=bi, ti=j % 3):
                    ser(sc)
                    gs = gsc[ti]
                    sc.wait_ge(sem["dve"], M[f"arg_{j}"])
                    m = sc.activation(gs[:, :], prs[:, 48:56], AF.Sigmoid)
                    M[f"eneg_{j}"] = inc(m, "act")

                @block.vector
                def _(v, j=j, bi=bi, ti=j % 3):
                    ser(v)
                    st_ = stat[ti]
                    gs = gsc[ti]
                    v.wait_ge(sem["act"], M[f"eneg_{j}"])
                    if j >= 1:
                        v.wait_ge(sem["pe"], M[f"ett_{j-1}"])
                    v.reduce_max(st_[:, 30:31], gs[:, :], axis=AX.X)
                    v.tensor_copy(prs[:, 96:104], m2bc[:, :])
                    v.tensor_copy(prs[:, 96:104], cenbc[:, :])
                    v.tensor_scalar(st_[:, 31:32], st_[:, 30:31], 2.0, -1.0, ALU.subtract, ALU.mult)
                    v.tensor_scalar(prs[:, 72:80], gs[:, :], st_[:, 30:31], None, ALU.is_ge)
                    v.tensor_copy(prs[:, 96:104], m2bc[:, :])
                    v.tensor_copy(prs[:, 96:104], cenbc[:, :])
                    v.tensor_tensor(prs[:, 80:88], prs[:, 72:80], wties[:, :], ALU.mult)
                    v.tensor_copy(prs[:, 96:104], m2bc[:, :])
                    v.tensor_copy(prs[:, 96:104], cenbc[:, :])
                    v.reduce_max(st_[:, 32:33], prs[:, 80:88], axis=AX.X)
                    v.tensor_copy(st_[:, 33:34], st_[:, 30:31])
                    v.tensor_copy(st_[:, 33:34], st_[:, 31:32])
                    v.tensor_scalar(prs[:, 88:96], prs[:, 80:88], st_[:, 32:33], None, ALU.is_ge)
                    v.tensor_copy(prs[:, 96:104], m2bc[:, :])
                    v.tensor_copy(prs[:, 96:104], cenbc[:, :])
                    v.tensor_copy(ebf[0][:, :], prs[:, 88:96])
                    if DBG4:
                        v.tensor_copy(dbgg[:, j * 16:j * 16 + 8], gs[:, :])
                        v.tensor_copy(dbgg[:, j * 16 + 8:j * 16 + 9], st_[:, 30:31])
                        v.tensor_copy(dbgg[:, j * 16 + 9:j * 16 + 10], st_[:, 18:19])
                        v.tensor_copy(dbgg[:, j * 16 + 10:j * 16 + 11], prs[:, 8:9])
                        v.tensor_copy(dbgg[:, j * 16 + 11:j * 16 + 12], prs[:, 88:89])
                    last = None
                    for n in range(8):
                        last = v.tensor_scalar(mmbv[0][:, n * 8:(n + 1) * 8],
                                               PC[:, 72 + n * 8: 80 + n * 8],
                                               prs[:, 88 + n:89 + n], None, ALU.mult)
                    M[f"gate_{j}"] = inc(last, "dve")

            def back2(j):
                bi = j % 2
                ti = j % 3
                snap_ser()

                @block.tensor
                def _(t, j=j, bi=bi, ti=j % 3):
                    ser(t)
                    t.wait_ge(sem["dve"], M[f"gate_{j}"])
                    if j >= 1:
                        t.wait_ge(sem["dve"], M[f"etv_{j-1}"])
                    t.transpose(PBb[0:8, 1024:1152], ebf[0][:, :], idb[:, :])
                    m = t.transpose(PBb[0:64, 1152:1280], mmbv[0][:, :], idb[:, :])
                    M[f"ett_{j}"] = inc(m, "pe")

                @block.vector
                def _(v, j=j, bi=bi, ti=j % 3):
                    ser(v)
                    v.wait_ge(sem["pe"], M[f"ett_{j}"])
                    v.tensor_copy(ets[0][:, :], PBb[0:8, 1024:1152])
                    m = v.tensor_copy(mts[0][:, :], PBb[0:64, 1152:1280])
                    M[f"etv_{j}"] = inc(m, "dve")

                dve_zb = C["dve"]
                z2_mark = lambda p: dve_zb + p + 1

                @block.tensor
                def _(t, j=j, bi=bi, ti=j % 3, z2_mark=z2_mark):
                    ser(t)
                    t.wait_ge(sem["dve"], M[f"etv_{j}"])
                    for p in range(4):
                        bank = "PE" if p % 2 == 0 else "PF"
                        ps = PE_ if p % 2 == 0 else PF
                        if p >= 2:
                            t.wait_ge(sem["dve"], z2_mark(p - 2))
                        elif j > 0:
                            t.wait_ge(sem["dve"], M[f"z2_{j-1}_{p+2}"])
                        else:
                            wait_free(t, bank)
                        t.matmul(ps[:, 0:512], mts[0][:, :], dgn[:, p * 512:(p + 1) * 512],
                                 start=True, stop=False)
                        m = t.matmul(ps[:, 0:512], ets[0][:, :], bwsb[:, p * 512:(p + 1) * 512],
                                     start=False, stop=True)
                        M[f"mbx_{j}_{p}"] = inc(m, "pe")

                @block.vector
                def _(v, j=j, bi=bi, ti=j % 3, z2_mark=z2_mark):
                    ser(v)
                    st_ = stat[ti]
                    for p in range(4):
                        ps = PE_ if p % 2 == 0 else PF
                        v.wait_ge(sem["pe"], M[f"mbx_{j}_{p}"])
                        if p == 0 and j > 0:
                            v.wait_ge(sem["act"], M[f"sq2_{j-1}_3"])
                        v.tensor_scalar(ts1[:, :], x1[ti][:, p * 512:(p + 1) * 512],
                                        st_[:, 31:32], None, ALU.mult)
                        m = v.scalar_tensor_tensor(z2[:, p * 512:(p + 1) * 512], ps[:, 0:512],
                                                   st_[:, 30:31], ts1[:, :], ALU.mult, ALU.add,
                                                   accum_out=st_[:, 20 + p:21 + p])
                        M[f"z2_{j}_{p}"] = inc(m, "dve")
                        assert M[f"z2_{j}_{p}"] == z2_mark(p), (M[f"z2_{j}_{p}"], z2_mark(p))

                @block.scalar
                def _(sc, j=j, bi=bi, ti=j % 3):
                    ser(sc)
                    st_ = stat[ti]
                    for p in range(4):
                        sc.wait_ge(sem["dve"], M[f"z2_{j}_{p}"])
                        m = sc.activation(sqs[:, :], z2[:, p * 512:(p + 1) * 512], AF.Square,
                                          accum_out=st_[:, 24 + p:25 + p])
                        M[f"sq2_{j}_{p}"] = inc(m, "act")

                @block.vector
                def _(v, j=j, bi=bi, ti=j % 3):
                    ser(v)
                    st_ = stat[ti]
                    v.wait_ge(sem["act"], M[f"sq2_{j}_3"])
                    v.reduce_sum(st_[:, 28:29], st_[:, 20:24], axis=AX.X)
                    v.reduce_sum(st_[:, 29:30], st_[:, 24:28], axis=AX.X)
                    v.tensor_scalar(st_[:, 34:35], st_[:, 28:29], 1.0 / D, None, ALU.mult)
                    v.tensor_scalar(st_[:, 35:36], st_[:, 29:30], 1.0 / D, None, ALU.mult)
                    v.tensor_tensor(st_[:, 36:37], st_[:, 34:35], st_[:, 34:35], ALU.mult)
                    v.tensor_copy(st_[:, 33:34], st_[:, 28:29])
                    v.tensor_copy(st_[:, 33:34], st_[:, 29:30])
                    v.tensor_tensor(st_[:, 37:38], st_[:, 35:36], st_[:, 36:37], ALU.subtract)
                    v.tensor_copy(st_[:, 33:34], st_[:, 28:29])
                    v.tensor_copy(st_[:, 33:34], st_[:, 29:30])
                    m = v.tensor_scalar(st_[:, 38:39], st_[:, 37:38], EPS, None, ALU.add)
                    M[f"st2_{j}"] = inc(m, "dve")

                @block.vector
                def _(v, j=j, bi=bi, ti=j % 3):
                    ser(v)
                    st_ = stat[ti]
                    m = newton_rsqrt(v, st_, 38, 40, (41, 42, 43, 39), j)
                    M[f"rs2_{j}"] = inc(m, "dve")
                    if j >= 1:
                        v.wait_ge(sem["st"], M[f"yst_{j-1}"])
                    v.tensor_copy(st_[:, 33:34], st_[:, 8:9])
                    v.tensor_copy(st_[:, 33:34], st_[:, 9:10])
                    m = v.tensor_scalar(outb[0][:, :], z2[:, :], st_[:, 34:35], st_[:, 40:41],
                                        ALU.subtract, ALU.mult)
                    M[f"out_{j}"] = inc(m, "dve")

                @block.sync
                def _(s_, j=j, bi=bi):
                    ser(s_)
                    s_.wait_ge(sem["dve"], M[f"out_{j}"])
                    m = s_.dma_start(out=y_d[j * 128:(j + 1) * 128, :], in_=outb[0][:, :])
                    M[f"yst_{j}"] = inc(m, "st", 16)

            for j in range(NT):
                if j >= 3:
                    back2(j - 3)
                if j >= 2:
                    back1(j - 2)
                front(j)
            back2(NT - 3)
            back1(NT - 2)
            back2(NT - 2)
            back1(NT - 1)
            back2(NT - 1)

            if DBG4:
                @block.gpsimd
                def _(g):
                    g.wait_ge(sem["dve"], M[f"x1t_{NT-1}"])
                    inc(g.dma_start(out=dbg["d_ota"], in_=ot_hbm[:, :, :]), "st", 16)
                    g.wait_ge(sem["dve"], M[f"gate_{NT-1}"])
                    inc(g.dma_start(out=dbg["d_gg"], in_=dbgg[:, :]), "st", 16)
                    g.wait_ge(sem["dve"], M["p1_done"])
                    inc(g.dma_start(out=dbg["d_slpbc"], in_=slpbc[:, :]), "st", 16)
                    g.wait_ge(sem["dve"], M[f"etv_{NT-1}"])
                    g.wait_ge(sem["dve"], M[f"out_{NT-1}"])
                    inc(g.dma_start(out=dbg["d_ets"], in_=ets[(NT-1) % 2][:, :]), "st", 16)
                    inc(g.dma_start(out=dbg["d_mts"], in_=mts[(NT-1) % 2][:, :]), "st", 16)
                    inc(g.dma_start(out=dbg["d_ebf"], in_=ebf[(NT-1) % 2][:, :]), "st", 16)
                    inc(g.dma_start(out=dbg["d_mmb"], in_=mmbv[(NT-1) % 2][:, :]), "st", 16)

            @block.sync
            def _(s_):
                s_.wait_ge(sem["st"], C["st"])
        else:
            @block.sync
            def _(s_):
                s_.wait_ge(sem["st"], C["st"])

    return nc, dbg_names


def assert_eq(a, b):
    assert a == b, (a, b)
    return a


def make_in_maps(inputs):
    x = np.asarray(inputs["L0_attn_hidden_state"])
    perm = _perm_deinterleave()
    Wq = np.ascontiguousarray(np.asarray(inputs["L0_attn_Wq"])[:, perm])
    Wk = np.ascontiguousarray(np.asarray(inputs["L0_attn_Wk"])[:, perm])
    Wv = np.ascontiguousarray(np.asarray(inputs["L0_attn_Wv"]))
    Wo = np.ascontiguousarray(np.asarray(inputs["L0_attn_Wo"]))
    bq = np.asarray(inputs["L0_attn_bq"])[perm][None, :].copy()
    bk = np.asarray(inputs["L0_attn_bk"])[perm][None, :].copy()
    bv = np.asarray(inputs["L0_attn_bv"])[None, :].copy()
    bo = np.asarray(inputs["L0_attn_bo"])[None, :].copy()
    selw = np.asarray(inputs["L0_ffn_select_weight"])
    md = np.asarray(inputs["L0_ffn_select_mean_diff"])
    cen = np.asarray(inputs["L0_ffn_imp_center"])[None, :].copy()
    slp = np.asarray(inputs["L0_ffn_imp_slope"])[None, :].copy()
    upd = np.asarray(inputs["L0_ffn_lora_update"])
    dgnw = np.asarray(inputs["L0_ffn_lora_degen"])
    bwv = np.ascontiguousarray(np.asarray(inputs["L0_ffn_bias_w"]))
    debw = np.asarray(inputs["L0_ffn_debias_w"])

    ffnw = np.ascontiguousarray(
        np.concatenate([md.T, selw.transpose(1, 0, 2).reshape(D, 64),
                        upd.transpose(1, 0, 2).reshape(D, 64)], axis=1)).astype(np.float32)
    dgnt = np.ascontiguousarray(dgnw.transpose(0, 2, 1).reshape(64, D))
    debt = np.ascontiguousarray(debw.T)
    ropec, ropes = _rope_tables()
    sel16 = np.zeros((16, D), dtype=np.float32)
    for jj in range(16):
        sel16[jj, jj * 128:(jj + 1) * 128] = 1.0

    in_maps = []
    for b in range(8):
        in_maps.append({
            "x_tm": np.ascontiguousarray(x[b]),
            "x_fm": np.ascontiguousarray(x[b].T),
            "wq": Wq, "wk": Wk, "wv": Wv, "wo": Wo,
            "bq": bq, "bk": bk, "bv": bv, "bo": bo,
            "ffnw": ffnw, "dgnt": dgnt, "bw": bwv, "debt": debt,
            "cen": cen, "slp": slp,
            "ropec": ropec, "ropes": ropes, "sel16": sel16,
        })
    return in_maps


def kernel(**inputs):
    in_maps = make_in_maps(inputs)
    nc, _ = build_nc(STAGE)
    res = run_bass_kernel_spmd(nc, in_maps, core_ids=list(range(8)))
    return np.stack([np.asarray(r["y"], dtype=np.float32) for r in res.results], axis=0)


# version 14
# speedup vs baseline: 1.4347x; 1.0299x over previous
import math
import numpy as np
import ml_dtypes
import concourse.bass as bass
import concourse.mybir as mybir
from concourse.bass_utils import run_bass_kernel_spmd
from contextlib import ExitStack

F32 = mybir.dt.float32
BF16 = mybir.dt.bfloat16
AF = mybir.ActivationFunctionType
ALU = mybir.AluOpType
AX = mybir.AxisListType
bfloat16 = ml_dtypes.bfloat16

D = 2048
S = 2048
NH = 16
HD = 128
NCH = 16
NT = 16
G = 2
NG = NH // G
WQ = 256
NQP = S // WQ
SCALE = 1.0 / math.sqrt(HD)
EPS = 1e-5

STAGE = 4
DBG4 = False
SERIAL = False
LAG = True
PIN_ACT = False
ACT_SET_ID = 0


def _perm_deinterleave():
    p = np.concatenate([np.arange(0, HD, 2), np.arange(1, HD, 2)])
    return np.concatenate([h * HD + p for h in range(NH)])


def _rope_tables():
    k = np.arange(HD)
    j = (k % 64).astype(np.float64)
    inv = np.power(10000.0, -2.0 * j / HD)
    ang = inv[:, None] * np.arange(S, dtype=np.float64)[None, :]
    c = np.cos(ang).astype(np.float32).astype(bfloat16)
    s = np.sin(ang).astype(np.float32).astype(bfloat16)
    return c, s


def build_nc(stage=STAGE):
    nc = bass.Bass("TRN2", target_bir_lowering=False, debug=False)
    dbg_names = []

    x_tm = nc.dram_tensor("x_tm", [S, D], F32, kind="ExternalInput").ap()
    x_fm = nc.dram_tensor("x_fm", [D, S], F32, kind="ExternalInput").ap()
    wq_d = nc.dram_tensor("wq", [D, D], F32, kind="ExternalInput").ap()
    wk_d = nc.dram_tensor("wk", [D, D], F32, kind="ExternalInput").ap()
    wv_d = nc.dram_tensor("wv", [D, D], F32, kind="ExternalInput").ap()
    wo_d = nc.dram_tensor("wo", [D, D], F32, kind="ExternalInput").ap()
    bq_d = nc.dram_tensor("bq", [1, D], F32, kind="ExternalInput").ap()
    bk_d = nc.dram_tensor("bk", [1, D], F32, kind="ExternalInput").ap()
    bv_d = nc.dram_tensor("bv", [1, D], F32, kind="ExternalInput").ap()
    bo_d = nc.dram_tensor("bo", [1, D], F32, kind="ExternalInput").ap()
    ffnw_d = nc.dram_tensor("ffnw", [D, 136], F32, kind="ExternalInput").ap()
    dgnt_d = nc.dram_tensor("dgnt", [64, D], F32, kind="ExternalInput").ap()
    bw_d = nc.dram_tensor("bw", [8, D], F32, kind="ExternalInput").ap()
    debt_d = nc.dram_tensor("debt", [D, 8], F32, kind="ExternalInput").ap()
    cen_d = nc.dram_tensor("cen", [1, 8], F32, kind="ExternalInput").ap()
    slp_d = nc.dram_tensor("slp", [1, 8], F32, kind="ExternalInput").ap()
    ropec_d = nc.dram_tensor("ropec", [HD, S], BF16, kind="ExternalInput").ap()
    ropes_d = nc.dram_tensor("ropes", [HD, S], BF16, kind="ExternalInput").ap()
    sel_d = nc.dram_tensor("sel16", [16, D], F32, kind="ExternalInput").ap()

    y_d = nc.dram_tensor("y", [S, D], F32, kind="ExternalOutput").ap()
    ot_hbm = nc.dram_tensor("ot_hbm", [NH, HD, S], BF16, kind="Internal").ap()
    x1b_hbm = nc.dram_tensor("x1b_hbm", [S, D], BF16, kind="Internal").ap()

    dbg = {}
    if stage == 4 and DBG4:
        dbg["d_x1"] = nc.dram_tensor("d_x1", [S, D], F32, kind="ExternalOutput").ap()
        dbg_names.append("d_x1")
        dbg["d_ota"] = nc.dram_tensor("d_ota", [NH, HD, S], F32, kind="ExternalOutput").ap()
        dbg_names.append("d_ota")
        dbg["d_gg"] = nc.dram_tensor("d_gg", [128, NT * 16], F32, kind="ExternalOutput").ap()
        dbg_names.append("d_gg")
        dbg["d_slpbc"] = nc.dram_tensor("d_slpbc", [128, 8], F32, kind="ExternalOutput").ap()
        dbg_names.append("d_slpbc")
        dbg["d_ets"] = nc.dram_tensor("d_ets", [8, 128], F32, kind="ExternalOutput").ap()
        dbg["d_mts"] = nc.dram_tensor("d_mts", [64, 128], F32, kind="ExternalOutput").ap()
        dbg["d_ebf"] = nc.dram_tensor("d_ebf", [128, 8], F32, kind="ExternalOutput").ap()
        dbg["d_mmb"] = nc.dram_tensor("d_mmb", [128, 64], F32, kind="ExternalOutput").ap()
        dbg_names.extend(["d_ets", "d_mts", "d_ebf", "d_mmb"])
    if stage < 4:
        for nm, shp in (("d_crow", [1, 136]), ("d_m2bc", [128, 8]), ("d_cen", [128, 8])):
            dbg[nm] = nc.dram_tensor(nm, shp, F32, kind="ExternalOutput").ap()
            dbg_names.append(nm)
    if stage in (2, 3):
        for nm, shp in (("d_q0", [HD, S]), ("d_k0", [HD, S]), ("d_v0", [128, 256]),
                        ("d_ot0", [HD, S]), ("d_ot15", [HD, S])):
            dbg[nm] = nc.dram_tensor(nm, shp, F32, kind="ExternalOutput").ap()
            dbg_names.append(nm)

    ident_bf = nc.inline_tensor(np.eye(128, dtype=bfloat16), name="ident_bf").ap()
    ident_f = nc.inline_tensor(np.eye(128, dtype=np.float32), name="ident_f").ap()
    rt = np.zeros((128, 128), dtype=np.float32)
    for i in range(64):
        rt[i, 64 + i] = 1.0
        rt[64 + i, i] = -1.0
    rt_bf = nc.inline_tensor(rt.astype(bfloat16), name="rt_bf").ap()
    ones_r_bf = nc.inline_tensor(np.ones((1, 512), dtype=bfloat16), name="ones_r_bf").ap()
    ones_c_bf = nc.inline_tensor(np.ones((128, 1), dtype=bfloat16), name="ones_c_bf").ap()
    ones_r_f = nc.inline_tensor(np.ones((1, 128), dtype=np.float32), name="ones_r_f").ap()
    ones8_f = nc.inline_tensor(np.ones((8, 1), dtype=np.float32), name="ones8_f").ap()
    eye8_c = nc.inline_tensor(np.eye(8, dtype=np.float32), name="eye8_c").ap()
    m64 = np.zeros((8, 64), dtype=np.float32)
    for n in range(8):
        m64[n, n * 8:(n + 1) * 8] = 1.0
    mask64_c = nc.inline_tensor(m64, name="mask64_c").ap()
    wt = np.tile((8.0 - np.arange(8, dtype=np.float32))[None, :], (128, 1))
    wtie_c = nc.inline_tensor(wt, name="wtie_c").ap()

    with ExitStack() as ctx:
        E = ctx.enter_context
        xT = E(nc.sbuf_tensor("xT", [128, NCH * S], BF16))
        ffnw = E(nc.sbuf_tensor("ffnw_sb", [128, NCH * 136], BF16))
        dgn = E(nc.sbuf_tensor("dgn", [64, D], BF16))
        bwsb = E(nc.sbuf_tensor("bwsb", [8, D], BF16))
        debt = E(nc.sbuf_tensor("debt_sb", [128, NCH * 8], BF16))
        bqr = E(nc.sbuf_tensor("bqr", [1, D], BF16))
        bkr = E(nc.sbuf_tensor("bkr", [1, D], BF16))
        bvr = E(nc.sbuf_tensor("bvr", [1, D], BF16))
        bor = E(nc.sbuf_tensor("bor", [1, D], BF16))
        cenr = E(nc.sbuf_tensor("cenr", [1, 8], F32))
        slpr = E(nc.sbuf_tensor("slpr", [1, 8], F32))
        selsb = E(nc.sbuf_tensor("selsb", [16, D], F32))
        ropecs = E(nc.sbuf_tensor("ropecs", [128, S], BF16))
        ropess = E(nc.sbuf_tensor("ropess", [128, S], BF16))
        idb = E(nc.sbuf_tensor("idb", [128, 128], BF16))
        idf = E(nc.sbuf_tensor("idf", [128, 128], F32))
        rtb = E(nc.sbuf_tensor("rtb", [128, 128], BF16))
        onerb = E(nc.sbuf_tensor("onerb", [1, 512], BF16))
        onecb = E(nc.sbuf_tensor("onecb", [128, 1], BF16))
        onerf = E(nc.sbuf_tensor("onerf", [1, 128], F32))
        one8f = E(nc.sbuf_tensor("one8f", [8, 1], F32))
        eye8s = E(nc.sbuf_tensor("eye8s", [8, 8], F32))
        m64s = E(nc.sbuf_tensor("m64s", [8, 64], F32))
        wties = E(nc.sbuf_tensor("wties", [128, 8], F32))
        crow = E(nc.sbuf_tensor("crow_sb", [1, 136], BF16))
        m2bc = E(nc.sbuf_tensor("m2bc", [128, 8], F32))
        cenbc = E(nc.sbuf_tensor("cenbc", [128, 8], F32))
        slpbc = E(nc.sbuf_tensor("slpbc", [128, 8], F32))
        scr8 = E(nc.sbuf_tensor("scr8", [8, 136], F32))

        PA = E(nc.psum_tensor("PA", [128, 1024], F32))
        PB = E(nc.psum_tensor("PB", [128, 1024], F32))
        PC = E(nc.psum_tensor("PC", [128, 512], F32))
        PD = E(nc.psum_tensor("PD", [128, 512], F32))
        PE_ = E(nc.psum_tensor("PEb", [128, 512], F32))
        PF = E(nc.psum_tensor("PFb", [128, 512], F32))
        PDb = PD[:, :].bitcast(BF16)
        PBb = PB[:, :].bitcast(BF16)

        sem = {}
        for name in ("pe", "act", "dve", "dG", "dS", "dA", "st"):
            sem[name] = E(nc.semaphore(name))
        C = {k: 0 for k in sem}
        block = E(nc.Block())
        M = {}

        def inc(ins, s_, n=1):
            ins.then_inc(sem[s_], n)
            C[s_] += n
            return C[s_]

        SNAP = {}

        def snap_ser():
            SNAP.update({k: C[k] for k in ("pe", "act", "dve")})

        def ser(eng):
            if SERIAL:
                for s2 in ("pe", "act", "dve"):
                    eng.wait_ge(sem[s2], SNAP.get(s2, 0))

        free = {"PA": None, "PB": None, "PC": None, "PD": None, "PE": None, "PF": None}

        def wait_free(eng, bank):
            if free[bank] is not None:
                s_, v_ = free[bank]
                eng.wait_ge(sem[s_], v_)

        @block.gpsimd
        def _(g):
            inc(g.dma_start(out=xT[:, :].rearrange("p (c t) -> p c t", c=NCH),
                            in_=x_fm.rearrange("(c p) t -> p c t", p=128)), "dG", 16)
            inc(g.dma_start(out=ffnw[:, :].rearrange("p (c f) -> p c f", c=NCH),
                            in_=ffnw_d.rearrange("(c p) f -> p c f", p=128)), "dG", 16)
            inc(g.dma_start(out=debt[:, :].rearrange("p (c f) -> p c f", c=NCH),
                            in_=debt_d.rearrange("(c p) f -> p c f", p=128)), "dG", 16)
            inc(g.dma_start(out=dgn[:, :], in_=dgnt_d), "dG", 16)
            inc(g.dma_start(out=bwsb[:, :], in_=bw_d), "dG", 16)
            inc(g.dma_start(out=bqr[:, :], in_=bq_d), "dG", 16)
            inc(g.dma_start(out=bkr[:, :], in_=bk_d), "dG", 16)
            inc(g.dma_start(out=bvr[:, :], in_=bv_d), "dG", 16)
            inc(g.dma_start(out=bor[:, :], in_=bo_d), "dG", 16)
            M["p0_gp"] = C["dG"]

        @block.sync
        def _(s_):
            for dst, src in ((idb, ident_bf), (idf, ident_f), (rtb, rt_bf),
                             (onerb, ones_r_bf), (onecb, ones_c_bf), (onerf, ones_r_f),
                             (one8f, ones8_f), (eye8s, eye8_c), (m64s, mask64_c),
                             (wties, wtie_c), (cenr, cen_d), (slpr, slp_d), (selsb, sel_d),
                             (ropecs, ropec_d), (ropess, ropes_d)):
                inc(s_.dma_start(out=dst[:, :], in_=src), "dS", 16)
            M["p0_sy"] = C["dS"]

        if PIN_ACT:
            @block.scalar
            def _(sc):
                sc.add_instruction(mybir.InstLoadActFuncSet(
                    name=nc.get_next_instruction_name(), act_func_set_id=ACT_SET_ID, ins=[], outs=[]))

        @block.tensor
        def _(t):
            t.wait_ge(sem["dG"], M["p0_gp"])
            t.wait_ge(sem["dS"], M["p0_sy"])
            for c in range(NCH):
                f0 = c * 136
                t.matmul(PC[0:8, 0:64], ffnw[:, f0:f0 + 8], ffnw[:, f0 + 8:f0 + 72],
                         start=(c == 0), stop=(c == NCH - 1))
            for c in range(NCH):
                f0 = c * 136
                t.matmul(PC[0:8, 64:128], debt[:, c * 8:(c + 1) * 8], ffnw[:, f0 + 72:f0 + 136],
                         start=(c == 0), stop=(c == NCH - 1))
            for c in range(NCH):
                f0 = c * 136
                m = t.matmul(PC[0:8, 128:136], ffnw[:, f0:f0 + 8], ffnw[:, f0:f0 + 8],
                             start=(c == 0), stop=(c == NCH - 1))
            M["p1_mm"] = inc(m, "pe")

        @block.vector
        def _(v):
            v.wait_ge(sem["pe"], M["p1_mm"])
            v.tensor_tensor(scr8[0:8, 0:64], PC[0:8, 0:64], m64s[:, :], ALU.mult)
            v.tensor_tensor(scr8[0:8, 64:128], PC[0:8, 64:128], m64s[:, :], ALU.mult)
            m = v.tensor_tensor(scr8[0:8, 128:136], PC[0:8, 128:136], eye8s[:, :], ALU.mult)
            M["p1_msk"] = inc(m, "dve")
        free["PC"] = ("dve", M["p1_msk"])

        @block.tensor
        def _(t):
            t.wait_ge(sem["dve"], M["p1_msk"])
            t.matmul(PD[0:1, 0:64], one8f[:, :], scr8[0:8, 0:64], start=True, stop=True)
            t.matmul(PD[0:1, 64:128], one8f[:, :], scr8[0:8, 64:128], start=True, stop=True)
            m = t.matmul(PD[0:1, 128:136], one8f[:, :], scr8[0:8, 128:136], start=True, stop=True)
            M["p1_row"] = inc(m, "pe")

        @block.vector
        def _(v):
            v.wait_ge(sem["pe"], M["p1_row"])
            v.memset(crow[:, :], 0.0)
            v.tensor_scalar(crow[0:1, 8:72], PD[0:1, 0:64], -1.0, None, ALU.mult)
            v.tensor_scalar(crow[0:1, 72:136], PD[0:1, 64:128], -1.0, None, ALU.mult)
            m = v.tensor_copy(scr8[0:1, 0:8], PD[0:1, 128:136])
            M["p1_m2"] = inc(m, "dve")
        free["PD"] = ("dve", M["p1_m2"])

        @block.tensor
        def _(t):
            t.wait_ge(sem["dve"], M["p1_m2"])
            t.matmul(PE_[:, 0:8], onerf[:, :], scr8[0:1, 0:8], start=True, stop=True)
            t.matmul(PE_[:, 8:16], onerf[:, :], cenr[:, :], start=True, stop=True)
            m = t.matmul(PE_[:, 16:24], onerf[:, :], slpr[:, :], start=True, stop=True)
            M["p1_bc"] = inc(m, "pe")

        @block.vector
        def _(v):
            v.wait_ge(sem["pe"], M["p1_bc"])
            v.tensor_copy(m2bc[:, :], PE_[:, 0:8])
            v.tensor_copy(cenbc[:, :], PE_[:, 8:16])
            m = v.tensor_copy(slpbc[:, :], PE_[:, 16:24])
            M["p1_done"] = inc(m, "dve")
        free["PE"] = ("dve", M["p1_done"])

        if stage < 4:
            @block.gpsimd
            def _(g):
                g.wait_ge(sem["dve"], M["p1_done"])
                inc(g.dma_start(out=dbg["d_crow"], in_=crow[:, :]), "st", 16)

            @block.sync
            def _(s_):
                s_.wait_ge(sem["dve"], M["p1_done"])
                inc(s_.dma_start(out=dbg["d_m2bc"], in_=m2bc[:, :]), "st", 16)
                inc(s_.dma_start(out=dbg["d_cen"], in_=cenbc[:, :]), "st", 16)

        if stage >= 2:
            bctx = ctx.enter_context(ExitStack())
            EB = bctx.enter_context
            wsl = {w: EB(nc.sbuf_tensor(f"wsl_{w}", [128, NCH * 256], BF16)) for w in "qkv"}
            qk = EB(nc.sbuf_tensor("qk", [128, 4 * S], BF16))
            vsb = EB(nc.sbuf_tensor("vsb", [128, NT * 256], BF16))
            rotb = EB(nc.sbuf_tensor("rotb", [128, S], BF16))
            foldb = EB(nc.sbuf_tensor("foldb", [128, S], BF16))
            pt = [EB(nc.sbuf_tensor(f"pt{i}", [128, 4096], BF16)) for i in range(2)]
            oth = EB(nc.sbuf_tensor("oth", [128, S], F32))
            otn = [EB(nc.sbuf_tensor(f"otn{i}", [128, S], BF16)) for i in range(2)]
            rcol = EB(nc.sbuf_tensor("rcol", [128, 16], F32))
            rz16 = EB(nc.sbuf_tensor("rz16", [16, 128], F32))
            nsh = EB(nc.sbuf_tensor("nsh", [128, 2], F32))
            sml = EB(nc.sbuf_tensor("sml", [128, 8], F32))
            smr = EB(nc.sbuf_tensor("smr", [1, 16], F32))
            smb = EB(nc.sbuf_tensor("smb", [128, 2], BF16))

            ngroups = NG if stage >= 3 else 1

            @block.gpsimd
            def _(g):
                for w, wd in (("q", wq_d), ("k", wk_d), ("v", wv_d)):
                    m = g.dma_start(out=wsl[w][:, :].rearrange("p (c f) -> p c f", c=NCH),
                                    in_=wd[:, 0:G * 128].rearrange("(c p) f -> p c f", p=128))
                    inc(m, "dG", 16)
                M["wsl_0"] = C["dG"]

            for gi in range(ngroups):
                h0 = G * gi

                act_base = C["act"]
                ev_mark = lambda pp: act_base + pp + 1

                @block.tensor
                def _(t, gi=gi, h0=h0, ev_mark=ev_mark):
                    t.wait_ge(sem["dG"], M[f"wsl_{gi}"])
                    if gi == 0:
                        t.wait_ge(sem["dG"], M["p0_gp"])
                        t.wait_ge(sem["dS"], M["p0_sy"])
                        t.wait_ge(sem["dve"], M["p1_done"])
                    pp = 0
                    for hl in range(G):
                        for wname, brow in (("q", bqr), ("k", bkr)):
                            for p in range(4):
                                ps = PA if pp % 2 == 0 else PB
                                if pp >= 2:
                                    t.wait_ge(sem["act"], ev_mark(pp - 2))
                                else:
                                    wait_free(t, "PA" if pp % 2 == 0 else "PB")
                                for c in range(NCH):
                                    t.matmul(ps[:, 0:512],
                                             wsl[wname][:, c * 256 + hl * 128: c * 256 + hl * 128 + 128],
                                             xT[:, c * S + p * 512: c * S + p * 512 + 512],
                                             start=(c == 0), stop=False)
                                m = t.matmul(ps[:, 0:512],
                                             brow[0:1, (h0 + hl) * 128:(h0 + hl) * 128 + 128],
                                             onerb[0:1, 0:512], start=False, stop=True)
                                M[f"qk_{gi}_{pp}"] = inc(m, "pe")
                                pp += 1

                @block.scalar
                def _(sc, gi=gi):
                    pp = 0
                    for hl in range(G):
                        for wname in ("q", "k"):
                            slot = (0 if wname == "q" else 2) + hl
                            for p in range(4):
                                ps = PA if pp % 2 == 0 else PB
                                sc.wait_ge(sem["pe"], M[f"qk_{gi}_{pp}"])
                                m = sc.activation(qk[:, slot * S + p * 512: slot * S + p * 512 + 512],
                                                  ps[:, 0:512], AF.Copy)
                                assert inc(m, "act") == ev_mark(pp)
                                pp += 1
                free["PA"] = ("act", ev_mark(14))
                free["PB"] = ("act", ev_mark(15))

                dve_vbase = C["dve"]
                vev_mark = lambda tt: dve_vbase + tt + 1

                @block.tensor
                def _(t, gi=gi, h0=h0, vev_mark=vev_mark):
                    wait_free(t, "PC")
                    wait_free(t, "PD")
                    for tt in range(NT):
                        ps = PC if tt % 2 == 0 else PD
                        if tt >= 2:
                            t.wait_ge(sem["dve"], vev_mark(tt - 2))
                        for c in range(NCH):
                            t.matmul(ps[:, 0:256],
                                     xT[:, c * S + tt * 128: c * S + tt * 128 + 128],
                                     wsl["v"][:, c * 256: c * 256 + 256],
                                     start=(c == 0), stop=False)
                        m = t.matmul(ps[:, 0:256], onerb[0:1, 0:128],
                                     bvr[0:1, h0 * 128: h0 * 128 + 256], start=False, stop=True)
                        M[f"v_{gi}_{tt}"] = inc(m, "pe")

                @block.vector
                def _(v, gi=gi, vev_mark=vev_mark):
                    for tt in range(NT):
                        ps = PC if tt % 2 == 0 else PD
                        v.wait_ge(sem["pe"], M[f"v_{gi}_{tt}"])
                        m = v.tensor_copy(vsb[:, tt * 256:(tt + 1) * 256], ps[:, 0:256])
                        M[f"vev_{gi}_{tt}"] = inc(m, "dve")
                        assert M[f"vev_{gi}_{tt}"] == vev_mark(tt)
                free["PC"] = ("dve", vev_mark(14))
                free["PD"] = ("dve", vev_mark(15))

                if gi + 1 < ngroups:
                    @block.gpsimd
                    def _(g, gi=gi):
                        g.wait_ge(sem["pe"], M[f"v_{gi}_{NT-1}"])
                        for w, wd in (("q", wq_d), ("k", wk_d), ("v", wv_d)):
                            c0 = (gi + 1) * G * 128
                            m = g.dma_start(out=wsl[w][:, :].rearrange("p (c f) -> p c f", c=NCH),
                                            in_=wd[:, c0:c0 + G * 128].rearrange("(c p) f -> p c f", p=128))
                            inc(m, "dG", 16)
                        M[f"wsl_{gi+1}"] = C["dG"]

                for hl in range(G):
                    for wi, wname in enumerate(("q", "k")):
                        slot = (0 if wname == "q" else 2) + hl
                        for half in range(2):
                            bank = "PA" if half == 0 else "PB"
                            ps = PA if half == 0 else PB

                            @block.tensor
                            def _(t, gi=gi, hl=hl, wi=wi, slot=slot, half=half, ps=ps, bank=bank):
                                wait_free(t, bank)
                                if wi == 0 and half == 0:
                                    pass
                                for j2 in range(2):
                                    o = half * 1024 + j2 * 512
                                    m = t.matmul(ps[:, j2 * 512:(j2 + 1) * 512], rtb[:, :],
                                                 qk[:, slot * S + o: slot * S + o + 512],
                                                 start=True, stop=True)
                                M[f"rot_{gi}_{hl}_{wi}_{half}"] = inc(m, "pe")

                            @block.scalar
                            def _(sc, gi=gi, hl=hl, wi=wi, half=half, ps=ps):
                                sc.wait_ge(sem["pe"], M[f"rot_{gi}_{hl}_{wi}_{half}"])
                                m = sc.activation(rotb[:, half * 1024:(half + 1) * 1024],
                                                  ps[:, 0:1024], AF.Copy)
                                M[f"roev_{gi}_{hl}_{wi}_{half}"] = inc(m, "act")
                            free[bank] = ("act", M[f"roev_{gi}_{hl}_{wi}_{half}"])

                            @block.vector
                            def _(v, gi=gi, hl=hl, wi=wi, slot=slot, half=half):
                                o = half * 1024
                                v.wait_ge(sem["act"], M[f"roev_{gi}_{hl}_{wi}_{half}"])
                                v.tensor_tensor(rotb[:, o:o + 1024], rotb[:, o:o + 1024],
                                                ropess[:, o:o + 1024], ALU.mult)
                                v.tensor_tensor(qk[:, slot * S + o: slot * S + o + 1024],
                                                qk[:, slot * S + o: slot * S + o + 1024],
                                                ropecs[:, o:o + 1024], ALU.mult)
                                m = v.tensor_tensor(qk[:, slot * S + o: slot * S + o + 1024],
                                                    qk[:, slot * S + o: slot * S + o + 1024],
                                                    rotb[:, o:o + 1024], ALU.add)
                                M[f"rope_{gi}_{hl}_{wi}_{half}"] = inc(m, "dve")

                for hl in range(G):
                    @block.vector
                    def _(v, gi=gi, hl=hl):
                        for wi in range(2):
                            for half in range(2):
                                v.wait_ge(sem["dve"], M[f"rope_{gi}_{hl}_{wi}_{half}"])
                        m = v.tensor_tensor(foldb[:, :], qk[:, hl * S:(hl + 1) * S],
                                            qk[:, hl * S:(hl + 1) * S], ALU.mult)
                        M[f"sqq_{gi}_{hl}"] = inc(m, "dve")

                    @block.tensor
                    def _(t, gi=gi, hl=hl):
                        t.wait_ge(sem["dve"], M[f"sqq_{gi}_{hl}"])
                        wait_free(t, "PF")
                        for qt in range(16):
                            m = t.matmul(PF[:, qt:qt + 1], foldb[:, qt * 128:(qt + 1) * 128],
                                         onecb[:, :], start=True, stop=True)
                        M[f"nq_{gi}_{hl}"] = inc(m, "pe")

                    @block.vector
                    def _(v, gi=gi, hl=hl):
                        v.wait_ge(sem["pe"], M[f"nq_{gi}_{hl}"])
                        v.reduce_max(sml[:, 0:1], PF[:, 0:16], axis=AX.X)
                        m = v.tensor_tensor(foldb[:, :], qk[:, (2 + hl) * S:(3 + hl) * S],
                                            qk[:, (2 + hl) * S:(3 + hl) * S], ALU.mult)
                        M[f"sqk_{gi}_{hl}"] = inc(m, "dve")
                        m = v.tensor_copy(smb[:, 0:1], sml[:, 0:1])
                        M[f"mq1_{gi}_{hl}"] = inc(m, "dve")

                    @block.tensor
                    def _(t, gi=gi, hl=hl):
                        t.wait_ge(sem["dve"], M[f"mq1_{gi}_{hl}"])
                        wait_free(t, "PD")
                        t.transpose(PDb[0:1, 0:128], smb[:, 0:1], idb[:, :])
                        t.wait_ge(sem["dve"], M[f"sqk_{gi}_{hl}"])
                        for qt in range(16):
                            m = t.matmul(PF[:, qt:qt + 1], foldb[:, qt * 128:(qt + 1) * 128],
                                         onecb[:, :], start=True, stop=True)
                        M[f"nk_{gi}_{hl}"] = inc(m, "pe")

                    @block.vector
                    def _(v, gi=gi, hl=hl):
                        v.wait_ge(sem["pe"], M[f"nk_{gi}_{hl}"])
                        v.reduce_max(sml[:, 1:2], PF[:, 0:16], axis=AX.X)
                        v.reduce_max(smr[0:1, 0:1], PDb[0:1, 0:128], axis=AX.X)
                        v.tensor_copy(sml[:, 3:4], sml[:, 0:1])
                        m = v.tensor_copy(smb[:, 1:2], sml[:, 1:2])
                        M[f"mk1_{gi}_{hl}"] = inc(m, "dve")

                    @block.tensor
                    def _(t, gi=gi, hl=hl):
                        t.wait_ge(sem["dve"], M[f"mk1_{gi}_{hl}"])
                        m = t.transpose(PDb[0:1, 128:256], smb[:, 1:2], idb[:, :])
                        M[f"tk_{gi}_{hl}"] = inc(m, "pe")

                    @block.vector
                    def _(v, gi=gi, hl=hl):
                        v.wait_ge(sem["pe"], M[f"tk_{gi}_{hl}"])
                        v.reduce_max(smr[0:1, 1:2], PDb[0:1, 128:256], axis=AX.X)
                        v.tensor_copy(sml[:, 4:5], sml[:, 0:1])
                        v.tensor_copy(sml[:, 4:5], sml[:, 1:2])
                        v.tensor_tensor(smr[0:1, 2:3], smr[0:1, 0:1], smr[0:1, 1:2], ALU.add)
                        v.tensor_copy(sml[:, 4:5], sml[:, 0:1])
                        v.tensor_copy(sml[:, 4:5], sml[:, 1:2])
                        m = v.tensor_scalar(smr[0:1, 3:4], smr[0:1, 2:3], -SCALE / 2.0, None, ALU.mult)
                        M[f"nsr_{gi}_{hl}"] = inc(m, "dve")
                    free["PD"] = ("dve", M[f"nsr_{gi}_{hl}"])
                    free["PF"] = ("dve", M[f"mk1_{gi}_{hl}"])

                    @block.tensor
                    def _(t, gi=gi, hl=hl):
                        t.wait_ge(sem["dve"], M[f"nsr_{gi}_{hl}"])
                        wait_free(t, "PE")
                        m = t.matmul(PE_[:, 256 + hl:257 + hl], onerf[:, :], smr[0:1, 3:4],
                                     start=True, stop=True)
                        M[f"nsb_{gi}_{hl}"] = inc(m, "pe")

                    @block.vector
                    def _(v, gi=gi, hl=hl):
                        v.wait_ge(sem["pe"], M[f"nsb_{gi}_{hl}"])
                        m = v.tensor_copy(nsh[:, hl:hl + 1], PE_[:, 256 + hl:257 + hl])
                        M[f"nsh_{gi}_{hl}"] = inc(m, "dve")
                    free["PE"] = ("dve", M[f"nsh_{gi}_{hl}"])

                for hl in range(G):
                    h = h0 + hl
                    act_hbase = C["act"]
                    exp_mark = lambda qp, kg: act_hbase + qp * 4 + kg + 1
                    exp_jobs = []

                    def emit_sg(qp):
                        it = f"{gi}_{hl}_{qp}"

                        @block.tensor
                        def _(t, qp=qp, it=it):
                            if qp == 0:
                                t.wait_ge(sem["dve"], M[f"nsh_{gi}_{hl}"])
                                if hl == 0:
                                    t.wait_ge(sem["dve"], M[f"vev_{gi}_{NT-1}"])
                                else:
                                    t.wait_ge(sem["dve"], M[f"rcp_{gi}_{hl-1}"])
                                wait_free(t, "PA")
                                wait_free(t, "PB")
                            for kg in range(4):
                                ps = PA if kg % 2 == 0 else PB
                                ei = qp * 4 + kg
                                if ei >= 2:
                                    t.wait_ge(sem["act"], exp_mark((ei - 2) // 4, (ei - 2) % 4))
                                for k4 in range(4):
                                    kt = kg * 4 + k4
                                    m = t.matmul(ps[:, k4 * 256:(k4 + 1) * 256],
                                                 qk[:, (2 + hl) * S + kt * 128: (2 + hl) * S + kt * 128 + 128],
                                                 qk[:, hl * S + qp * 256: hl * S + qp * 256 + 256],
                                                 start=True, stop=True)
                                M[f"sg_{it}_{kg}"] = inc(m, "pe")

                        exp_jobs.append((qp, it))

                    def emit_pv(qp):
                        it = f"{gi}_{hl}_{qp}"

                        @block.tensor
                        def _(t, qp=qp, it=it):
                            buf = pt[qp % 2]
                            t.wait_ge(sem["act"], exp_mark(qp, 3))
                            if qp >= 2:
                                t.wait_ge(sem["dve"], M[f"otev_{gi}_{hl}_{qp-2}"])
                            else:
                                wait_free(t, "PC" if qp % 2 == 0 else "PD")
                            ps = PC if qp % 2 == 0 else PD
                            for kt in range(16):
                                m = t.matmul(ps[:, 0:256],
                                             vsb[:, kt * 256 + hl * 128: kt * 256 + hl * 128 + 128],
                                             buf[:, kt * 256:(kt + 1) * 256],
                                             start=(kt == 0), stop=(kt == 15))
                            M[f"pv_{it}"] = inc(m, "pe")
                            t.wait_ge(sem["dve"], M[f"f1_{it}"])
                            if hl == 0 and qp == 0:
                                wait_free(t, "PF")
                            for j in range(2):
                                m = t.matmul(PF[:, qp * 2 + j: qp * 2 + j + 1],
                                             foldb[:, j * 128:(j + 1) * 128], onecb[:, :],
                                             start=True, stop=True)
                            M[f"z_{it}"] = inc(m, "pe")

                        @block.vector
                        def _(v, qp=qp, it=it):
                            v.wait_ge(sem["pe"], M[f"pv_{it}"])
                            ps = PC if qp % 2 == 0 else PD
                            m = v.tensor_copy(oth[:, qp * 256:(qp + 1) * 256], ps[:, 0:256])
                            M[f"otev_{it}"] = inc(m, "dve")

                    def emit_fold(qp):
                        it = f"{gi}_{hl}_{qp}"

                        @block.vector
                        def _(v, qp=qp, it=it):
                            buf = pt[qp % 2]
                            v.wait_ge(sem["act"], exp_mark(qp, 3))
                            if qp > 0:
                                v.wait_ge(sem["pe"], M[f"z_{gi}_{hl}_{qp-1}"])
                            v.tensor_tensor(foldb[:, 0:2048], buf[:, 0:2048], buf[:, 2048:4096], ALU.add)
                            v.tensor_tensor(foldb[:, 0:1024], foldb[:, 0:1024], foldb[:, 1024:2048], ALU.add)
                            v.tensor_tensor(foldb[:, 0:512], foldb[:, 0:512], foldb[:, 512:1024], ALU.add)
                            m = v.tensor_tensor(foldb[:, 0:256], foldb[:, 0:256], foldb[:, 256:512], ALU.add)
                            M[f"f1_{it}"] = inc(m, "dve")

                    def emit_exps():
                        @block.scalar
                        def _(sc):
                            for qp, it in exp_jobs:
                                buf = pt[qp % 2]
                                for kg in range(4):
                                    ps = PA if kg % 2 == 0 else PB
                                    sc.wait_ge(sem["pe"], M[f"sg_{it}_{kg}"])
                                    if qp >= 2 and kg == 0:
                                        sc.wait_ge(sem["pe"], M[f"pv_{gi}_{hl}_{qp-2}"])
                                        sc.wait_ge(sem["dve"], M[f"f1_{gi}_{hl}_{qp-2}"])
                                    m = sc.activation(buf[:, kg * 1024:(kg + 1) * 1024], ps[:, 0:1024],
                                                      AF.Exp, bias=nsh[:, hl:hl + 1], scale=SCALE)
                                    assert inc(m, "act") == exp_mark(qp, kg)
                            exp_jobs.clear()

                    if LAG:
                        for qp in range(NQP + 1):
                            if qp < NQP:
                                emit_sg(qp)
                            if qp == NQP - 1:
                                emit_exps()
                            if qp >= 1:
                                emit_pv(qp - 1)
                            if qp < NQP:
                                emit_fold(qp)
                        emit_exps()
                    else:
                        for qp in range(NQP):
                            emit_sg(qp)
                            emit_exps()
                            emit_fold(qp)
                            emit_pv(qp)
                    free["PC"] = ("dve", M[f"otev_{gi}_{hl}_{NQP-2}"])
                    free["PD"] = ("dve", M[f"otev_{gi}_{hl}_{NQP-1}"])
                    free["PA"] = ("act", exp_mark(NQP - 1, 3))
                    free["PB"] = ("act", exp_mark(NQP - 1, 3))

                    @block.vector
                    def _(v, gi=gi, hl=hl):
                        v.wait_ge(sem["pe"], M[f"z_{gi}_{hl}_{NQP-1}"])
                        m = v.reciprocal(rcol[:, :], PF[:, 0:16])
                        M[f"rcp_{gi}_{hl}"] = inc(m, "dve")
                    free["PF"] = ("dve", M[f"rcp_{gi}_{hl}"])

                    @block.tensor
                    def _(t, gi=gi, hl=hl):
                        t.wait_ge(sem["dve"], M[f"rcp_{gi}_{hl}"])
                        wait_free(t, "PE")
                        m = t.transpose(PE_[0:16, 0:128], rcol[:, :], idf[:, :])
                        M[f"rct_{gi}_{hl}"] = inc(m, "pe")

                    @block.vector
                    def _(v, gi=gi, hl=hl):
                        v.wait_ge(sem["pe"], M[f"rct_{gi}_{hl}"])
                        m = v.tensor_copy(rz16[:, :], PE_[0:16, 0:128])
                        M[f"rz_{gi}_{hl}"] = inc(m, "dve")
                    free["PE"] = ("dve", M[f"rz_{gi}_{hl}"])

                    for qp in range(NQP):
                        @block.tensor
                        def _(t, gi=gi, hl=hl, qp=qp):
                            t.wait_ge(sem["dve"], M[f"rz_{gi}_{hl}"])
                            bank = "PE" if qp % 2 == 0 else "PF"
                            ps = PE_ if qp % 2 == 0 else PF
                            if qp >= 2:
                                t.wait_ge(sem["dve"], M[f"nrm_{gi}_{hl}_{qp-2}"])
                            else:
                                wait_free(t, bank)
                            for j in range(2):
                                jj = qp * 2 + j
                                m = t.matmul(ps[:, j * 128:(j + 1) * 128],
                                             selsb[:, jj * 128:(jj + 1) * 128],
                                             rz16[:, :], start=True, stop=True)
                            M[f"bmm_{gi}_{hl}_{qp}"] = inc(m, "pe")

                        @block.vector
                        def _(v, gi=gi, hl=hl, qp=qp, h=h):
                            v.wait_ge(sem["pe"], M[f"bmm_{gi}_{hl}_{qp}"])
                            if qp == 0 and h >= 2:
                                v.wait_ge(sem["dS"], M[f"ots_{h-2}"])
                            ps = PE_ if qp % 2 == 0 else PF
                            m = v.tensor_tensor(otn[hl][:, qp * 256:(qp + 1) * 256],
                                                oth[:, qp * 256:(qp + 1) * 256],
                                                ps[:, 0:256], ALU.mult)
                            M[f"nrm_{gi}_{hl}_{qp}"] = inc(m, "dve")
                    free["PE"] = ("dve", M[f"nrm_{gi}_{hl}_{NQP-2}"])
                    free["PF"] = ("dve", M[f"nrm_{gi}_{hl}_{NQP-1}"])

                    @block.sync
                    def _(s_, gi=gi, hl=hl, h=h):
                        for qp in range(NQP):
                            s_.wait_ge(sem["dve"], M[f"nrm_{gi}_{hl}_{qp}"])
                        m = s_.dma_start(out=ot_hbm[h, :, :], in_=otn[hl][:, :])
                        M[f"ots_{h}"] = inc(m, "dS", 16)

            if stage == 2:
                @block.gpsimd
                def _(g):
                    g.wait_ge(sem["dve"], M["rope_0_0_0_1"])
                    inc(g.dma_start(out=dbg["d_q0"], in_=qk[:, 0:S]), "st", 16)
                    g.wait_ge(sem["dve"], M["rope_0_0_1_1"])
                    inc(g.dma_start(out=dbg["d_k0"], in_=qk[:, 2 * S:3 * S]), "st", 16)
                    g.wait_ge(sem["dve"], M["vev_0_0"])
                    inc(g.dma_start(out=dbg["d_v0"], in_=vsb[:, 0:256]), "st", 16)
            if stage in (2, 3):
                @block.gpsimd
                def _(g):
                    g.wait_ge(sem["dS"], M["ots_0"])
                    inc(g.dma_start(out=dbg["d_ot0"], in_=ot_hbm[0, :, :]), "st", 16)
                    g.wait_ge(sem["dS"], M[f"ots_{2*ngroups-1}"])
                    inc(g.dma_start(out=dbg["d_ot15"], in_=ot_hbm[2*ngroups-1, :, :]), "st", 16)

        if stage >= 4:
            bctx.close()

            @block.gpsimd
            def _(g):
                g.wait_ge(sem["pe"], M[f"v_{NG-1}_{NT-1}"])
                m = g.dma_start(out=xT[:, :].rearrange("p (c f) -> p c f", c=NCH),
                                in_=wo_d.rearrange("(c p) f -> p c f", p=128))
                M["wo"] = inc(m, "dG", 16)

            cctx = ctx.enter_context(ExitStack())
            EC = cctx.enter_context
            otc = [EC(nc.sbuf_tensor(f"otc{i}", [128, S], BF16)) for i in range(2)]
            xtl = [EC(nc.sbuf_tensor(f"xtl{i}", [128, D], F32)) for i in range(2)]
            z1 = EC(nc.sbuf_tensor("z1", [128, D], F32))
            x1 = [EC(nc.sbuf_tensor(f"x1_{i}", [128, D], F32)) for i in range(3)]
            x1b = [EC(nc.sbuf_tensor(f"x1b{i}", [128, D], BF16)) for i in range(3)]
            x1t = [EC(nc.sbuf_tensor("x1t0", [128, D], BF16))]
            sqs = EC(nc.sbuf_tensor("sqs", [128, 512], F32))
            z2 = EC(nc.sbuf_tensor("z2", [128, D], F32))
            outb = [EC(nc.sbuf_tensor("outb0", [128, D], F32))]
            ts1 = EC(nc.sbuf_tensor("ts1", [128, 512], F32))
            stat = [EC(nc.sbuf_tensor(f"stat{i}", [128, 76], F32)) for i in range(3)]
            gsc = [EC(nc.sbuf_tensor(f"gsc{i}", [128, 8], F32)) for i in range(3)]
            prs = EC(nc.sbuf_tensor("prs", [128, 104], F32))
            pr2s = EC(nc.sbuf_tensor("pr2s", [128, 64], F32))
            prr = EC(nc.sbuf_tensor("prr", [128, 64], F32))
            mmbv = [EC(nc.sbuf_tensor("mmb0", [128, 64], BF16))]
            dbgg = EC(nc.sbuf_tensor("dbgg", [128, NT * 16], F32)) if DBG4 else None
            ebf = [EC(nc.sbuf_tensor("ebf0", [128, 8], BF16))]
            ets = [EC(nc.sbuf_tensor("ets0", [8, 128], BF16))]
            mts = [EC(nc.sbuf_tensor("mts0", [64, 128], BF16))]



            def newton_rsqrt(v, st_, vcol, ycol, tmp, j):
                A_, B_ = 0.430762, 0.476801
                sp = lambda: v.tensor_copy(st_[:, 33:34], st_[:, 8:9])
                v.tensor_scalar_min(st_[:, tmp[0]:tmp[0] + 1], st_[:, vcol:vcol + 1], 7.5)
                sp()
                sp()
                v.reciprocal(st_[:, tmp[1]:tmp[1] + 1], st_[:, tmp[0]:tmp[0] + 1])
                sp()
                sp()
                v.tensor_scalar(st_[:, ycol:ycol + 1], st_[:, tmp[1]:tmp[1] + 1],
                                -A_ / B_, B_, ALU.subtract, ALU.mult)
                for _ in range(5):
                    sp()
                    v.tensor_tensor(st_[:, tmp[2]:tmp[2] + 1], st_[:, ycol:ycol + 1],
                                    st_[:, ycol:ycol + 1], ALU.mult)
                    sp()
                    v.tensor_tensor(st_[:, tmp[3]:tmp[3] + 1], st_[:, tmp[2]:tmp[2] + 1],
                                    st_[:, vcol:vcol + 1], ALU.mult)
                    sp()
                    v.tensor_scalar(st_[:, tmp[2]:tmp[2] + 1], st_[:, tmp[3]:tmp[3] + 1],
                                    3.0, -0.5, ALU.subtract, ALU.mult)
                    sp()
                    m = v.tensor_tensor(st_[:, ycol:ycol + 1], st_[:, ycol:ycol + 1],
                                        st_[:, tmp[2]:tmp[2] + 1], ALU.mult)
                return m
            def front(j):
                bi = j % 2
                ti = j % 3
                snap_ser()

                @block.sync
                def _(s_, j=j, bi=bi):
                    ser(s_)
                    if j == 0:
                        jjs = [0, 1]
                        for h in range(NH):
                            s_.wait_ge(sem["dS"], M[f"ots_{h}"])
                    elif j + 1 < NT:
                        jjs = [j + 1]
                        s_.wait_ge(sem["pe"], M[f"y_{j-1}_3"])
                        s_.wait_ge(sem["dve"], M[f"z1_{j-1}_3"])
                    else:
                        jjs = []
                    for jj in jjs:
                        bj = jj % 2
                        m = s_.dma_start(out=xtl[bj][:, :], in_=x_tm[jj * 128:(jj + 1) * 128, :])
                        inc(m, "dS", 16)
                        m = s_.dma_start(out=otc[bj][:, :].rearrange("p (h t) -> p h t", h=NH),
                                         in_=ot_hbm[:, :, jj * 128:(jj + 1) * 128].rearrange("h p t -> p h t"))
                        M[f"cin_{jj}"] = inc(m, "dS", 16)

                dve_fb = C["dve"]
                z1_mark = lambda p: dve_fb + p + 1

                @block.tensor
                def _(t, j=j, bi=bi, ti=j % 3, z1_mark=z1_mark):
                    ser(t)
                    t.wait_ge(sem["dS"], M[f"cin_{j}"])
                    if j == 0:
                        t.wait_ge(sem["dG"], M["wo"])
                        wait_free(t, "PA")
                        wait_free(t, "PB")
                    for p in range(4):
                        ps = PA if p % 2 == 0 else PB
                        if p >= 2:
                            t.wait_ge(sem["dve"], z1_mark(p - 2))
                        elif j > 0:
                            t.wait_ge(sem["dve"], M[f"z1_{j-1}_{p+2}"])
                        for c in range(NCH):
                            t.matmul(ps[:, 0:512], otc[bi][:, c * 128:(c + 1) * 128],
                                     xT[:, c * S + p * 512: c * S + p * 512 + 512],
                                     start=(c == 0), stop=False)
                        m = t.matmul(ps[:, 0:512], onerb[0:1, 0:128],
                                     bor[0:1, p * 512:(p + 1) * 512], start=False, stop=True)
                        M[f"y_{j}_{p}"] = inc(m, "pe")

                @block.vector
                def _(v, j=j, bi=bi, ti=j % 3, z1_mark=z1_mark):
                    ser(v)
                    st_ = stat[ti]
                    for p in range(4):
                        ps = PA if p % 2 == 0 else PB
                        v.wait_ge(sem["pe"], M[f"y_{j}_{p}"])
                        if p == 0 and j > 0:
                            v.wait_ge(sem["act"], M[f"sq1_{j-1}_3"])
                        m = v.scalar_tensor_tensor(z1[:, p * 512:(p + 1) * 512],
                                                   xtl[bi][:, p * 512:(p + 1) * 512], 1.0,
                                                   ps[:, 0:512], ALU.mult, ALU.add,
                                                   accum_out=st_[:, p:p + 1])
                        M[f"z1_{j}_{p}"] = assert_eq(inc(m, "dve"), z1_mark(p))

                @block.scalar
                def _(sc, j=j, bi=bi, ti=j % 3):
                    ser(sc)
                    st_ = stat[ti]
                    for p in range(4):
                        sc.wait_ge(sem["dve"], M[f"z1_{j}_{p}"])
                        m = sc.activation(sqs[:, :], z1[:, p * 512:(p + 1) * 512], AF.Square,
                                          accum_out=st_[:, 4 + p:5 + p])
                        M[f"sq1_{j}_{p}"] = inc(m, "act")

            def front_tail(j):
                bi = j % 2
                ti = j % 3
                snap_ser()

                @block.vector
                def _(v, j=j, bi=bi, ti=j % 3):
                    ser(v)
                    st_ = stat[ti]
                    v.wait_ge(sem["act"], M[f"sq1_{j}_3"])
                    v.reduce_sum(st_[:, 8:9], st_[:, 0:4], axis=AX.X)
                    v.reduce_sum(st_[:, 9:10], st_[:, 4:8], axis=AX.X)
                    v.tensor_scalar(st_[:, 10:11], st_[:, 8:9], 1.0 / D, None, ALU.mult)
                    v.tensor_scalar(st_[:, 11:12], st_[:, 9:10], 1.0 / D, None, ALU.mult)
                    v.tensor_tensor(st_[:, 12:13], st_[:, 10:11], st_[:, 10:11], ALU.mult)
                    v.tensor_copy(st_[:, 33:34], st_[:, 8:9])
                    v.tensor_copy(st_[:, 33:34], st_[:, 9:10])
                    v.tensor_tensor(st_[:, 13:14], st_[:, 11:12], st_[:, 12:13], ALU.subtract)
                    v.tensor_copy(st_[:, 33:34], st_[:, 8:9])
                    v.tensor_copy(st_[:, 33:34], st_[:, 9:10])
                    m = v.tensor_scalar(st_[:, 14:15], st_[:, 13:14], EPS, None, ALU.add)
                    v.tensor_copy(st_[:, 33:34], st_[:, 8:9])
                    v.tensor_copy(st_[:, 33:34], st_[:, 9:10])
                    M[f"st1_{j}"] = inc(m, "dve")

                @block.vector
                def _(v, j=j, bi=bi, ti=j % 3):
                    ser(v)
                    st_ = stat[ti]
                    m = newton_rsqrt(v, st_, 14, 16, (41, 42, 43, 15), j)
                    M[f"rs1_{j}"] = inc(m, "dve")
                    v.tensor_copy(st_[:, 33:34], st_[:, 8:9])
                    v.tensor_tensor(st_[:, 17:18], st_[:, 16:17], st_[:, 16:17], ALU.mult)
                    v.tensor_scalar(x1[ti][:, :], z1[:, :], st_[:, 10:11], st_[:, 16:17],
                                    ALU.subtract, ALU.mult)
                    v.tensor_scalar(st_[:, 19:20], st_[:, 17:18], -EPS * float(D), None, ALU.mult)
                    v.tensor_copy(st_[:, 33:34], st_[:, 14:15])
                    v.tensor_copy(st_[:, 33:34], st_[:, 17:18])
                    v.tensor_scalar(st_[:, 18:19], st_[:, 19:20], float(D), None, ALU.add)
                    v.tensor_copy(st_[:, 33:34], st_[:, 14:15])
                    v.tensor_copy(st_[:, 33:34], st_[:, 17:18])
                    m = v.tensor_scalar(st_[:, 44:52], m2bc[:, :], st_[:, 18:19], None, ALU.add)
                    if DBG4:
                        v.tensor_copy(dbgg[:, j * 16 + 12:j * 16 + 13], st_[:, 16:17])
                        v.tensor_copy(dbgg[:, j * 16 + 13:j * 16 + 14], st_[:, 17:18])
                        v.tensor_copy(dbgg[:, j * 16 + 14:j * 16 + 15], st_[:, 14:15])
                        v.tensor_copy(dbgg[:, j * 16 + 15:j * 16 + 16], st_[:, 18:19])
                    M[f"x1_{j}"] = inc(m, "dve")

                @block.scalar
                def _(sc, j=j, bi=bi, ti=j % 3):
                    ser(sc)
                    sc.wait_ge(sem["dve"], M[f"x1_{j}"])
                    if j >= 3:
                        sc.wait_ge(sem["pe"], M[f"x1tB_{j-3}"])
                    m = sc.activation(x1b[ti][:, :], x1[ti][:, :], AF.Copy)
                    M[f"x1b_{j}"] = inc(m, "act")


            def back1(j):
                bi = j % 2
                ti = j % 3
                snap_ser()

                @block.tensor
                def _(t, j=j, bi=bi, ti=j % 3):
                    ser(t)
                    t.wait_ge(sem["act"], M[f"x1b_{j}"])
                    if j >= 1:
                        t.wait_ge(sem["dve"], M[f"x1t_{j-1}"])
                    else:
                        wait_free(t, "PD")
                    for c in range(8):
                        m = t.transpose(PDb[:, c * 128:(c + 1) * 128],
                                        x1b[ti][:, c * 128:(c + 1) * 128], idb[:, :])
                    M[f"x1tA_{j}"] = inc(m, "pe")

                @block.vector
                def _(v, j=j, bi=bi, ti=j % 3):
                    ser(v)
                    v.wait_ge(sem["pe"], M[f"x1tA_{j}"])
                    m = v.tensor_copy(x1t[0][:, 0:1024], PDb[:, 0:1024])
                    M[f"x1eA_{j}"] = inc(m, "dve")

                @block.tensor
                def _(t, j=j, bi=bi, ti=j % 3):
                    ser(t)
                    t.wait_ge(sem["dve"], M[f"x1eA_{j}"])
                    for c in range(8, NCH):
                        m = t.transpose(PDb[:, (c - 8) * 128:(c - 7) * 128],
                                        x1b[ti][:, c * 128:(c + 1) * 128], idb[:, :])
                    M[f"x1tB_{j}"] = inc(m, "pe")

                @block.vector
                def _(v, j=j, bi=bi, ti=j % 3):
                    ser(v)
                    v.wait_ge(sem["pe"], M[f"x1tB_{j}"])
                    m = v.tensor_copy(x1t[0][:, 1024:2048], PDb[:, 0:1024])
                    M[f"x1t_{j}"] = inc(m, "dve")

                @block.tensor
                def _(t, j=j, bi=bi, ti=j % 3):
                    ser(t)
                    t.wait_ge(sem["dve"], M[f"x1t_{j}"])
                    if j >= 1:
                        t.wait_ge(sem["dve"], M[f"gate_{j-1}"])
                    else:
                        wait_free(t, "PC")
                    for c in range(NCH):
                        t.matmul(PC[:, 0:136], x1t[0][:, c * 128:(c + 1) * 128],
                                 ffnw[:, c * 136:(c + 1) * 136], start=(c == 0), stop=False)
                    m = t.matmul(PC[:, 0:136], onerb[0:1, 0:128], crow[0:1, :],
                                 start=False, stop=True)
                    M[f"ffn_{j}"] = inc(m, "pe")

                @block.vector
                def _(v, j=j):
                    ser(v)
                    v.wait_ge(sem["pe"], M[f"ffn_{j}"])
                    v.tensor_copy(prr[:, :], PC[:, 8:72])
                    v.tensor_copy(prs[:, 96:104], m2bc[:, :])
                    v.tensor_copy(prs[:, 96:104], cenbc[:, :])
                    m = v.scalar_tensor_tensor(pr2s[:, :], prr[:, :], 1.0, prr[:, :],
                                               ALU.mult, ALU.mult)
                    M[f"pr2_{j}"] = inc(m, "dve")

                @block.vector
                def _(v, j=j, bi=bi, ti=j % 3):
                    ser(v)
                    st_ = stat[ti]
                    gs = gsc[ti]
                    v.wait_ge(sem["dve"], M[f"pr2_{j}"])
                    v.scalar_tensor_tensor(prs[:, 8:16], PC[:, 0:8], -2.0, st_[:, 44:52],
                                           ALU.mult, ALU.add)
                    for n in range(8):
                        v.reduce_sum(prs[:, 40 + n:41 + n], pr2s[:, n * 8:(n + 1) * 8], axis=AX.X)
                    v.reciprocal(prs[:, 16:24], prs[:, 8:16])
                    v.tensor_copy(prs[:, 96:104], m2bc[:, :])
                    v.tensor_copy(prs[:, 96:104], cenbc[:, :])
                    v.tensor_tensor(prs[:, 24:32], prs[:, 40:48], prs[:, 16:24], ALU.mult)
                    v.tensor_copy(prs[:, 96:104], m2bc[:, :])
                    v.tensor_copy(prs[:, 96:104], cenbc[:, :])
                    v.tensor_tensor(prs[:, 32:40], prs[:, 24:32], cenbc[:, :], ALU.subtract)
                    v.tensor_copy(prs[:, 96:104], m2bc[:, :])
                    v.tensor_copy(prs[:, 96:104], cenbc[:, :])
                    m = v.tensor_tensor(prs[:, 48:56], prs[:, 32:40], slpbc[:, :], ALU.mult)
                    M[f"arg_{j}"] = inc(m, "dve")

                @block.scalar
                def _(sc, j=j, bi=bi, ti=j % 3):
                    ser(sc)
                    gs = gsc[ti]
                    sc.wait_ge(sem["dve"], M[f"arg_{j}"])
                    m = sc.activation(gs[:, :], prs[:, 48:56], AF.Sigmoid)
                    M[f"eneg_{j}"] = inc(m, "act")

                @block.vector
                def _(v, j=j, bi=bi, ti=j % 3):
                    ser(v)
                    st_ = stat[ti]
                    gs = gsc[ti]
                    v.wait_ge(sem["act"], M[f"eneg_{j}"])
                    if j >= 1:
                        v.wait_ge(sem["pe"], M[f"ett_{j-1}"])
                    v.reduce_max(st_[:, 30:31], gs[:, :], axis=AX.X)
                    v.tensor_copy(prs[:, 96:104], m2bc[:, :])
                    v.tensor_copy(prs[:, 96:104], cenbc[:, :])
                    v.tensor_scalar(st_[:, 31:32], st_[:, 30:31], 2.0, -1.0, ALU.subtract, ALU.mult)
                    v.tensor_scalar(prs[:, 72:80], gs[:, :], st_[:, 30:31], None, ALU.is_ge)
                    v.tensor_copy(prs[:, 96:104], m2bc[:, :])
                    v.tensor_copy(prs[:, 96:104], cenbc[:, :])
                    v.tensor_tensor(prs[:, 80:88], prs[:, 72:80], wties[:, :], ALU.mult)
                    v.tensor_copy(prs[:, 96:104], m2bc[:, :])
                    v.tensor_copy(prs[:, 96:104], cenbc[:, :])
                    v.reduce_max(st_[:, 32:33], prs[:, 80:88], axis=AX.X)
                    v.tensor_copy(st_[:, 33:34], st_[:, 30:31])
                    v.tensor_copy(st_[:, 33:34], st_[:, 31:32])
                    v.tensor_scalar(prs[:, 88:96], prs[:, 80:88], st_[:, 32:33], None, ALU.is_ge)
                    v.tensor_copy(prs[:, 96:104], m2bc[:, :])
                    v.tensor_copy(prs[:, 96:104], cenbc[:, :])
                    v.tensor_copy(ebf[0][:, :], prs[:, 88:96])
                    if DBG4:
                        v.tensor_copy(dbgg[:, j * 16:j * 16 + 8], gs[:, :])
                        v.tensor_copy(dbgg[:, j * 16 + 8:j * 16 + 9], st_[:, 30:31])
                        v.tensor_copy(dbgg[:, j * 16 + 9:j * 16 + 10], st_[:, 18:19])
                        v.tensor_copy(dbgg[:, j * 16 + 10:j * 16 + 11], prs[:, 8:9])
                        v.tensor_copy(dbgg[:, j * 16 + 11:j * 16 + 12], prs[:, 88:89])
                    last = None
                    for n in range(8):
                        last = v.tensor_scalar(mmbv[0][:, n * 8:(n + 1) * 8],
                                               PC[:, 72 + n * 8: 80 + n * 8],
                                               prs[:, 88 + n:89 + n], None, ALU.mult)
                    M[f"gate_{j}"] = inc(last, "dve")

            def back2a(j):
                bi = j % 2
                ti = j % 3
                snap_ser()

                @block.tensor
                def _(t, j=j, bi=bi, ti=j % 3):
                    ser(t)
                    t.wait_ge(sem["dve"], M[f"gate_{j}"])
                    if j >= 1:
                        t.wait_ge(sem["dve"], M[f"etv_{j-1}"])
                    t.transpose(PBb[0:8, 1024:1152], ebf[0][:, :], idb[:, :])
                    m = t.transpose(PBb[0:64, 1152:1280], mmbv[0][:, :], idb[:, :])
                    M[f"ett_{j}"] = inc(m, "pe")

                @block.vector
                def _(v, j=j, bi=bi, ti=j % 3):
                    ser(v)
                    v.wait_ge(sem["pe"], M[f"ett_{j}"])
                    v.tensor_copy(ets[0][:, :], PBb[0:8, 1024:1152])
                    m = v.tensor_copy(mts[0][:, :], PBb[0:64, 1152:1280])
                    M[f"etv_{j}"] = inc(m, "dve")

                dve_zb = C["dve"]
                z2_mark = lambda p: dve_zb + p + 1

                @block.tensor
                def _(t, j=j, bi=bi, ti=j % 3, z2_mark=z2_mark):
                    ser(t)
                    t.wait_ge(sem["dve"], M[f"etv_{j}"])
                    for p in range(4):
                        bank = "PE" if p % 2 == 0 else "PF"
                        ps = PE_ if p % 2 == 0 else PF
                        if p >= 2:
                            t.wait_ge(sem["dve"], z2_mark(p - 2))
                        elif j > 0:
                            t.wait_ge(sem["dve"], M[f"z2_{j-1}_{p+2}"])
                        else:
                            wait_free(t, bank)
                        t.matmul(ps[:, 0:512], mts[0][:, :], dgn[:, p * 512:(p + 1) * 512],
                                 start=True, stop=False)
                        m = t.matmul(ps[:, 0:512], ets[0][:, :], bwsb[:, p * 512:(p + 1) * 512],
                                     start=False, stop=True)
                        M[f"mbx_{j}_{p}"] = inc(m, "pe")

                @block.vector
                def _(v, j=j, bi=bi, ti=j % 3, z2_mark=z2_mark):
                    ser(v)
                    st_ = stat[ti]
                    for p in range(4):
                        ps = PE_ if p % 2 == 0 else PF
                        v.wait_ge(sem["pe"], M[f"mbx_{j}_{p}"])
                        if p == 0 and j > 0:
                            v.wait_ge(sem["act"], M[f"sq2_{j-1}_3"])
                        v.tensor_scalar(ts1[:, :], x1[ti][:, p * 512:(p + 1) * 512],
                                        st_[:, 31:32], None, ALU.mult)
                        m = v.scalar_tensor_tensor(z2[:, p * 512:(p + 1) * 512], ps[:, 0:512],
                                                   st_[:, 30:31], ts1[:, :], ALU.mult, ALU.add,
                                                   accum_out=st_[:, 20 + p:21 + p])
                        M[f"z2_{j}_{p}"] = inc(m, "dve")
                        assert M[f"z2_{j}_{p}"] == z2_mark(p), (M[f"z2_{j}_{p}"], z2_mark(p))

                @block.scalar
                def _(sc, j=j, bi=bi, ti=j % 3):
                    ser(sc)
                    st_ = stat[ti]
                    for p in range(4):
                        sc.wait_ge(sem["dve"], M[f"z2_{j}_{p}"])
                        m = sc.activation(sqs[:, :], z2[:, p * 512:(p + 1) * 512], AF.Square,
                                          accum_out=st_[:, 24 + p:25 + p])
                        M[f"sq2_{j}_{p}"] = inc(m, "act")

            def back2b(j):
                bi = j % 2
                ti = j % 3
                snap_ser()

                @block.vector
                def _(v, j=j, bi=bi, ti=j % 3):
                    ser(v)
                    st_ = stat[ti]
                    v.wait_ge(sem["act"], M[f"sq2_{j}_3"])
                    v.reduce_sum(st_[:, 52:53], st_[:, 20:24], axis=AX.X)
                    v.reduce_sum(st_[:, 53:54], st_[:, 24:28], axis=AX.X)
                    v.tensor_scalar(st_[:, 54:55], st_[:, 52:53], 1.0 / D, None, ALU.mult)
                    v.tensor_scalar(st_[:, 55:56], st_[:, 53:54], 1.0 / D, None, ALU.mult)
                    v.tensor_tensor(st_[:, 56:57], st_[:, 54:55], st_[:, 54:55], ALU.mult)
                    v.tensor_copy(st_[:, 64:65], st_[:, 52:53])
                    v.tensor_copy(st_[:, 64:65], st_[:, 53:54])
                    v.tensor_tensor(st_[:, 57:58], st_[:, 55:56], st_[:, 56:57], ALU.subtract)
                    v.tensor_copy(st_[:, 64:65], st_[:, 52:53])
                    v.tensor_copy(st_[:, 64:65], st_[:, 53:54])
                    m = v.tensor_scalar(st_[:, 58:59], st_[:, 57:58], EPS, None, ALU.add)
                    v.tensor_copy(st_[:, 64:65], st_[:, 52:53])
                    v.tensor_copy(st_[:, 64:65], st_[:, 53:54])
                    M[f"st2_{j}"] = inc(m, "dve")

                @block.vector
                def _(v, j=j, bi=bi, ti=j % 3):
                    ser(v)
                    st_ = stat[ti]
                    m = newton_rsqrt(v, st_, 58, 63, (60, 61, 62, 59), j)
                    M[f"rs2_{j}"] = inc(m, "dve")
                    if j >= 1:
                        v.wait_ge(sem["st"], M[f"yst_{j-1}"])
                    v.tensor_copy(st_[:, 64:65], st_[:, 52:53])
                    v.tensor_copy(st_[:, 64:65], st_[:, 53:54])
                    m = v.tensor_scalar(outb[0][:, :], z2[:, :], st_[:, 54:55], st_[:, 63:64],
                                        ALU.subtract, ALU.mult)
                    M[f"out_{j}"] = inc(m, "dve")

                @block.sync
                def _(s_, j=j, bi=bi):
                    ser(s_)
                    s_.wait_ge(sem["dve"], M[f"out_{j}"])
                    m = s_.dma_start(out=y_d[j * 128:(j + 1) * 128, :], in_=outb[0][:, :])
                    M[f"yst_{j}"] = inc(m, "st", 16)

            for j in range(NT):
                if j >= 3:
                    back2a(j - 3)
                if j >= 2:
                    back1(j - 2)
                if j >= 1:
                    front_tail(j - 1)
                if j >= 3:
                    back2b(j - 3)
                front(j)
            back2a(NT - 3)
            back1(NT - 2)
            front_tail(NT - 1)
            back2b(NT - 3)
            back2a(NT - 2)
            back1(NT - 1)
            back2b(NT - 2)
            back2a(NT - 1)
            back2b(NT - 1)

            if DBG4:
                @block.gpsimd
                def _(g):
                    g.wait_ge(sem["dve"], M[f"x1t_{NT-1}"])
                    inc(g.dma_start(out=dbg["d_ota"], in_=ot_hbm[:, :, :]), "st", 16)
                    g.wait_ge(sem["dve"], M[f"gate_{NT-1}"])
                    inc(g.dma_start(out=dbg["d_gg"], in_=dbgg[:, :]), "st", 16)
                    g.wait_ge(sem["dve"], M["p1_done"])
                    inc(g.dma_start(out=dbg["d_slpbc"], in_=slpbc[:, :]), "st", 16)
                    g.wait_ge(sem["dve"], M[f"etv_{NT-1}"])
                    g.wait_ge(sem["dve"], M[f"out_{NT-1}"])
                    inc(g.dma_start(out=dbg["d_ets"], in_=ets[(NT-1) % 2][:, :]), "st", 16)
                    inc(g.dma_start(out=dbg["d_mts"], in_=mts[(NT-1) % 2][:, :]), "st", 16)
                    inc(g.dma_start(out=dbg["d_ebf"], in_=ebf[(NT-1) % 2][:, :]), "st", 16)
                    inc(g.dma_start(out=dbg["d_mmb"], in_=mmbv[(NT-1) % 2][:, :]), "st", 16)

            @block.sync
            def _(s_):
                s_.wait_ge(sem["st"], C["st"])
        else:
            @block.sync
            def _(s_):
                s_.wait_ge(sem["st"], C["st"])

    return nc, dbg_names


def assert_eq(a, b):
    assert a == b, (a, b)
    return a


def make_in_maps(inputs):
    x = np.asarray(inputs["L0_attn_hidden_state"])
    perm = _perm_deinterleave()
    Wq = np.ascontiguousarray(np.asarray(inputs["L0_attn_Wq"])[:, perm])
    Wk = np.ascontiguousarray(np.asarray(inputs["L0_attn_Wk"])[:, perm])
    Wv = np.ascontiguousarray(np.asarray(inputs["L0_attn_Wv"]))
    Wo = np.ascontiguousarray(np.asarray(inputs["L0_attn_Wo"]))
    bq = np.asarray(inputs["L0_attn_bq"])[perm][None, :].copy()
    bk = np.asarray(inputs["L0_attn_bk"])[perm][None, :].copy()
    bv = np.asarray(inputs["L0_attn_bv"])[None, :].copy()
    bo = np.asarray(inputs["L0_attn_bo"])[None, :].copy()
    selw = np.asarray(inputs["L0_ffn_select_weight"])
    md = np.asarray(inputs["L0_ffn_select_mean_diff"])
    cen = np.asarray(inputs["L0_ffn_imp_center"])[None, :].copy()
    slp = np.asarray(inputs["L0_ffn_imp_slope"])[None, :].copy()
    upd = np.asarray(inputs["L0_ffn_lora_update"])
    dgnw = np.asarray(inputs["L0_ffn_lora_degen"])
    bwv = np.ascontiguousarray(np.asarray(inputs["L0_ffn_bias_w"]))
    debw = np.asarray(inputs["L0_ffn_debias_w"])

    ffnw = np.ascontiguousarray(
        np.concatenate([md.T, selw.transpose(1, 0, 2).reshape(D, 64),
                        upd.transpose(1, 0, 2).reshape(D, 64)], axis=1)).astype(np.float32)
    dgnt = np.ascontiguousarray(dgnw.transpose(0, 2, 1).reshape(64, D))
    debt = np.ascontiguousarray(debw.T)
    ropec, ropes = _rope_tables()
    sel16 = np.zeros((16, D), dtype=np.float32)
    for jj in range(16):
        sel16[jj, jj * 128:(jj + 1) * 128] = 1.0

    in_maps = []
    for b in range(8):
        in_maps.append({
            "x_tm": np.ascontiguousarray(x[b]),
            "x_fm": np.ascontiguousarray(x[b].T),
            "wq": Wq, "wk": Wk, "wv": Wv, "wo": Wo,
            "bq": bq, "bk": bk, "bv": bv, "bo": bo,
            "ffnw": ffnw, "dgnt": dgnt, "bw": bwv, "debt": debt,
            "cen": cen, "slp": slp,
            "ropec": ropec, "ropes": ropes, "sel16": sel16,
        })
    return in_maps


def kernel(**inputs):
    in_maps = make_in_maps(inputs)
    nc, _ = build_nc(STAGE)
    res = run_bass_kernel_spmd(nc, in_maps, core_ids=list(range(8)))
    return np.stack([np.asarray(r["y"], dtype=np.float32) for r in res.results], axis=0)


# version 15
# speedup vs baseline: 1.4399x; 1.0036x over previous
import math
import numpy as np
import ml_dtypes
import concourse.bass as bass
import concourse.mybir as mybir
from concourse.bass_utils import run_bass_kernel_spmd
from contextlib import ExitStack

F32 = mybir.dt.float32
BF16 = mybir.dt.bfloat16
AF = mybir.ActivationFunctionType
ALU = mybir.AluOpType
AX = mybir.AxisListType
bfloat16 = ml_dtypes.bfloat16

D = 2048
S = 2048
NH = 16
HD = 128
NCH = 16
NT = 16
G = 2
NG = NH // G
WQ = 256
NQP = S // WQ
SCALE = 1.0 / math.sqrt(HD)
EPS = 1e-5

STAGE = 4
DBG4 = False
SERIAL = False
LAG = True
PIN_ACT = False
ACT_SET_ID = 0


def _perm_deinterleave():
    p = np.concatenate([np.arange(0, HD, 2), np.arange(1, HD, 2)])
    return np.concatenate([h * HD + p for h in range(NH)])


def _rope_tables():
    k = np.arange(HD)
    j = (k % 64).astype(np.float64)
    inv = np.power(10000.0, -2.0 * j / HD)
    ang = inv[:, None] * np.arange(S, dtype=np.float64)[None, :]
    c = np.cos(ang).astype(np.float32).astype(bfloat16)
    s = np.sin(ang).astype(np.float32).astype(bfloat16)
    return c, s


def build_nc(stage=STAGE):
    nc = bass.Bass("TRN2", target_bir_lowering=False, debug=False)
    dbg_names = []

    x_tm = nc.dram_tensor("x_tm", [S, D], F32, kind="ExternalInput").ap()
    x_fm = nc.dram_tensor("x_fm", [D, S], F32, kind="ExternalInput").ap()
    wq_d = nc.dram_tensor("wq", [D, D], F32, kind="ExternalInput").ap()
    wk_d = nc.dram_tensor("wk", [D, D], F32, kind="ExternalInput").ap()
    wv_d = nc.dram_tensor("wv", [D, D], F32, kind="ExternalInput").ap()
    wo_d = nc.dram_tensor("wo", [D, D], F32, kind="ExternalInput").ap()
    bq_d = nc.dram_tensor("bq", [1, D], F32, kind="ExternalInput").ap()
    bk_d = nc.dram_tensor("bk", [1, D], F32, kind="ExternalInput").ap()
    bv_d = nc.dram_tensor("bv", [1, D], F32, kind="ExternalInput").ap()
    bo_d = nc.dram_tensor("bo", [1, D], F32, kind="ExternalInput").ap()
    ffnw_d = nc.dram_tensor("ffnw", [D, 136], F32, kind="ExternalInput").ap()
    dgnt_d = nc.dram_tensor("dgnt", [64, D], F32, kind="ExternalInput").ap()
    bw_d = nc.dram_tensor("bw", [8, D], F32, kind="ExternalInput").ap()
    debt_d = nc.dram_tensor("debt", [D, 8], F32, kind="ExternalInput").ap()
    cen_d = nc.dram_tensor("cen", [1, 8], F32, kind="ExternalInput").ap()
    slp_d = nc.dram_tensor("slp", [1, 8], F32, kind="ExternalInput").ap()
    ropec_d = nc.dram_tensor("ropec", [HD, S], BF16, kind="ExternalInput").ap()
    ropes_d = nc.dram_tensor("ropes", [HD, S], BF16, kind="ExternalInput").ap()
    sel_d = nc.dram_tensor("sel16", [16, D], F32, kind="ExternalInput").ap()

    y_d = nc.dram_tensor("y", [S, D], F32, kind="ExternalOutput").ap()
    ot_hbm = nc.dram_tensor("ot_hbm", [NH, HD, S], BF16, kind="Internal").ap()
    x1b_hbm = nc.dram_tensor("x1b_hbm", [S, D], BF16, kind="Internal").ap()

    dbg = {}
    if stage == 4 and DBG4:
        dbg["d_x1"] = nc.dram_tensor("d_x1", [S, D], F32, kind="ExternalOutput").ap()
        dbg_names.append("d_x1")
        dbg["d_ota"] = nc.dram_tensor("d_ota", [NH, HD, S], F32, kind="ExternalOutput").ap()
        dbg_names.append("d_ota")
        dbg["d_gg"] = nc.dram_tensor("d_gg", [128, NT * 16], F32, kind="ExternalOutput").ap()
        dbg_names.append("d_gg")
        dbg["d_slpbc"] = nc.dram_tensor("d_slpbc", [128, 8], F32, kind="ExternalOutput").ap()
        dbg_names.append("d_slpbc")
        dbg["d_ets"] = nc.dram_tensor("d_ets", [8, 128], F32, kind="ExternalOutput").ap()
        dbg["d_mts"] = nc.dram_tensor("d_mts", [64, 128], F32, kind="ExternalOutput").ap()
        dbg["d_ebf"] = nc.dram_tensor("d_ebf", [128, 8], F32, kind="ExternalOutput").ap()
        dbg["d_mmb"] = nc.dram_tensor("d_mmb", [128, 64], F32, kind="ExternalOutput").ap()
        dbg_names.extend(["d_ets", "d_mts", "d_ebf", "d_mmb"])
    if stage < 4:
        for nm, shp in (("d_crow", [1, 136]), ("d_m2bc", [128, 8]), ("d_cen", [128, 8])):
            dbg[nm] = nc.dram_tensor(nm, shp, F32, kind="ExternalOutput").ap()
            dbg_names.append(nm)
    if stage in (2, 3):
        for nm, shp in (("d_q0", [HD, S]), ("d_k0", [HD, S]), ("d_v0", [128, 256]),
                        ("d_ot0", [HD, S]), ("d_ot15", [HD, S])):
            dbg[nm] = nc.dram_tensor(nm, shp, F32, kind="ExternalOutput").ap()
            dbg_names.append(nm)

    ident_bf = nc.inline_tensor(np.eye(128, dtype=bfloat16), name="ident_bf").ap()
    ident_f = nc.inline_tensor(np.eye(128, dtype=np.float32), name="ident_f").ap()
    rt = np.zeros((128, 128), dtype=np.float32)
    for i in range(64):
        rt[i, 64 + i] = 1.0
        rt[64 + i, i] = -1.0
    rt_bf = nc.inline_tensor(rt.astype(bfloat16), name="rt_bf").ap()
    ones_r_bf = nc.inline_tensor(np.ones((1, 512), dtype=bfloat16), name="ones_r_bf").ap()
    ones_c_bf = nc.inline_tensor(np.ones((128, 1), dtype=bfloat16), name="ones_c_bf").ap()
    ones_r_f = nc.inline_tensor(np.ones((1, 128), dtype=np.float32), name="ones_r_f").ap()
    ones8_f = nc.inline_tensor(np.ones((8, 1), dtype=np.float32), name="ones8_f").ap()
    eye8_c = nc.inline_tensor(np.eye(8, dtype=np.float32), name="eye8_c").ap()
    m64 = np.zeros((8, 64), dtype=np.float32)
    for n in range(8):
        m64[n, n * 8:(n + 1) * 8] = 1.0
    mask64_c = nc.inline_tensor(m64, name="mask64_c").ap()
    wt = np.tile((8.0 - np.arange(8, dtype=np.float32))[None, :], (128, 1))
    wtie_c = nc.inline_tensor(wt, name="wtie_c").ap()

    with ExitStack() as ctx:
        E = ctx.enter_context
        xT = E(nc.sbuf_tensor("xT", [128, NCH * S], BF16))
        ffnw = E(nc.sbuf_tensor("ffnw_sb", [128, NCH * 136], BF16))
        dgn = E(nc.sbuf_tensor("dgn", [64, D], BF16))
        bwsb = E(nc.sbuf_tensor("bwsb", [8, D], BF16))
        debt = E(nc.sbuf_tensor("debt_sb", [128, NCH * 8], BF16))
        bqr = E(nc.sbuf_tensor("bqr", [1, D], BF16))
        bkr = E(nc.sbuf_tensor("bkr", [1, D], BF16))
        bvr = E(nc.sbuf_tensor("bvr", [1, D], BF16))
        bor = E(nc.sbuf_tensor("bor", [1, D], BF16))
        cenr = E(nc.sbuf_tensor("cenr", [1, 8], F32))
        slpr = E(nc.sbuf_tensor("slpr", [1, 8], F32))
        selsb = E(nc.sbuf_tensor("selsb", [16, D], F32))
        ropecs = E(nc.sbuf_tensor("ropecs", [128, S], BF16))
        ropess = E(nc.sbuf_tensor("ropess", [128, S], BF16))
        idb = E(nc.sbuf_tensor("idb", [128, 128], BF16))
        idf = E(nc.sbuf_tensor("idf", [128, 128], F32))
        rtb = E(nc.sbuf_tensor("rtb", [128, 128], BF16))
        onerb = E(nc.sbuf_tensor("onerb", [1, 512], BF16))
        onecb = E(nc.sbuf_tensor("onecb", [128, 1], BF16))
        onerf = E(nc.sbuf_tensor("onerf", [1, 128], F32))
        one8f = E(nc.sbuf_tensor("one8f", [8, 1], F32))
        eye8s = E(nc.sbuf_tensor("eye8s", [8, 8], F32))
        m64s = E(nc.sbuf_tensor("m64s", [8, 64], F32))
        wties = E(nc.sbuf_tensor("wties", [128, 8], F32))
        crow = E(nc.sbuf_tensor("crow_sb", [1, 136], BF16))
        m2bc = E(nc.sbuf_tensor("m2bc", [128, 8], F32))
        cenbc = E(nc.sbuf_tensor("cenbc", [128, 8], F32))
        slpbc = E(nc.sbuf_tensor("slpbc", [128, 8], F32))
        scr8 = E(nc.sbuf_tensor("scr8", [8, 136], F32))

        PA = E(nc.psum_tensor("PA", [128, 1024], F32))
        PB = E(nc.psum_tensor("PB", [128, 1024], F32))
        PC = E(nc.psum_tensor("PC", [128, 512], F32))
        PD = E(nc.psum_tensor("PD", [128, 512], F32))
        PE_ = E(nc.psum_tensor("PEb", [128, 512], F32))
        PF = E(nc.psum_tensor("PFb", [128, 512], F32))
        PDb = PD[:, :].bitcast(BF16)
        PBb = PB[:, :].bitcast(BF16)

        sem = {}
        for name in ("pe", "act", "dve", "dG", "dS", "dA", "st", "cA", "cB"):
            sem[name] = E(nc.semaphore(name))
        C = {k: 0 for k in sem}
        block = E(nc.Block())
        M = {}

        def inc(ins, s_, n=1):
            ins.then_inc(sem[s_], n)
            C[s_] += n
            return C[s_]

        SNAP = {}

        def snap_ser():
            SNAP.update({k: C[k] for k in ("pe", "act", "dve")})

        def ser(eng):
            if SERIAL:
                for s2 in ("pe", "act", "dve"):
                    eng.wait_ge(sem[s2], SNAP.get(s2, 0))

        free = {"PA": None, "PB": None, "PC": None, "PD": None, "PE": None, "PF": None}

        def wait_free(eng, bank):
            if free[bank] is not None:
                s_, v_ = free[bank]
                eng.wait_ge(sem[s_], v_)

        @block.gpsimd
        def _(g):
            inc(g.dma_start(out=xT[:, :].rearrange("p (c t) -> p c t", c=NCH),
                            in_=x_fm.rearrange("(c p) t -> p c t", p=128)), "dG", 16)
            inc(g.dma_start(out=ffnw[:, :].rearrange("p (c f) -> p c f", c=NCH),
                            in_=ffnw_d.rearrange("(c p) f -> p c f", p=128)), "dG", 16)
            inc(g.dma_start(out=debt[:, :].rearrange("p (c f) -> p c f", c=NCH),
                            in_=debt_d.rearrange("(c p) f -> p c f", p=128)), "dG", 16)
            inc(g.dma_start(out=dgn[:, :], in_=dgnt_d), "dG", 16)
            inc(g.dma_start(out=bwsb[:, :], in_=bw_d), "dG", 16)
            inc(g.dma_start(out=bqr[:, :], in_=bq_d), "dG", 16)
            inc(g.dma_start(out=bkr[:, :], in_=bk_d), "dG", 16)
            inc(g.dma_start(out=bvr[:, :], in_=bv_d), "dG", 16)
            inc(g.dma_start(out=bor[:, :], in_=bo_d), "dG", 16)
            M["p0_gp"] = C["dG"]

        @block.sync
        def _(s_):
            for dst, src in ((idb, ident_bf), (idf, ident_f), (rtb, rt_bf),
                             (onerb, ones_r_bf), (onecb, ones_c_bf), (onerf, ones_r_f),
                             (one8f, ones8_f), (eye8s, eye8_c), (m64s, mask64_c),
                             (wties, wtie_c), (cenr, cen_d), (slpr, slp_d), (selsb, sel_d),
                             (ropecs, ropec_d), (ropess, ropes_d)):
                inc(s_.dma_start(out=dst[:, :], in_=src), "dS", 16)
            M["p0_sy"] = C["dS"]

        if PIN_ACT:
            @block.scalar
            def _(sc):
                sc.add_instruction(mybir.InstLoadActFuncSet(
                    name=nc.get_next_instruction_name(), act_func_set_id=ACT_SET_ID, ins=[], outs=[]))

        @block.tensor
        def _(t):
            t.wait_ge(sem["dG"], M["p0_gp"] + (48 if stage >= 2 else 0))
            t.wait_ge(sem["dS"], M["p0_sy"])
            for c in range(NCH):
                f0 = c * 136
                t.matmul(PC[0:8, 0:64], ffnw[:, f0:f0 + 8], ffnw[:, f0 + 8:f0 + 72],
                         start=(c == 0), stop=(c == NCH - 1))
            for c in range(NCH):
                f0 = c * 136
                t.matmul(PC[0:8, 64:128], debt[:, c * 8:(c + 1) * 8], ffnw[:, f0 + 72:f0 + 136],
                         start=(c == 0), stop=(c == NCH - 1))
            for c in range(NCH):
                f0 = c * 136
                m = t.matmul(PC[0:8, 128:136], ffnw[:, f0:f0 + 8], ffnw[:, f0:f0 + 8],
                             start=(c == 0), stop=(c == NCH - 1))
            M["p1_mm"] = inc(m, "pe")

        @block.vector
        def _(v):
            v.wait_ge(sem["pe"], M["p1_mm"])
            v.tensor_tensor(scr8[0:8, 0:64], PC[0:8, 0:64], m64s[:, :], ALU.mult)
            v.tensor_tensor(scr8[0:8, 64:128], PC[0:8, 64:128], m64s[:, :], ALU.mult)
            m = v.tensor_tensor(scr8[0:8, 128:136], PC[0:8, 128:136], eye8s[:, :], ALU.mult)
            M["p1_msk"] = inc(m, "dve")
        free["PC"] = ("dve", M["p1_msk"])

        @block.tensor
        def _(t):
            t.wait_ge(sem["dve"], M["p1_msk"])
            t.matmul(PD[0:1, 0:64], one8f[:, :], scr8[0:8, 0:64], start=True, stop=True)
            t.matmul(PD[0:1, 64:128], one8f[:, :], scr8[0:8, 64:128], start=True, stop=True)
            m = t.matmul(PD[0:1, 128:136], one8f[:, :], scr8[0:8, 128:136], start=True, stop=True)
            M["p1_row"] = inc(m, "pe")

        @block.vector
        def _(v):
            v.wait_ge(sem["pe"], M["p1_row"])
            v.memset(crow[:, :], 0.0)
            v.tensor_scalar(crow[0:1, 8:72], PD[0:1, 0:64], -1.0, None, ALU.mult)
            v.tensor_scalar(crow[0:1, 72:136], PD[0:1, 64:128], -1.0, None, ALU.mult)
            m = v.tensor_copy(scr8[0:1, 0:8], PD[0:1, 128:136])
            M["p1_m2"] = inc(m, "dve")
        free["PD"] = ("dve", M["p1_m2"])

        @block.tensor
        def _(t):
            t.wait_ge(sem["dve"], M["p1_m2"])
            t.matmul(PE_[:, 0:8], onerf[:, :], scr8[0:1, 0:8], start=True, stop=True)
            t.matmul(PE_[:, 8:16], onerf[:, :], cenr[:, :], start=True, stop=True)
            m = t.matmul(PE_[:, 16:24], onerf[:, :], slpr[:, :], start=True, stop=True)
            M["p1_bc"] = inc(m, "pe")

        @block.vector
        def _(v):
            v.wait_ge(sem["pe"], M["p1_bc"])
            v.tensor_copy(m2bc[:, :], PE_[:, 0:8])
            v.tensor_copy(cenbc[:, :], PE_[:, 8:16])
            m = v.tensor_copy(slpbc[:, :], PE_[:, 16:24])
            M["p1_done"] = inc(m, "dve")
        free["PE"] = ("dve", M["p1_done"])

        if stage < 4:
            @block.gpsimd
            def _(g):
                g.wait_ge(sem["dve"], M["p1_done"])
                inc(g.dma_start(out=dbg["d_crow"], in_=crow[:, :]), "st", 16)

            @block.sync
            def _(s_):
                s_.wait_ge(sem["dve"], M["p1_done"])
                inc(s_.dma_start(out=dbg["d_m2bc"], in_=m2bc[:, :]), "st", 16)
                inc(s_.dma_start(out=dbg["d_cen"], in_=cenbc[:, :]), "st", 16)

        if stage >= 2:
            bctx = ctx.enter_context(ExitStack())
            EB = bctx.enter_context
            wsl = {w: EB(nc.sbuf_tensor(f"wsl_{w}", [128, NCH * 256], BF16)) for w in "qkv"}
            qk = EB(nc.sbuf_tensor("qk", [128, 4 * S], BF16))
            vsb = EB(nc.sbuf_tensor("vsb", [128, NT * 256], BF16))
            rotb = EB(nc.sbuf_tensor("rotb", [128, S], BF16))
            foldb = EB(nc.sbuf_tensor("foldb", [128, S], BF16))
            pt = [EB(nc.sbuf_tensor(f"pt{i}", [128, 4096], BF16)) for i in range(2)]
            oth = EB(nc.sbuf_tensor("oth", [128, S], F32))
            otn = [EB(nc.sbuf_tensor(f"otn{i}", [128, S], BF16)) for i in range(2)]
            rcol = EB(nc.sbuf_tensor("rcol", [128, 16], F32))
            rz16 = EB(nc.sbuf_tensor("rz16", [16, 128], F32))
            nsh = EB(nc.sbuf_tensor("nsh", [128, 2], F32))
            sml = EB(nc.sbuf_tensor("sml", [128, 8], F32))
            smr = EB(nc.sbuf_tensor("smr", [1, 16], F32))
            smb = EB(nc.sbuf_tensor("smb", [128, 2], BF16))

            ngroups = NG if stage >= 3 else 1

            @block.gpsimd
            def _(g):
                for w, wd in (("q", wq_d), ("k", wk_d), ("v", wv_d)):
                    m = g.dma_start(out=wsl[w][:, :].rearrange("p (c f) -> p c f", c=NCH),
                                    in_=wd[:, 0:G * 128].rearrange("(c p) f -> p c f", p=128))
                    inc(m, "dG", 16)
                M["wsl_0"] = C["dG"]

            for gi in range(ngroups):
                h0 = G * gi

                act_base = C["act"]
                ev_mark = lambda pp: act_base + pp + 1

                @block.tensor
                def _(t, gi=gi, h0=h0, ev_mark=ev_mark):
                    t.wait_ge(sem["dG"], M[f"wsl_{gi}"])
                    if gi == 0:
                        t.wait_ge(sem["dS"], M["p0_sy"])
                        t.wait_ge(sem["dve"], M["p1_done"])
                    pp = 0
                    for hl in range(G):
                        for wname, brow in (("q", bqr), ("k", bkr)):
                            for p in range(4):
                                ps = PA if pp % 2 == 0 else PB
                                if pp >= 2:
                                    t.wait_ge(sem["act"], ev_mark(pp - 2))
                                else:
                                    wait_free(t, "PA" if pp % 2 == 0 else "PB")
                                for c in range(NCH):
                                    t.matmul(ps[:, 0:512],
                                             wsl[wname][:, c * 256 + hl * 128: c * 256 + hl * 128 + 128],
                                             xT[:, c * S + p * 512: c * S + p * 512 + 512],
                                             start=(c == 0), stop=False)
                                m = t.matmul(ps[:, 0:512],
                                             brow[0:1, (h0 + hl) * 128:(h0 + hl) * 128 + 128],
                                             onerb[0:1, 0:512], start=False, stop=True)
                                M[f"qk_{gi}_{pp}"] = inc(m, "pe")
                                pp += 1

                @block.scalar
                def _(sc, gi=gi):
                    pp = 0
                    for hl in range(G):
                        for wname in ("q", "k"):
                            slot = (0 if wname == "q" else 2) + hl
                            for p in range(4):
                                ps = PA if pp % 2 == 0 else PB
                                sc.wait_ge(sem["pe"], M[f"qk_{gi}_{pp}"])
                                m = sc.activation(qk[:, slot * S + p * 512: slot * S + p * 512 + 512],
                                                  ps[:, 0:512], AF.Copy)
                                assert inc(m, "act") == ev_mark(pp)
                                pp += 1
                free["PA"] = ("act", ev_mark(14))
                free["PB"] = ("act", ev_mark(15))

                dve_vbase = C["dve"]
                vev_mark = lambda tt: dve_vbase + tt + 1

                @block.tensor
                def _(t, gi=gi, h0=h0, vev_mark=vev_mark):
                    wait_free(t, "PC")
                    wait_free(t, "PD")
                    for tt in range(NT):
                        ps = PC if tt % 2 == 0 else PD
                        if tt >= 2:
                            t.wait_ge(sem["dve"], vev_mark(tt - 2))
                        for c in range(NCH):
                            t.matmul(ps[:, 0:256],
                                     xT[:, c * S + tt * 128: c * S + tt * 128 + 128],
                                     wsl["v"][:, c * 256: c * 256 + 256],
                                     start=(c == 0), stop=False)
                        m = t.matmul(ps[:, 0:256], onerb[0:1, 0:128],
                                     bvr[0:1, h0 * 128: h0 * 128 + 256], start=False, stop=True)
                        M[f"v_{gi}_{tt}"] = inc(m, "pe")

                @block.vector
                def _(v, gi=gi, vev_mark=vev_mark):
                    for tt in range(NT):
                        ps = PC if tt % 2 == 0 else PD
                        v.wait_ge(sem["pe"], M[f"v_{gi}_{tt}"])
                        m = v.tensor_copy(vsb[:, tt * 256:(tt + 1) * 256], ps[:, 0:256])
                        M[f"vev_{gi}_{tt}"] = inc(m, "dve")
                        assert M[f"vev_{gi}_{tt}"] == vev_mark(tt)
                free["PC"] = ("dve", vev_mark(14))
                free["PD"] = ("dve", vev_mark(15))

                if gi + 1 < ngroups:
                    @block.gpsimd
                    def _(g, gi=gi):
                        g.wait_ge(sem["pe"], M[f"v_{gi}_{NT-1}"])
                        for w, wd in (("q", wq_d), ("k", wk_d), ("v", wv_d)):
                            c0 = (gi + 1) * G * 128
                            m = g.dma_start(out=wsl[w][:, :].rearrange("p (c f) -> p c f", c=NCH),
                                            in_=wd[:, c0:c0 + G * 128].rearrange("(c p) f -> p c f", p=128))
                            inc(m, "dG", 16)
                        M[f"wsl_{gi+1}"] = C["dG"]

                for hl in range(G):
                    for wi, wname in enumerate(("q", "k")):
                        slot = (0 if wname == "q" else 2) + hl
                        for half in range(2):
                            bank = "PA" if half == 0 else "PB"
                            ps = PA if half == 0 else PB

                            @block.tensor
                            def _(t, gi=gi, hl=hl, wi=wi, slot=slot, half=half, ps=ps, bank=bank):
                                wait_free(t, bank)
                                if wi == 0 and half == 0:
                                    pass
                                for j2 in range(2):
                                    o = half * 1024 + j2 * 512
                                    m = t.matmul(ps[:, j2 * 512:(j2 + 1) * 512], rtb[:, :],
                                                 qk[:, slot * S + o: slot * S + o + 512],
                                                 start=True, stop=True)
                                M[f"rot_{gi}_{hl}_{wi}_{half}"] = inc(m, "pe")

                            @block.scalar
                            def _(sc, gi=gi, hl=hl, wi=wi, half=half, ps=ps):
                                sc.wait_ge(sem["pe"], M[f"rot_{gi}_{hl}_{wi}_{half}"])
                                m = sc.activation(rotb[:, half * 1024:(half + 1) * 1024],
                                                  ps[:, 0:1024], AF.Copy)
                                M[f"roev_{gi}_{hl}_{wi}_{half}"] = inc(m, "act")
                            free[bank] = ("act", M[f"roev_{gi}_{hl}_{wi}_{half}"])

                            @block.vector
                            def _(v, gi=gi, hl=hl, wi=wi, slot=slot, half=half):
                                o = half * 1024
                                v.wait_ge(sem["act"], M[f"roev_{gi}_{hl}_{wi}_{half}"])
                                v.tensor_tensor(rotb[:, o:o + 1024], rotb[:, o:o + 1024],
                                                ropess[:, o:o + 1024], ALU.mult)
                                v.tensor_tensor(qk[:, slot * S + o: slot * S + o + 1024],
                                                qk[:, slot * S + o: slot * S + o + 1024],
                                                ropecs[:, o:o + 1024], ALU.mult)
                                m = v.tensor_tensor(qk[:, slot * S + o: slot * S + o + 1024],
                                                    qk[:, slot * S + o: slot * S + o + 1024],
                                                    rotb[:, o:o + 1024], ALU.add)
                                M[f"rope_{gi}_{hl}_{wi}_{half}"] = inc(m, "dve")

                for hl in range(G):
                    @block.vector
                    def _(v, gi=gi, hl=hl):
                        for wi in range(2):
                            for half in range(2):
                                v.wait_ge(sem["dve"], M[f"rope_{gi}_{hl}_{wi}_{half}"])
                        m = v.tensor_tensor(foldb[:, :], qk[:, hl * S:(hl + 1) * S],
                                            qk[:, hl * S:(hl + 1) * S], ALU.mult)
                        M[f"sqq_{gi}_{hl}"] = inc(m, "dve")

                    @block.tensor
                    def _(t, gi=gi, hl=hl):
                        t.wait_ge(sem["dve"], M[f"sqq_{gi}_{hl}"])
                        wait_free(t, "PF")
                        for qt in range(16):
                            m = t.matmul(PF[:, qt:qt + 1], foldb[:, qt * 128:(qt + 1) * 128],
                                         onecb[:, :], start=True, stop=True)
                        M[f"nq_{gi}_{hl}"] = inc(m, "pe")

                    @block.vector
                    def _(v, gi=gi, hl=hl):
                        v.wait_ge(sem["pe"], M[f"nq_{gi}_{hl}"])
                        v.reduce_max(sml[:, 0:1], PF[:, 0:16], axis=AX.X)
                        m = v.tensor_tensor(foldb[:, :], qk[:, (2 + hl) * S:(3 + hl) * S],
                                            qk[:, (2 + hl) * S:(3 + hl) * S], ALU.mult)
                        M[f"sqk_{gi}_{hl}"] = inc(m, "dve")
                        m = v.tensor_copy(smb[:, 0:1], sml[:, 0:1])
                        M[f"mq1_{gi}_{hl}"] = inc(m, "dve")

                    @block.tensor
                    def _(t, gi=gi, hl=hl):
                        t.wait_ge(sem["dve"], M[f"mq1_{gi}_{hl}"])
                        wait_free(t, "PD")
                        t.transpose(PDb[0:1, 0:128], smb[:, 0:1], idb[:, :])
                        t.wait_ge(sem["dve"], M[f"sqk_{gi}_{hl}"])
                        for qt in range(16):
                            m = t.matmul(PF[:, qt:qt + 1], foldb[:, qt * 128:(qt + 1) * 128],
                                         onecb[:, :], start=True, stop=True)
                        M[f"nk_{gi}_{hl}"] = inc(m, "pe")

                    @block.vector
                    def _(v, gi=gi, hl=hl):
                        v.wait_ge(sem["pe"], M[f"nk_{gi}_{hl}"])
                        v.reduce_max(sml[:, 1:2], PF[:, 0:16], axis=AX.X)
                        v.reduce_max(smr[0:1, 0:1], PDb[0:1, 0:128], axis=AX.X)
                        v.tensor_copy(sml[:, 3:4], sml[:, 0:1])
                        m = v.tensor_copy(smb[:, 1:2], sml[:, 1:2])
                        M[f"mk1_{gi}_{hl}"] = inc(m, "dve")

                    @block.tensor
                    def _(t, gi=gi, hl=hl):
                        t.wait_ge(sem["dve"], M[f"mk1_{gi}_{hl}"])
                        m = t.transpose(PDb[0:1, 128:256], smb[:, 1:2], idb[:, :])
                        M[f"tk_{gi}_{hl}"] = inc(m, "pe")

                    @block.vector
                    def _(v, gi=gi, hl=hl):
                        v.wait_ge(sem["pe"], M[f"tk_{gi}_{hl}"])
                        v.reduce_max(smr[0:1, 1:2], PDb[0:1, 128:256], axis=AX.X)
                        v.tensor_copy(sml[:, 4:5], sml[:, 0:1])
                        v.tensor_copy(sml[:, 4:5], sml[:, 1:2])
                        v.tensor_tensor(smr[0:1, 2:3], smr[0:1, 0:1], smr[0:1, 1:2], ALU.add)
                        v.tensor_copy(sml[:, 4:5], sml[:, 0:1])
                        v.tensor_copy(sml[:, 4:5], sml[:, 1:2])
                        m = v.tensor_scalar(smr[0:1, 3:4], smr[0:1, 2:3], -SCALE / 2.0, None, ALU.mult)
                        M[f"nsr_{gi}_{hl}"] = inc(m, "dve")
                    free["PD"] = ("dve", M[f"nsr_{gi}_{hl}"])
                    free["PF"] = ("dve", M[f"mk1_{gi}_{hl}"])

                    @block.tensor
                    def _(t, gi=gi, hl=hl):
                        t.wait_ge(sem["dve"], M[f"nsr_{gi}_{hl}"])
                        wait_free(t, "PE")
                        m = t.matmul(PE_[:, 256 + hl:257 + hl], onerf[:, :], smr[0:1, 3:4],
                                     start=True, stop=True)
                        M[f"nsb_{gi}_{hl}"] = inc(m, "pe")

                    @block.vector
                    def _(v, gi=gi, hl=hl):
                        v.wait_ge(sem["pe"], M[f"nsb_{gi}_{hl}"])
                        m = v.tensor_copy(nsh[:, hl:hl + 1], PE_[:, 256 + hl:257 + hl])
                        M[f"nsh_{gi}_{hl}"] = inc(m, "dve")
                    free["PE"] = ("dve", M[f"nsh_{gi}_{hl}"])

                for hl in range(G):
                    h = h0 + hl
                    act_hbase = C["act"]
                    exp_mark = lambda qp, kg: act_hbase + qp * 4 + kg + 1
                    exp_jobs = []

                    def emit_sg(qp):
                        it = f"{gi}_{hl}_{qp}"

                        @block.tensor
                        def _(t, qp=qp, it=it):
                            if qp == 0:
                                t.wait_ge(sem["dve"], M[f"nsh_{gi}_{hl}"])
                                if hl == 0:
                                    t.wait_ge(sem["dve"], M[f"vev_{gi}_{NT-1}"])
                                else:
                                    t.wait_ge(sem["dve"], M[f"rcp_{gi}_{hl-1}"])
                                wait_free(t, "PA")
                                wait_free(t, "PB")
                            for kg in range(4):
                                ps = PA if kg % 2 == 0 else PB
                                ei = qp * 4 + kg
                                if ei >= 2:
                                    t.wait_ge(sem["act"], exp_mark((ei - 2) // 4, (ei - 2) % 4))
                                for k4 in range(4):
                                    kt = kg * 4 + k4
                                    m = t.matmul(ps[:, k4 * 256:(k4 + 1) * 256],
                                                 qk[:, (2 + hl) * S + kt * 128: (2 + hl) * S + kt * 128 + 128],
                                                 qk[:, hl * S + qp * 256: hl * S + qp * 256 + 256],
                                                 start=True, stop=True)
                                M[f"sg_{it}_{kg}"] = inc(m, "pe")

                        exp_jobs.append((qp, it))

                    def emit_pv(qp):
                        it = f"{gi}_{hl}_{qp}"

                        @block.tensor
                        def _(t, qp=qp, it=it):
                            buf = pt[qp % 2]
                            t.wait_ge(sem["act"], exp_mark(qp, 3))
                            if qp >= 2:
                                t.wait_ge(sem["dve"], M[f"otev_{gi}_{hl}_{qp-2}"])
                            else:
                                wait_free(t, "PC" if qp % 2 == 0 else "PD")
                            ps = PC if qp % 2 == 0 else PD
                            for kt in range(16):
                                m = t.matmul(ps[:, 0:256],
                                             vsb[:, kt * 256 + hl * 128: kt * 256 + hl * 128 + 128],
                                             buf[:, kt * 256:(kt + 1) * 256],
                                             start=(kt == 0), stop=(kt == 15))
                            M[f"pv_{it}"] = inc(m, "pe")
                            t.wait_ge(sem["dve"], M[f"f1_{it}"])
                            if hl == 0 and qp == 0:
                                wait_free(t, "PF")
                            for j in range(2):
                                m = t.matmul(PF[:, qp * 2 + j: qp * 2 + j + 1],
                                             foldb[:, j * 128:(j + 1) * 128], onecb[:, :],
                                             start=True, stop=True)
                            M[f"z_{it}"] = inc(m, "pe")

                        @block.vector
                        def _(v, qp=qp, it=it):
                            v.wait_ge(sem["pe"], M[f"pv_{it}"])
                            ps = PC if qp % 2 == 0 else PD
                            m = v.tensor_copy(oth[:, qp * 256:(qp + 1) * 256], ps[:, 0:256])
                            M[f"otev_{it}"] = inc(m, "dve")

                    def emit_fold(qp):
                        it = f"{gi}_{hl}_{qp}"

                        @block.vector
                        def _(v, qp=qp, it=it):
                            buf = pt[qp % 2]
                            v.wait_ge(sem["act"], exp_mark(qp, 3))
                            if qp > 0:
                                v.wait_ge(sem["pe"], M[f"z_{gi}_{hl}_{qp-1}"])
                            v.tensor_tensor(foldb[:, 0:2048], buf[:, 0:2048], buf[:, 2048:4096], ALU.add)
                            v.tensor_tensor(foldb[:, 0:1024], foldb[:, 0:1024], foldb[:, 1024:2048], ALU.add)
                            v.tensor_tensor(foldb[:, 0:512], foldb[:, 0:512], foldb[:, 512:1024], ALU.add)
                            m = v.tensor_tensor(foldb[:, 0:256], foldb[:, 0:256], foldb[:, 256:512], ALU.add)
                            M[f"f1_{it}"] = inc(m, "dve")

                    def emit_exps():
                        @block.scalar
                        def _(sc):
                            for qp, it in exp_jobs:
                                buf = pt[qp % 2]
                                for kg in range(4):
                                    ps = PA if kg % 2 == 0 else PB
                                    sc.wait_ge(sem["pe"], M[f"sg_{it}_{kg}"])
                                    if qp >= 2 and kg == 0:
                                        sc.wait_ge(sem["pe"], M[f"pv_{gi}_{hl}_{qp-2}"])
                                        sc.wait_ge(sem["dve"], M[f"f1_{gi}_{hl}_{qp-2}"])
                                    m = sc.activation(buf[:, kg * 1024:(kg + 1) * 1024], ps[:, 0:1024],
                                                      AF.Exp, bias=nsh[:, hl:hl + 1], scale=SCALE)
                                    assert inc(m, "act") == exp_mark(qp, kg)
                            exp_jobs.clear()

                    if LAG:
                        for qp in range(NQP + 1):
                            if qp < NQP:
                                emit_sg(qp)
                            if qp == NQP - 1:
                                emit_exps()
                            if qp >= 1:
                                emit_pv(qp - 1)
                            if qp < NQP:
                                emit_fold(qp)
                        emit_exps()
                    else:
                        for qp in range(NQP):
                            emit_sg(qp)
                            emit_exps()
                            emit_fold(qp)
                            emit_pv(qp)
                    free["PC"] = ("dve", M[f"otev_{gi}_{hl}_{NQP-2}"])
                    free["PD"] = ("dve", M[f"otev_{gi}_{hl}_{NQP-1}"])
                    free["PA"] = ("act", exp_mark(NQP - 1, 3))
                    free["PB"] = ("act", exp_mark(NQP - 1, 3))

                    @block.vector
                    def _(v, gi=gi, hl=hl):
                        v.wait_ge(sem["pe"], M[f"z_{gi}_{hl}_{NQP-1}"])
                        m = v.reciprocal(rcol[:, :], PF[:, 0:16])
                        M[f"rcp_{gi}_{hl}"] = inc(m, "dve")
                    free["PF"] = ("dve", M[f"rcp_{gi}_{hl}"])

                    @block.tensor
                    def _(t, gi=gi, hl=hl):
                        t.wait_ge(sem["dve"], M[f"rcp_{gi}_{hl}"])
                        wait_free(t, "PE")
                        m = t.transpose(PE_[0:16, 0:128], rcol[:, :], idf[:, :])
                        M[f"rct_{gi}_{hl}"] = inc(m, "pe")

                    @block.vector
                    def _(v, gi=gi, hl=hl):
                        v.wait_ge(sem["pe"], M[f"rct_{gi}_{hl}"])
                        m = v.tensor_copy(rz16[:, :], PE_[0:16, 0:128])
                        M[f"rz_{gi}_{hl}"] = inc(m, "dve")
                    free["PE"] = ("dve", M[f"rz_{gi}_{hl}"])

                    for qp in range(NQP):
                        @block.tensor
                        def _(t, gi=gi, hl=hl, qp=qp):
                            t.wait_ge(sem["dve"], M[f"rz_{gi}_{hl}"])
                            bank = "PE" if qp % 2 == 0 else "PF"
                            ps = PE_ if qp % 2 == 0 else PF
                            if qp >= 2:
                                t.wait_ge(sem["dve"], M[f"nrm_{gi}_{hl}_{qp-2}"])
                            else:
                                wait_free(t, bank)
                            for j in range(2):
                                jj = qp * 2 + j
                                m = t.matmul(ps[:, j * 128:(j + 1) * 128],
                                             selsb[:, jj * 128:(jj + 1) * 128],
                                             rz16[:, :], start=True, stop=True)
                            M[f"bmm_{gi}_{hl}_{qp}"] = inc(m, "pe")

                        @block.vector
                        def _(v, gi=gi, hl=hl, qp=qp, h=h):
                            v.wait_ge(sem["pe"], M[f"bmm_{gi}_{hl}_{qp}"])
                            if qp == 0 and h >= 2:
                                v.wait_ge(sem["dS"], M[f"ots_{h-1}"])
                            ps = PE_ if qp % 2 == 0 else PF
                            m = v.tensor_tensor(otn[hl][:, qp * 256:(qp + 1) * 256],
                                                oth[:, qp * 256:(qp + 1) * 256],
                                                ps[:, 0:256], ALU.mult)
                            M[f"nrm_{gi}_{hl}_{qp}"] = inc(m, "dve")
                    free["PE"] = ("dve", M[f"nrm_{gi}_{hl}_{NQP-2}"])
                    free["PF"] = ("dve", M[f"nrm_{gi}_{hl}_{NQP-1}"])

                    @block.sync
                    def _(s_, gi=gi, hl=hl, h=h):
                        for qp in range(NQP):
                            s_.wait_ge(sem["dve"], M[f"nrm_{gi}_{hl}_{qp}"])
                        m = s_.dma_start(out=ot_hbm[h, :, :], in_=otn[hl][:, :])
                        M[f"ots_{h}"] = inc(m, "dS", 16)

            if stage == 2:
                @block.gpsimd
                def _(g):
                    g.wait_ge(sem["dve"], M["rope_0_0_0_1"])
                    inc(g.dma_start(out=dbg["d_q0"], in_=qk[:, 0:S]), "st", 16)
                    g.wait_ge(sem["dve"], M["rope_0_0_1_1"])
                    inc(g.dma_start(out=dbg["d_k0"], in_=qk[:, 2 * S:3 * S]), "st", 16)
                    g.wait_ge(sem["dve"], M["vev_0_0"])
                    inc(g.dma_start(out=dbg["d_v0"], in_=vsb[:, 0:256]), "st", 16)
            if stage in (2, 3):
                @block.gpsimd
                def _(g):
                    g.wait_ge(sem["dS"], M["ots_0"])
                    inc(g.dma_start(out=dbg["d_ot0"], in_=ot_hbm[0, :, :]), "st", 16)
                    g.wait_ge(sem["dS"], M[f"ots_{2*ngroups-1}"])
                    inc(g.dma_start(out=dbg["d_ot15"], in_=ot_hbm[2*ngroups-1, :, :]), "st", 16)

        if stage >= 4:
            bctx.close()

            @block.gpsimd
            def _(g):
                g.wait_ge(sem["pe"], M[f"v_{NG-1}_{NT-1}"])
                m = g.dma_start(out=xT[:, :].rearrange("p (c f) -> p c f", c=NCH),
                                in_=wo_d.rearrange("(c p) f -> p c f", p=128))
                M["wo"] = inc(m, "dG", 16)

            cctx = ctx.enter_context(ExitStack())
            EC = cctx.enter_context
            otc = [EC(nc.sbuf_tensor(f"otc{i}", [128, S], BF16)) for i in range(2)]
            xtl = [EC(nc.sbuf_tensor(f"xtl{i}", [128, D], F32)) for i in range(2)]
            z1 = EC(nc.sbuf_tensor("z1", [128, D], F32))
            x1 = [EC(nc.sbuf_tensor(f"x1_{i}", [128, D], F32)) for i in range(3)]
            x1b = [EC(nc.sbuf_tensor(f"x1b{i}", [128, D], BF16)) for i in range(3)]
            x1t = [EC(nc.sbuf_tensor("x1t0", [128, D], BF16))]
            sqs = EC(nc.sbuf_tensor("sqs", [128, 512], F32))
            z2 = EC(nc.sbuf_tensor("z2", [128, D], F32))
            outb = [EC(nc.sbuf_tensor("outb0", [128, D], F32))]
            ts1 = EC(nc.sbuf_tensor("ts1", [128, 512], F32))
            stat = [EC(nc.sbuf_tensor(f"stat{i}", [128, 76], F32)) for i in range(3)]
            gsc = [EC(nc.sbuf_tensor(f"gsc{i}", [128, 8], F32)) for i in range(3)]
            prs = EC(nc.sbuf_tensor("prs", [128, 104], F32))
            pr2s = EC(nc.sbuf_tensor("pr2s", [128, 64], F32))
            prr = EC(nc.sbuf_tensor("prr", [128, 64], F32))
            mmbv = [EC(nc.sbuf_tensor("mmb0", [128, 64], BF16))]
            dbgg = EC(nc.sbuf_tensor("dbgg", [128, NT * 16], F32)) if DBG4 else None
            ebf = [EC(nc.sbuf_tensor("ebf0", [128, 8], BF16))]
            ets = [EC(nc.sbuf_tensor("ets0", [8, 128], BF16))]
            mts = [EC(nc.sbuf_tensor("mts0", [64, 128], BF16))]



            def newton_rsqrt(v, st_, vcol, ycol, tmp, j):
                A_, B_ = 0.430762, 0.476801
                sp = lambda: v.tensor_copy(st_[:, 33:34], st_[:, 8:9])
                v.tensor_scalar_min(st_[:, tmp[0]:tmp[0] + 1], st_[:, vcol:vcol + 1], 7.5)
                sp()
                sp()
                v.reciprocal(st_[:, tmp[1]:tmp[1] + 1], st_[:, tmp[0]:tmp[0] + 1])
                sp()
                sp()
                v.tensor_scalar(st_[:, ycol:ycol + 1], st_[:, tmp[1]:tmp[1] + 1],
                                -A_ / B_, B_, ALU.subtract, ALU.mult)
                for _ in range(5):
                    sp()
                    v.tensor_tensor(st_[:, tmp[2]:tmp[2] + 1], st_[:, ycol:ycol + 1],
                                    st_[:, ycol:ycol + 1], ALU.mult)
                    sp()
                    v.tensor_tensor(st_[:, tmp[3]:tmp[3] + 1], st_[:, tmp[2]:tmp[2] + 1],
                                    st_[:, vcol:vcol + 1], ALU.mult)
                    sp()
                    v.tensor_scalar(st_[:, tmp[2]:tmp[2] + 1], st_[:, tmp[3]:tmp[3] + 1],
                                    3.0, -0.5, ALU.subtract, ALU.mult)
                    sp()
                    m = v.tensor_tensor(st_[:, ycol:ycol + 1], st_[:, ycol:ycol + 1],
                                        st_[:, tmp[2]:tmp[2] + 1], ALU.mult)
                return m
            def front(j):
                bi = j % 2
                ti = j % 3
                snap_ser()

                @block.sync
                def _(s_, j=j, bi=bi):
                    ser(s_)
                    if j == 0:
                        jjs = [0, 1]
                        for h in range(NH):
                            s_.wait_ge(sem["dS"], M[f"ots_{h}"])
                    elif j + 1 < NT:
                        jjs = [j + 1]
                        s_.wait_ge(sem["pe"], M[f"y_{j-1}_3"])
                        s_.wait_ge(sem["dve"], M[f"z1_{j-1}_3"])
                    else:
                        jjs = []
                    for jj in jjs:
                        bj = jj % 2
                        cs = "cA" if bj == 0 else "cB"
                        m = s_.dma_start(out=xtl[bj][:, :], in_=x_tm[jj * 128:(jj + 1) * 128, :])
                        inc(m, cs, 16)
                        m = s_.dma_start(out=otc[bj][:, :].rearrange("p (h t) -> p h t", h=NH),
                                         in_=ot_hbm[:, :, jj * 128:(jj + 1) * 128].rearrange("h p t -> p h t"))
                        M[f"cin_{jj}"] = inc(m, cs, 16)

                dve_fb = C["dve"]
                z1_mark = lambda p: dve_fb + p + 1

                @block.tensor
                def _(t, j=j, bi=bi, ti=j % 3, z1_mark=z1_mark):
                    ser(t)
                    t.wait_ge(sem["cA" if j % 2 == 0 else "cB"], M[f"cin_{j}"])
                    if j == 0:
                        t.wait_ge(sem["dG"], M["wo"])
                        wait_free(t, "PA")
                        wait_free(t, "PB")
                    for p in range(4):
                        ps = PA if p % 2 == 0 else PB
                        if p >= 2:
                            t.wait_ge(sem["dve"], z1_mark(p - 2))
                        elif j > 0:
                            t.wait_ge(sem["dve"], M[f"z1_{j-1}_{p+2}"])
                        for c in range(NCH):
                            t.matmul(ps[:, 0:512], otc[bi][:, c * 128:(c + 1) * 128],
                                     xT[:, c * S + p * 512: c * S + p * 512 + 512],
                                     start=(c == 0), stop=False)
                        m = t.matmul(ps[:, 0:512], onerb[0:1, 0:128],
                                     bor[0:1, p * 512:(p + 1) * 512], start=False, stop=True)
                        M[f"y_{j}_{p}"] = inc(m, "pe")

                @block.vector
                def _(v, j=j, bi=bi, ti=j % 3, z1_mark=z1_mark):
                    ser(v)
                    st_ = stat[ti]
                    for p in range(4):
                        ps = PA if p % 2 == 0 else PB
                        v.wait_ge(sem["pe"], M[f"y_{j}_{p}"])
                        if p == 0 and j > 0:
                            v.wait_ge(sem["act"], M[f"sq1_{j-1}_3"])
                        m = v.scalar_tensor_tensor(z1[:, p * 512:(p + 1) * 512],
                                                   xtl[bi][:, p * 512:(p + 1) * 512], 1.0,
                                                   ps[:, 0:512], ALU.mult, ALU.add,
                                                   accum_out=st_[:, p:p + 1])
                        M[f"z1_{j}_{p}"] = assert_eq(inc(m, "dve"), z1_mark(p))

                @block.scalar
                def _(sc, j=j, bi=bi, ti=j % 3):
                    ser(sc)
                    st_ = stat[ti]
                    for p in range(4):
                        sc.wait_ge(sem["dve"], M[f"z1_{j}_{p}"])
                        m = sc.activation(sqs[:, :], z1[:, p * 512:(p + 1) * 512], AF.Square,
                                          accum_out=st_[:, 4 + p:5 + p])
                        M[f"sq1_{j}_{p}"] = inc(m, "act")

            def front_tail(j):
                bi = j % 2
                ti = j % 3
                snap_ser()

                @block.vector
                def _(v, j=j, bi=bi, ti=j % 3):
                    ser(v)
                    st_ = stat[ti]
                    v.wait_ge(sem["act"], M[f"sq1_{j}_3"])
                    v.reduce_sum(st_[:, 8:9], st_[:, 0:4], axis=AX.X)
                    v.reduce_sum(st_[:, 9:10], st_[:, 4:8], axis=AX.X)
                    v.tensor_scalar(st_[:, 10:11], st_[:, 8:9], 1.0 / D, None, ALU.mult)
                    v.tensor_scalar(st_[:, 11:12], st_[:, 9:10], 1.0 / D, None, ALU.mult)
                    v.tensor_tensor(st_[:, 12:13], st_[:, 10:11], st_[:, 10:11], ALU.mult)
                    v.tensor_copy(st_[:, 33:34], st_[:, 8:9])
                    v.tensor_copy(st_[:, 33:34], st_[:, 9:10])
                    v.tensor_tensor(st_[:, 13:14], st_[:, 11:12], st_[:, 12:13], ALU.subtract)
                    v.tensor_copy(st_[:, 33:34], st_[:, 8:9])
                    v.tensor_copy(st_[:, 33:34], st_[:, 9:10])
                    m = v.tensor_scalar(st_[:, 14:15], st_[:, 13:14], EPS, None, ALU.add)
                    v.tensor_copy(st_[:, 33:34], st_[:, 8:9])
                    v.tensor_copy(st_[:, 33:34], st_[:, 9:10])
                    M[f"st1_{j}"] = inc(m, "dve")

                @block.vector
                def _(v, j=j, bi=bi, ti=j % 3):
                    ser(v)
                    st_ = stat[ti]
                    m = newton_rsqrt(v, st_, 14, 16, (41, 42, 43, 15), j)
                    M[f"rs1_{j}"] = inc(m, "dve")
                    v.tensor_copy(st_[:, 33:34], st_[:, 8:9])
                    v.tensor_tensor(st_[:, 17:18], st_[:, 16:17], st_[:, 16:17], ALU.mult)
                    v.tensor_scalar(x1[ti][:, :], z1[:, :], st_[:, 10:11], st_[:, 16:17],
                                    ALU.subtract, ALU.mult)
                    v.tensor_scalar(st_[:, 19:20], st_[:, 17:18], -EPS * float(D), None, ALU.mult)
                    v.tensor_copy(st_[:, 33:34], st_[:, 14:15])
                    v.tensor_copy(st_[:, 33:34], st_[:, 17:18])
                    v.tensor_scalar(st_[:, 18:19], st_[:, 19:20], float(D), None, ALU.add)
                    v.tensor_copy(st_[:, 33:34], st_[:, 14:15])
                    v.tensor_copy(st_[:, 33:34], st_[:, 17:18])
                    m = v.tensor_scalar(st_[:, 44:52], m2bc[:, :], st_[:, 18:19], None, ALU.add)
                    if DBG4:
                        v.tensor_copy(dbgg[:, j * 16 + 12:j * 16 + 13], st_[:, 16:17])
                        v.tensor_copy(dbgg[:, j * 16 + 13:j * 16 + 14], st_[:, 17:18])
                        v.tensor_copy(dbgg[:, j * 16 + 14:j * 16 + 15], st_[:, 14:15])
                        v.tensor_copy(dbgg[:, j * 16 + 15:j * 16 + 16], st_[:, 18:19])
                    M[f"x1_{j}"] = inc(m, "dve")

                @block.scalar
                def _(sc, j=j, bi=bi, ti=j % 3):
                    ser(sc)
                    sc.wait_ge(sem["dve"], M[f"x1_{j}"])
                    if j >= 3:
                        sc.wait_ge(sem["pe"], M[f"x1tB_{j-3}"])
                    m = sc.activation(x1b[ti][:, :], x1[ti][:, :], AF.Copy)
                    M[f"x1b_{j}"] = inc(m, "act")


            def back1(j):
                bi = j % 2
                ti = j % 3
                snap_ser()

                @block.tensor
                def _(t, j=j, bi=bi, ti=j % 3):
                    ser(t)
                    t.wait_ge(sem["act"], M[f"x1b_{j}"])
                    if j >= 1:
                        t.wait_ge(sem["dve"], M[f"x1t_{j-1}"])
                    else:
                        wait_free(t, "PD")
                    for c in range(8):
                        m = t.transpose(PDb[:, c * 128:(c + 1) * 128],
                                        x1b[ti][:, c * 128:(c + 1) * 128], idb[:, :])
                    M[f"x1tA_{j}"] = inc(m, "pe")

                @block.vector
                def _(v, j=j, bi=bi, ti=j % 3):
                    ser(v)
                    v.wait_ge(sem["pe"], M[f"x1tA_{j}"])
                    m = v.tensor_copy(x1t[0][:, 0:1024], PDb[:, 0:1024])
                    M[f"x1eA_{j}"] = inc(m, "dve")

                @block.tensor
                def _(t, j=j, bi=bi, ti=j % 3):
                    ser(t)
                    t.wait_ge(sem["dve"], M[f"x1eA_{j}"])
                    for c in range(8, NCH):
                        m = t.transpose(PDb[:, (c - 8) * 128:(c - 7) * 128],
                                        x1b[ti][:, c * 128:(c + 1) * 128], idb[:, :])
                    M[f"x1tB_{j}"] = inc(m, "pe")

                @block.vector
                def _(v, j=j, bi=bi, ti=j % 3):
                    ser(v)
                    v.wait_ge(sem["pe"], M[f"x1tB_{j}"])
                    m = v.tensor_copy(x1t[0][:, 1024:2048], PDb[:, 0:1024])
                    M[f"x1t_{j}"] = inc(m, "dve")

                @block.tensor
                def _(t, j=j, bi=bi, ti=j % 3):
                    ser(t)
                    t.wait_ge(sem["dve"], M[f"x1t_{j}"])
                    if j >= 1:
                        t.wait_ge(sem["dve"], M[f"gate_{j-1}"])
                    else:
                        wait_free(t, "PC")
                    for c in range(NCH):
                        t.matmul(PC[:, 0:136], x1t[0][:, c * 128:(c + 1) * 128],
                                 ffnw[:, c * 136:(c + 1) * 136], start=(c == 0), stop=False)
                    m = t.matmul(PC[:, 0:136], onerb[0:1, 0:128], crow[0:1, :],
                                 start=False, stop=True)
                    M[f"ffn_{j}"] = inc(m, "pe")

                @block.vector
                def _(v, j=j):
                    ser(v)
                    v.wait_ge(sem["pe"], M[f"ffn_{j}"])
                    v.tensor_copy(prr[:, :], PC[:, 8:72])
                    v.tensor_copy(prs[:, 96:104], m2bc[:, :])
                    v.tensor_copy(prs[:, 96:104], cenbc[:, :])
                    m = v.scalar_tensor_tensor(pr2s[:, :], prr[:, :], 1.0, prr[:, :],
                                               ALU.mult, ALU.mult)
                    M[f"pr2_{j}"] = inc(m, "dve")

                @block.vector
                def _(v, j=j, bi=bi, ti=j % 3):
                    ser(v)
                    st_ = stat[ti]
                    gs = gsc[ti]
                    v.wait_ge(sem["dve"], M[f"pr2_{j}"])
                    v.scalar_tensor_tensor(prs[:, 8:16], PC[:, 0:8], -2.0, st_[:, 44:52],
                                           ALU.mult, ALU.add)
                    for n in range(8):
                        v.reduce_sum(prs[:, 40 + n:41 + n], pr2s[:, n * 8:(n + 1) * 8], axis=AX.X)
                    v.reciprocal(prs[:, 16:24], prs[:, 8:16])
                    v.tensor_copy(prs[:, 96:104], m2bc[:, :])
                    v.tensor_copy(prs[:, 96:104], cenbc[:, :])
                    v.tensor_tensor(prs[:, 24:32], prs[:, 40:48], prs[:, 16:24], ALU.mult)
                    v.tensor_copy(prs[:, 96:104], m2bc[:, :])
                    v.tensor_copy(prs[:, 96:104], cenbc[:, :])
                    v.tensor_tensor(prs[:, 32:40], prs[:, 24:32], cenbc[:, :], ALU.subtract)
                    v.tensor_copy(prs[:, 96:104], m2bc[:, :])
                    v.tensor_copy(prs[:, 96:104], cenbc[:, :])
                    m = v.tensor_tensor(prs[:, 48:56], prs[:, 32:40], slpbc[:, :], ALU.mult)
                    M[f"arg_{j}"] = inc(m, "dve")

                @block.scalar
                def _(sc, j=j, bi=bi, ti=j % 3):
                    ser(sc)
                    gs = gsc[ti]
                    sc.wait_ge(sem["dve"], M[f"arg_{j}"])
                    m = sc.activation(gs[:, :], prs[:, 48:56], AF.Sigmoid)
                    M[f"eneg_{j}"] = inc(m, "act")

                @block.vector
                def _(v, j=j, bi=bi, ti=j % 3):
                    ser(v)
                    st_ = stat[ti]
                    gs = gsc[ti]
                    v.wait_ge(sem["act"], M[f"eneg_{j}"])
                    if j >= 1:
                        v.wait_ge(sem["pe"], M[f"ett_{j-1}"])
                    v.reduce_max(st_[:, 30:31], gs[:, :], axis=AX.X)
                    v.tensor_copy(prs[:, 96:104], m2bc[:, :])
                    v.tensor_copy(prs[:, 96:104], cenbc[:, :])
                    v.tensor_scalar(st_[:, 31:32], st_[:, 30:31], 2.0, -1.0, ALU.subtract, ALU.mult)
                    v.tensor_scalar(prs[:, 72:80], gs[:, :], st_[:, 30:31], None, ALU.is_ge)
                    v.tensor_copy(prs[:, 96:104], m2bc[:, :])
                    v.tensor_copy(prs[:, 96:104], cenbc[:, :])
                    v.tensor_tensor(prs[:, 80:88], prs[:, 72:80], wties[:, :], ALU.mult)
                    v.tensor_copy(prs[:, 96:104], m2bc[:, :])
                    v.tensor_copy(prs[:, 96:104], cenbc[:, :])
                    v.reduce_max(st_[:, 32:33], prs[:, 80:88], axis=AX.X)
                    v.tensor_copy(st_[:, 33:34], st_[:, 30:31])
                    v.tensor_copy(st_[:, 33:34], st_[:, 31:32])
                    v.tensor_scalar(prs[:, 88:96], prs[:, 80:88], st_[:, 32:33], None, ALU.is_ge)
                    v.tensor_copy(prs[:, 96:104], m2bc[:, :])
                    v.tensor_copy(prs[:, 96:104], cenbc[:, :])
                    v.tensor_copy(ebf[0][:, :], prs[:, 88:96])
                    if DBG4:
                        v.tensor_copy(dbgg[:, j * 16:j * 16 + 8], gs[:, :])
                        v.tensor_copy(dbgg[:, j * 16 + 8:j * 16 + 9], st_[:, 30:31])
                        v.tensor_copy(dbgg[:, j * 16 + 9:j * 16 + 10], st_[:, 18:19])
                        v.tensor_copy(dbgg[:, j * 16 + 10:j * 16 + 11], prs[:, 8:9])
                        v.tensor_copy(dbgg[:, j * 16 + 11:j * 16 + 12], prs[:, 88:89])
                    last = None
                    for n in range(8):
                        last = v.tensor_scalar(mmbv[0][:, n * 8:(n + 1) * 8],
                                               PC[:, 72 + n * 8: 80 + n * 8],
                                               prs[:, 88 + n:89 + n], None, ALU.mult)
                    M[f"gate_{j}"] = inc(last, "dve")

            def back2a(j):
                bi = j % 2
                ti = j % 3
                snap_ser()

                @block.tensor
                def _(t, j=j, bi=bi, ti=j % 3):
                    ser(t)
                    t.wait_ge(sem["dve"], M[f"gate_{j}"])
                    if j >= 1:
                        t.wait_ge(sem["dve"], M[f"etv_{j-1}"])
                    t.transpose(PBb[0:8, 1024:1152], ebf[0][:, :], idb[:, :])
                    m = t.transpose(PBb[0:64, 1152:1280], mmbv[0][:, :], idb[:, :])
                    M[f"ett_{j}"] = inc(m, "pe")

                @block.vector
                def _(v, j=j, bi=bi, ti=j % 3):
                    ser(v)
                    v.wait_ge(sem["pe"], M[f"ett_{j}"])
                    v.tensor_copy(ets[0][:, :], PBb[0:8, 1024:1152])
                    m = v.tensor_copy(mts[0][:, :], PBb[0:64, 1152:1280])
                    M[f"etv_{j}"] = inc(m, "dve")

                dve_zb = C["dve"]
                z2_mark = lambda p: dve_zb + p + 1

                @block.tensor
                def _(t, j=j, bi=bi, ti=j % 3, z2_mark=z2_mark):
                    ser(t)
                    t.wait_ge(sem["dve"], M[f"etv_{j}"])
                    for p in range(4):
                        bank = "PE" if p % 2 == 0 else "PF"
                        ps = PE_ if p % 2 == 0 else PF
                        if p >= 2:
                            t.wait_ge(sem["dve"], z2_mark(p - 2))
                        elif j > 0:
                            t.wait_ge(sem["dve"], M[f"z2_{j-1}_{p+2}"])
                        else:
                            wait_free(t, bank)
                        t.matmul(ps[:, 0:512], mts[0][:, :], dgn[:, p * 512:(p + 1) * 512],
                                 start=True, stop=False)
                        m = t.matmul(ps[:, 0:512], ets[0][:, :], bwsb[:, p * 512:(p + 1) * 512],
                                     start=False, stop=True)
                        M[f"mbx_{j}_{p}"] = inc(m, "pe")

                @block.vector
                def _(v, j=j, bi=bi, ti=j % 3, z2_mark=z2_mark):
                    ser(v)
                    st_ = stat[ti]
                    for p in range(4):
                        ps = PE_ if p % 2 == 0 else PF
                        v.wait_ge(sem["pe"], M[f"mbx_{j}_{p}"])
                        if p == 0 and j > 0:
                            v.wait_ge(sem["act"], M[f"sq2_{j-1}_3"])
                        v.tensor_scalar(ts1[:, :], x1[ti][:, p * 512:(p + 1) * 512],
                                        st_[:, 31:32], None, ALU.mult)
                        m = v.scalar_tensor_tensor(z2[:, p * 512:(p + 1) * 512], ps[:, 0:512],
                                                   st_[:, 30:31], ts1[:, :], ALU.mult, ALU.add,
                                                   accum_out=st_[:, 20 + p:21 + p])
                        M[f"z2_{j}_{p}"] = inc(m, "dve")
                        assert M[f"z2_{j}_{p}"] == z2_mark(p), (M[f"z2_{j}_{p}"], z2_mark(p))

                @block.scalar
                def _(sc, j=j, bi=bi, ti=j % 3):
                    ser(sc)
                    st_ = stat[ti]
                    for p in range(4):
                        sc.wait_ge(sem["dve"], M[f"z2_{j}_{p}"])
                        m = sc.activation(sqs[:, :], z2[:, p * 512:(p + 1) * 512], AF.Square,
                                          accum_out=st_[:, 24 + p:25 + p])
                        M[f"sq2_{j}_{p}"] = inc(m, "act")

            def back2b(j):
                bi = j % 2
                ti = j % 3
                snap_ser()

                @block.vector
                def _(v, j=j, bi=bi, ti=j % 3):
                    ser(v)
                    st_ = stat[ti]
                    v.wait_ge(sem["act"], M[f"sq2_{j}_3"])
                    v.reduce_sum(st_[:, 52:53], st_[:, 20:24], axis=AX.X)
                    v.reduce_sum(st_[:, 53:54], st_[:, 24:28], axis=AX.X)
                    v.tensor_scalar(st_[:, 54:55], st_[:, 52:53], 1.0 / D, None, ALU.mult)
                    v.tensor_scalar(st_[:, 55:56], st_[:, 53:54], 1.0 / D, None, ALU.mult)
                    v.tensor_tensor(st_[:, 56:57], st_[:, 54:55], st_[:, 54:55], ALU.mult)
                    v.tensor_copy(st_[:, 64:65], st_[:, 52:53])
                    v.tensor_copy(st_[:, 64:65], st_[:, 53:54])
                    v.tensor_tensor(st_[:, 57:58], st_[:, 55:56], st_[:, 56:57], ALU.subtract)
                    v.tensor_copy(st_[:, 64:65], st_[:, 52:53])
                    v.tensor_copy(st_[:, 64:65], st_[:, 53:54])
                    m = v.tensor_scalar(st_[:, 58:59], st_[:, 57:58], EPS, None, ALU.add)
                    v.tensor_copy(st_[:, 64:65], st_[:, 52:53])
                    v.tensor_copy(st_[:, 64:65], st_[:, 53:54])
                    M[f"st2_{j}"] = inc(m, "dve")

                @block.vector
                def _(v, j=j, bi=bi, ti=j % 3):
                    ser(v)
                    st_ = stat[ti]
                    m = newton_rsqrt(v, st_, 58, 63, (60, 61, 62, 59), j)
                    M[f"rs2_{j}"] = inc(m, "dve")
                    if j >= 1:
                        v.wait_ge(sem["st"], M[f"yst_{j-1}"])
                    v.tensor_copy(st_[:, 64:65], st_[:, 52:53])
                    v.tensor_copy(st_[:, 64:65], st_[:, 53:54])
                    m = v.tensor_scalar(outb[0][:, :], z2[:, :], st_[:, 54:55], st_[:, 63:64],
                                        ALU.subtract, ALU.mult)
                    M[f"out_{j}"] = inc(m, "dve")

                @block.sync
                def _(s_, j=j, bi=bi):
                    ser(s_)
                    s_.wait_ge(sem["dve"], M[f"out_{j}"])
                    m = s_.dma_start(out=y_d[j * 128:(j + 1) * 128, :], in_=outb[0][:, :])
                    M[f"yst_{j}"] = inc(m, "st", 16)

            for j in range(NT):
                if j >= 3:
                    back2a(j - 3)
                if j >= 2:
                    back1(j - 2)
                if j >= 1:
                    front_tail(j - 1)
                if j >= 3:
                    back2b(j - 3)
                front(j)
            back2a(NT - 3)
            back1(NT - 2)
            front_tail(NT - 1)
            back2b(NT - 3)
            back2a(NT - 2)
            back1(NT - 1)
            back2b(NT - 2)
            back2a(NT - 1)
            back2b(NT - 1)

            if DBG4:
                @block.gpsimd
                def _(g):
                    g.wait_ge(sem["dve"], M[f"x1t_{NT-1}"])
                    inc(g.dma_start(out=dbg["d_ota"], in_=ot_hbm[:, :, :]), "st", 16)
                    g.wait_ge(sem["dve"], M[f"gate_{NT-1}"])
                    inc(g.dma_start(out=dbg["d_gg"], in_=dbgg[:, :]), "st", 16)
                    g.wait_ge(sem["dve"], M["p1_done"])
                    inc(g.dma_start(out=dbg["d_slpbc"], in_=slpbc[:, :]), "st", 16)
                    g.wait_ge(sem["dve"], M[f"etv_{NT-1}"])
                    g.wait_ge(sem["dve"], M[f"out_{NT-1}"])
                    inc(g.dma_start(out=dbg["d_ets"], in_=ets[(NT-1) % 2][:, :]), "st", 16)
                    inc(g.dma_start(out=dbg["d_mts"], in_=mts[(NT-1) % 2][:, :]), "st", 16)
                    inc(g.dma_start(out=dbg["d_ebf"], in_=ebf[(NT-1) % 2][:, :]), "st", 16)
                    inc(g.dma_start(out=dbg["d_mmb"], in_=mmbv[(NT-1) % 2][:, :]), "st", 16)

            @block.sync
            def _(s_):
                s_.wait_ge(sem["st"], C["st"])
        else:
            @block.sync
            def _(s_):
                s_.wait_ge(sem["st"], C["st"])

    return nc, dbg_names


def assert_eq(a, b):
    assert a == b, (a, b)
    return a


def make_in_maps(inputs):
    x = np.asarray(inputs["L0_attn_hidden_state"])
    perm = _perm_deinterleave()
    Wq = np.ascontiguousarray(np.asarray(inputs["L0_attn_Wq"])[:, perm])
    Wk = np.ascontiguousarray(np.asarray(inputs["L0_attn_Wk"])[:, perm])
    Wv = np.ascontiguousarray(np.asarray(inputs["L0_attn_Wv"]))
    Wo = np.ascontiguousarray(np.asarray(inputs["L0_attn_Wo"]))
    bq = np.asarray(inputs["L0_attn_bq"])[perm][None, :].copy()
    bk = np.asarray(inputs["L0_attn_bk"])[perm][None, :].copy()
    bv = np.asarray(inputs["L0_attn_bv"])[None, :].copy()
    bo = np.asarray(inputs["L0_attn_bo"])[None, :].copy()
    selw = np.asarray(inputs["L0_ffn_select_weight"])
    md = np.asarray(inputs["L0_ffn_select_mean_diff"])
    cen = np.asarray(inputs["L0_ffn_imp_center"])[None, :].copy()
    slp = np.asarray(inputs["L0_ffn_imp_slope"])[None, :].copy()
    upd = np.asarray(inputs["L0_ffn_lora_update"])
    dgnw = np.asarray(inputs["L0_ffn_lora_degen"])
    bwv = np.ascontiguousarray(np.asarray(inputs["L0_ffn_bias_w"]))
    debw = np.asarray(inputs["L0_ffn_debias_w"])

    ffnw = np.ascontiguousarray(
        np.concatenate([md.T, selw.transpose(1, 0, 2).reshape(D, 64),
                        upd.transpose(1, 0, 2).reshape(D, 64)], axis=1)).astype(np.float32)
    dgnt = np.ascontiguousarray(dgnw.transpose(0, 2, 1).reshape(64, D))
    debt = np.ascontiguousarray(debw.T)
    ropec, ropes = _rope_tables()
    sel16 = np.zeros((16, D), dtype=np.float32)
    for jj in range(16):
        sel16[jj, jj * 128:(jj + 1) * 128] = 1.0

    in_maps = []
    for b in range(8):
        in_maps.append({
            "x_tm": np.ascontiguousarray(x[b]),
            "x_fm": np.ascontiguousarray(x[b].T),
            "wq": Wq, "wk": Wk, "wv": Wv, "wo": Wo,
            "bq": bq, "bk": bk, "bv": bv, "bo": bo,
            "ffnw": ffnw, "dgnt": dgnt, "bw": bwv, "debt": debt,
            "cen": cen, "slp": slp,
            "ropec": ropec, "ropes": ropes, "sel16": sel16,
        })
    return in_maps


def kernel(**inputs):
    in_maps = make_in_maps(inputs)
    nc, _ = build_nc(STAGE)
    res = run_bass_kernel_spmd(nc, in_maps, core_ids=list(range(8)))
    return np.stack([np.asarray(r["y"], dtype=np.float32) for r in res.results], axis=0)
